# version 04
# baseline (speedup 1.0000x reference)
import math
import numpy as np
import concourse.bass as bass
import concourse.mybir as mybir
from concourse.bass_utils import run_bass_kernel_spmd

F32 = mybir.dt.float32
BF16 = mybir.dt.bfloat16
AF = mybir.ActivationFunctionType
ALU = mybir.AluOpType
AX = mybir.AxisListType

D = 1024
NH = 16
NCH = 8
GT = 512
NT = 4
CORE_TOK = 4096
SEQ = 2048
EPS = 1e-5
QSCALE = (64 ** -0.5) * math.sqrt(64 ** -0.5)
KSCALE = math.sqrt(64 ** -0.5)
NEG = -1.0e30

ENGS = ("sync", "gpsimd", "scalar", "vector", "tensor")


class Buf:

    def __init__(self, ap, name, gran):
        self.ap = ap
        self.name = name
        self.gran = gran

    def __getitem__(self, sl):
        return self.ap[sl]

    def k(self, c0=None, c1=None):
        if c0 is None:
            c0, c1 = 0, self.ap.shape[-1]
        lo = c0 // self.gran
        hi = (c1 + self.gran - 1) // self.gran
        return [(self.name, b) for b in range(lo, hi)]


class Prog:

    def __init__(self, nc):
        self.nc = nc
        self.thunks = {e: [] for e in ENGS}
        self.sems = {}
        self.cnt = {}
        self.stepv = {}
        self.waited = {}
        self.seq = {e: 0 for e in ENGS}
        self.lastdrain = {e: 0 for e in ENGS}
        self.state = {}
        self.engsem = {"tensor": "tsem", "vector": "vsem",
                       "scalar": "ssem", "gpsimd": "gsem"}
        for s in ("tsem", "vsem", "ssem", "gsem"):
            self._mksem(s, 1)
        self.pool = {"sync": ["q%d" % i for i in range(16)],
                     "gpsimd": ["gq%d" % i for i in range(8)]}
        for q in self.pool["sync"] + self.pool["gpsimd"]:
            self._mksem(q, 16)
        self.pool_i = {"sync": 0, "gpsimd": 0}
        self.ndrain = {e: 0 for e in ENGS}

    def _mksem(self, name, step):
        self.sems[name] = self.nc.alloc_semaphore(name=name)
        self.cnt[name] = 0
        self.stepv[name] = step

    def _wait(self, eng, sem, val):
        if val <= self.waited.get((eng, sem), 0):
            return
        h = self.sems[sem]
        self.thunks[eng].append(lambda e, s=h, v=val: e.wait_ge(s, v))
        self.waited[(eng, sem)] = val

    def _drain(self, eng):
        self.thunks[eng].append(lambda e: e.drain())
        self.seq[eng] += 1
        self.lastdrain[eng] = self.seq[eng]
        self.ndrain[eng] += 1

    def _deps(self, eng, reads, writes, waw_ok):
        need = []
        for k in reads:
            st = self.state.get(k)
            if st and st[0] is not None:
                need.append((st[0], False))
        for k in writes:
            st = self.state.get(k)
            if st:
                if st[0] is not None:
                    need.append((st[0], waw_ok))
                for r in st[1]:
                    need.append((r, False))
        drained = False
        for (deng, dsem, dval, dseq), skip_same in need:
            if deng == eng:
                if (eng != "tensor" and not skip_same
                        and dseq > self.lastdrain[eng] and not drained):
                    self._drain(eng)
                    drained = True
            else:
                self._wait(eng, dsem, dval)

    def _record(self, acc, reads, writes):
        for k in writes:
            self.state[k] = [acc, []]
        for k in reads:
            self.state.setdefault(k, [None, []])[1].append(acc)

    def op(self, eng, fn, reads=(), writes=(), waw_ok=False):
        self._deps(eng, reads, writes, waw_ok or eng == "tensor")
        sem = self.engsem[eng]
        h, step = self.sems[sem], self.stepv[sem]

        def thunk(e, fn=fn, h=h, step=step):
            fn(e).then_inc(h, step)
        self.thunks[eng].append(thunk)
        self.cnt[sem] += step
        self.seq[eng] += 1
        acc = (eng, sem, self.cnt[sem], self.seq[eng])
        self._record(acc, reads, writes)
        return acc

    def dma(self, eng, out, in_, reads=(), writes=()):
        self._deps(eng, reads, writes, False)
        sems = self.pool[eng]
        sem = sems[self.pool_i[eng] % len(sems)]
        self.pool_i[eng] += 1
        if self.cnt[sem]:
            self._wait(eng, sem, self.cnt[sem])
        h = self.sems[sem]

        def thunk(e, o=out, i=in_, h=h):
            e.dma_start(out=o, in_=i).then_inc(h, 16)
        self.thunks[eng].append(thunk)
        self.cnt[sem] += 16
        self.seq[eng] += 1
        acc = ("dma:" + sem, sem, self.cnt[sem], self.seq[eng])
        self._record(acc, reads, writes)
        return acc

    def dma_barrier(self, eng, queue):
        for sem in self.pool[queue]:
            if self.cnt[sem]:
                self._wait(eng, sem, self.cnt[sem])

    def finalize(self):
        with self.nc.Block() as block:
            block.sync(lambda e: [t(e) for t in self.thunks["sync"]])
            block.gpsimd(lambda e: [t(e) for t in self.thunks["gpsimd"]])
            block.scalar(lambda e: [t(e) for t in self.thunks["scalar"]])
            block.vector(lambda e: [t(e) for t in self.thunks["vector"]])
            block.tensor(lambda e: [t(e) for t in self.thunks["tensor"]])


def build_core(cfg=None):
    cfg = cfg or {}
    n_groups = cfg.get("n_groups", 8)
    subs = cfg.get("subs", ("a", "f", "b", "f"))

    nc = bass.Bass("TRN2", target_bir_lowering=False, debug=False)
    p = Prog(nc)

    hbm = {}

    def hin(name, shape):
        hbm[name] = nc.dram_tensor(name, list(shape), F32, kind="ExternalInput").ap()
        return hbm[name]

    x_in = hin("x_in", (CORE_TOK, D))
    x_out = nc.dram_tensor("x_out", [CORE_TOK, D], F32, kind="ExternalOutput").ap()
    hin("wqT", (128, 8192))
    hin("wkvT", (128, 1024))
    hin("woutT", (128, 8192))
    hin("ag_col", (128, NCH))
    hin("ab_col", (128, NCH))
    hin("og_rep", (128, D))
    hin("ob_rep", (128, D))
    for w in ("q", "k", "v", "g", "o"):
        hin("wb%sT" % w, (128, 8192))
    hin("ld0", (1, 1))
    hin("r_w", (128, 64))
    hin("rb_rep", (128, 8))
    ew1 = hin("ew1", (10, 128, 4096))
    ew3 = hin("ew3", (10, 128, 4096))
    ew2 = hin("ew2", (10, 128, 4096))
    hin("ident", (128, 128))
    hin("eyemod", (128, 64))
    hin("trimask", (128, 128))
    hin("diffT", (128, 128))
    hin("indT", (128, 128))
    hin("ones128", (1, 128))

    wsplit = {}
    for name, cols in (("wq", 8192), ("wkv", 1024), ("wout", 8192),
                       ("wbq", 8192), ("wbk", 8192), ("wbv", 8192),
                       ("wbg", 8192), ("wbo", 8192)):
        wsplit[name] = (
            nc.dram_tensor("s_%s_h" % name, [128, cols], BF16, kind="Internal").ap(),
            nc.dram_tensor("s_%s_l" % name, [128, cols], BF16, kind="Internal").ap())
    for name in ("e1", "e3", "e2"):
        wsplit[name] = (
            nc.dram_tensor("s_%s_h" % name, [10, 128, 4096], BF16,
                           kind="Internal").ap(),
            nc.dram_tensor("s_%s_l" % name, [10, 128, 4096], BF16,
                           kind="Internal").ap())

    def sb(n, cols, gran, rows=128, dt=F32):
        ap = nc.alloc_sbuf_tensor(n, [rows, cols], dt).ap()
        return Buf(ap, n, gran)

    x = sb("x", NT * D, 1024)
    xn = sb("xn", NT * D, 512)
    fmh = sb("fmh", NCH * GT, 128, dt=BF16)
    fml = sb("fml", NCH * GT, 128, dt=BF16)
    buf1 = sb("buf1", NCH * GT, 128)
    b1h = sb("b1h", NCH * GT, 128, dt=BF16)
    b1l = sb("b1l", NCH * GT, 128, dt=BF16)
    b2h = sb("b2h", NCH * GT, 128, dt=BF16)
    b2l = sb("b2l", NCH * GT, 128, dt=BF16)
    Pb = sb("P", SEQ, 512)
    PTb = sb("PT", GT, 128)
    stmh = sb("stmh", NT * 128, 128, dt=BF16)
    stml = sb("stml", NT * 128, 128, dt=BF16)
    kcache = sb("kcache", 2 * SEQ, 512)
    vcache = sb("vcache", SEQ, 64)
    W1h = sb("W1h", 8192, 512, dt=BF16)
    W1l = sb("W1l", 8192, 512, dt=BF16)
    W2h = sb("W2h", 8192, 512, dt=BF16)
    W2l = sb("W2l", 8192, 512, dt=BF16)
    tmp5 = sb("tmp5", GT, 512)
    ident = sb("identS", 128, 128)
    eyemod = sb("eyemodS", 64, 64)
    trimask = sb("trimaskS", 128, 128)
    maskT = sb("maskTS", 128, 128)
    ones128 = sb("ones128S", 128, 128, rows=1)
    agc = sb("agc", NCH, 1)
    abc = sb("abc", NCH, 1)
    rbr = sb("rbr", 8, 8)
    rwh = sb("rwh", NCH * 8, 8, dt=BF16)
    rwl = sb("rwl", NCH * 8, 8, dt=BF16)
    st6 = sb("st6", 48, 6)
    mv = sb("mv", 2 * NT, 2)
    rr = sb("rr", NT, 1)
    ss = sb("ss", NT, 1)
    mx4 = sb("mx4", 4, 1)
    sums4 = sb("sums4", 4, 1)
    rsum = sb("rsum", 1, 1)
    biasq = sb("biasq", NCH, 8)
    biaskv = sb("biaskv", 1, 1)
    gates = sb("gates", NT * 8, 8)
    lgt = sb("lgt", 8, 8)
    l1t = sb("l1t", 1, 1)
    nl1t = sb("nl1t", 1, 1)
    l2t = sb("l2t", 1, 1)
    m1t = sb("m1t", 8, 8)
    ge2t = sb("ge2t", 8, 8)
    epsc = sb("epsc", 1, 1)
    nhc = sb("nhc", 1, 1)
    dect = sb("dect", 1, 1, rows=1)
    lnd128 = sb("lnd128", 1, 1)

    ps = [Buf(nc.alloc_psum_tensor("ps%d" % i, [128, 512], F32).ap(),
              "ps%d" % i, 512) for i in range(8)]

    for name, dst in (("ident", ident), ("eyemod", eyemod),
                      ("trimask", trimask), ("ones128", ones128),
                      ("ag_col", agc), ("ab_col", abc), ("rb_rep", rbr),
                      ("ld0", dect),):
        p.dma("sync", dst.ap, hbm[name], writes=dst.k())
    p.op("vector", lambda e: e.memset(epsc.ap, float(EPS)), writes=epsc.k())
    p.op("vector", lambda e: e.memset(nhc.ap, -0.5), writes=nhc.k())

    p.dma("sync", PTb[:, 0:128], hbm["diffT"], writes=PTb.k(0, 128))
    p.dma("sync", tmp5[:, 0:128], hbm["indT"], writes=tmp5.k(0, 128))
    p.op("scalar", lambda e: e.activation(dect.ap, dect.ap, AF.Sigmoid),
         reads=dect.k(), writes=dect.k())
    p.op("scalar", lambda e: e.activation(dect.ap, dect.ap, AF.Ln),
         reads=dect.k(), writes=dect.k())
    p.op("tensor", lambda e: e.matmul(ps[0][:, 0:1], ones128.ap, dect.ap,
                                      start=True, stop=True),
         reads=dect.k() + ones128.k(), writes=ps[0].k())
    p.op("vector", lambda e: e.tensor_copy(lnd128.ap, ps[0][:, 0:1]),
         reads=ps[0].k(), writes=lnd128.k())
    p.op("scalar", lambda e: e.activation(maskT.ap, PTb[:, 0:128], AF.Exp,
                                          scale=lnd128.ap),
         reads=PTb.k(0, 128) + lnd128.k(), writes=maskT.k())
    p.op("vector", lambda e: e.tensor_mul(maskT.ap, maskT.ap, tmp5[:, 0:128]),
         reads=maskT.k() + tmp5.k(0, 128), writes=maskT.k())

    def wprep(src_h, cols, dst_pair, prep=None, eslice=None):
        npc = (cols + 2047) // 2048
        for pc in range(npc):
            w = min(2048, cols - 2048 * pc)
            src = src_h[eslice] if eslice is not None else src_h
            p.dma("sync", Pb[:, 0:w], src[:, 2048 * pc:2048 * pc + w],
                  writes=Pb.k(0, w))
            if prep is not None:
                prep(pc, w)
            p.op("scalar", lambda e, w=w: e.copy(b2h[:, 0:w], Pb[:, 0:w]),
                 reads=Pb.k(0, w), writes=b2h.k(0, w))
            p.op("vector", lambda e, w=w: e.tensor_sub(
                b2l[:, 0:w], Pb[:, 0:w], b2h[:, 0:w]),
                reads=Pb.k(0, w) + b2h.k(0, w), writes=b2l.k(0, w))
            dh, dl = dst_pair
            dsth = dh[eslice] if eslice is not None else dh
            dstl = dl[eslice] if eslice is not None else dl
            p.dma("gpsimd", dsth[:, 2048 * pc:2048 * pc + w], b2h[:, 0:w],
                  reads=b2h.k(0, w))
            p.dma("gpsimd", dstl[:, 2048 * pc:2048 * pc + w], b2l[:, 0:w],
                  reads=b2l.k(0, w))

    for pc in range(4):
        p.dma("sync", Pb[:, 0:2048], hbm["wqT"][:, 2048 * pc:2048 * pc + 2048],
              writes=Pb.k(0, 2048))
        def bq_mm(e, pc=pc):
            for eo in range(NCH):
                for i, ci in enumerate(range(2 * pc, 2 * pc + 2)):
                    lo = 1024 * (ci - 2 * pc)
                    last = e.matmul(
                        ps[4][:, eo:eo + 1],
                        Pb[:, lo + 128 * eo:lo + 128 * eo + 128],
                        abc[:, ci:ci + 1],
                        start=(i == 0), stop=(i == 1))
            return last
        p.op("tensor", bq_mm, reads=Pb.k(0, 2048) + abc.k(), writes=ps[4].k(),
             waw_ok=True)
        if pc == 0:
            p.op("vector", lambda e: e.tensor_copy(biasq.ap, ps[4][:, 0:NCH]),
                 reads=ps[4].k(), writes=biasq.k())
        else:
            p.op("vector", lambda e: e.tensor_add(
                biasq.ap, biasq.ap, ps[4][:, 0:NCH]),
                reads=ps[4].k() + biasq.k(), writes=biasq.k())
    p.op("vector", lambda e: e.tensor_scalar_mul(
        biasq.ap, biasq.ap, float(QSCALE)),
        reads=biasq.k(), writes=biasq.k())

    def prep_wq(pc, w):
        for ci in range(2 * pc, 2 * pc + w // 1024):
            lo = 1024 * (ci - 2 * pc)
            p.op("vector", lambda e, lo=lo, ci=ci: e.tensor_scalar(
                Pb[:, lo:lo + 1024], Pb[:, lo:lo + 1024],
                agc[:, ci:ci + 1], float(QSCALE), ALU.mult, ALU.mult),
                reads=Pb.k(lo, lo + 1024) + agc.k(),
                writes=Pb.k(lo, lo + 1024))
    wprep(hbm["wqT"], 8192, wsplit["wq"], prep=prep_wq)

    p.dma("sync", Pb[:, 0:1024], hbm["wkvT"], writes=Pb.k(0, 1024))
    def bkv_mm(e):
        for ci in range(NCH):
            last = e.matmul(ps[2][:, 0:1], Pb[:, 128 * ci:128 * ci + 128],
                            abc[:, ci:ci + 1], start=(ci == 0), stop=(ci == 7))
        return last
    p.op("tensor", bkv_mm, reads=Pb.k(0, 1024) + abc.k(), writes=ps[2].k())
    p.op("vector", lambda e: e.tensor_copy(biaskv.ap, ps[2][:, 0:1]),
         reads=ps[2].k(), writes=biaskv.k())
    p.op("vector", lambda e: e.tensor_scalar_mul(
        biaskv[0:64, :], biaskv[0:64, :], float(KSCALE)),
        reads=biaskv.k(), writes=biaskv.k())

    def prep_wkv(pc, w):
        for ci in range(NCH):
            p.op("vector", lambda e, ci=ci: e.tensor_scalar_mul(
                Pb[:, 128 * ci:128 * ci + 128], Pb[:, 128 * ci:128 * ci + 128],
                agc[:, ci:ci + 1]),
                reads=Pb.k(128 * ci, 128 * ci + 128) + agc.k(),
                writes=Pb.k(128 * ci, 128 * ci + 128))
            p.op("vector", lambda e, ci=ci: e.tensor_scalar_mul(
                Pb[:, 128 * ci:128 * ci + 64], Pb[:, 128 * ci:128 * ci + 64],
                float(KSCALE)),
                reads=Pb.k(128 * ci, 128 * ci + 64),
                writes=Pb.k(128 * ci, 128 * ci + 64))
    wprep(hbm["wkvT"], 1024, wsplit["wkv"], prep=prep_wkv)

    wprep(hbm["woutT"], 8192, wsplit["wout"])
    for ei in (8, 9, 0, 1, 2, 3, 4, 5, 6, 7):
        wprep(ew1, 4096, wsplit["e1"], eslice=ei)
        wprep(ew3, 4096, wsplit["e3"], eslice=ei)
        wprep(ew2, 4096, wsplit["e2"], eslice=ei)
    for w in ("q", "k", "v", "g", "o"):
        wprep(hbm["wb%sT" % w], 8192, wsplit["wb" + w])
    p.dma("sync", PTb[:, 0:64], hbm["r_w"], writes=PTb.k(0, 64))
    p.op("scalar", lambda e: e.copy(rwh.ap, PTb[:, 0:64]),
         reads=PTb.k(0, 64), writes=rwh.k())
    p.op("vector", lambda e: e.tensor_sub(rwl.ap, PTb[:, 0:64], rwh.ap),
         reads=PTb.k(0, 64) + rwh.k(), writes=rwl.k())

    evac_tgl = [0]

    def evac_eng():
        evac_tgl[0] ^= 1
        return "vector" if evac_tgl[0] else "scalar"

    def copy_op(eng, dst, src):
        if eng == "vector":
            return lambda e: e.tensor_copy(dst, src)
        return lambda e: e.copy(dst, src)

    def split_psum(bank, w, dsth, dstl, d0):
        p.op("scalar", lambda e: e.copy(dsth[:, d0:d0 + w], bank[:, 0:w]),
             reads=bank.k(), writes=dsth.k(d0, d0 + w))
        p.op("vector", lambda e: e.tensor_sub(
            dstl[:, d0:d0 + w], bank[:, 0:w], dsth[:, d0:d0 + w]),
            reads=bank.k() + dsth.k(d0, d0 + w), writes=dstl.k(d0, d0 + w))

    def ln_tile(j, src=None, dst=None):
        src = src if src is not None else x
        dst = dst if dst is not None else xn
        sl = src[:, D * j:D * j + D]
        s0 = 12 * j
        p.op("vector", lambda e, sl=sl, s0=s0: e.bn_stats(
            st6[:, s0:s0 + 6], sl[:, 0:512]),
            reads=src.k(D * j, D * j + 512), writes=st6.k(s0, s0 + 6))
        p.op("vector", lambda e, sl=sl, s0=s0: e.bn_stats(
            st6[:, s0 + 6:s0 + 12], sl[:, 512:1024]),
            reads=src.k(D * j + 512, D * j + D), writes=st6.k(s0 + 6, s0 + 12))
        p.op("vector", lambda e, j=j, s0=s0: e.bn_aggr(
            mv[:, 2 * j:2 * j + 2], st6[:, s0:s0 + 12]),
            reads=st6.k(s0, s0 + 12), writes=mv.k(2 * j, 2 * j + 2))
        p.op("scalar", lambda e, j=j: e.activation(
            rr[:, j:j + 1], mv[:, 2 * j + 1:2 * j + 2], AF.Ln, bias=epsc.ap),
            reads=mv.k(2 * j + 1, 2 * j + 2) + epsc.k(), writes=rr.k(j, j + 1))
        p.op("scalar", lambda e, j=j: e.activation(
            rr[:, j:j + 1], rr[:, j:j + 1], AF.Exp, scale=nhc.ap),
            reads=rr.k(j, j + 1) + nhc.k(), writes=rr.k(j, j + 1))
        p.op("vector", lambda e, j=j: e.scalar_tensor_tensor(
            ss[:, j:j + 1], mv[:, 2 * j:2 * j + 1], -1.0, rr[:, j:j + 1],
            ALU.mult, ALU.mult),
            reads=mv.k(2 * j, 2 * j + 1) + rr.k(j, j + 1), writes=ss.k(j, j + 1))
        p.op("scalar", lambda e, j=j: e.activation(
            dst[:, D * j:D * j + D], src[:, D * j:D * j + D], AF.Identity,
            bias=ss[:, j:j + 1], scale=rr[:, j:j + 1]),
            reads=src.k(D * j, D * j + D) + rr.k(j, j + 1) + ss.k(j, j + 1),
            writes=dst.k(D * j, D * j + D))

    def ln_pass(src, dst):
        for j in range(NT):
            ln_tile(j, src, dst)

    def transpose_split_fm(src):
        for ci in range(NCH):
            bank = ps[ci % 2]
            for j in range(NT):
                p.op("tensor", lambda e, ci=ci, j=j, bank=bank: e.transpose(
                    bank[:, 128 * j:128 * j + 128],
                    src[:, D * j + 128 * ci:D * j + 128 * ci + 128], ident.ap),
                    reads=src.k(D * j + 128 * ci, D * j + 128 * ci + 128) + ident.k(),
                    writes=bank.k(), waw_ok=True)
            split_psum(bank, GT, fmh, fml, GT * ci)

    def loadw(slot_h, slot_l, name, col0=0, cols=8192, eslice=None):
        sh, sl_ = wsplit[name]
        if eslice is not None:
            sh, sl_ = sh[eslice], sl_[eslice]
        p.dma("sync", slot_h[:, col0:col0 + cols], sh[:, 0:cols],
              writes=slot_h.k(col0, col0 + cols))
        p.dma("sync", slot_l[:, col0:col0 + cols], sl_[:, 0:cols],
              writes=slot_l.k(col0, col0 + cols))

    def mm3(e, bank_ap, wh, wl, xh, xl, start, stop):
        e.matmul(bank_ap, wh, xh, start=start, stop=False)
        e.matmul(bank_ap, wh, xl, start=False, stop=False)
        return e.matmul(bank_ap, wl, xh, start=False, stop=stop)

    def fm_keys(c0, c1):
        return fmh.k(c0, c1) + fml.k(c0, c1)

    def attn_a(g, cb):
        grel = g % 4
        seqh = g // 4
        kcb = SEQ * seqh
        vb = 1024 * seqh
        ln_pass(xn, xn)
        transpose_split_fm(xn)

        loadw(W2h, W2l, "wkv", 0, 1024)
        def kv_mm(e):
            for ci in range(NCH):
                last = mm3(e, ps[2].ap,
                           W2h[:, 128 * ci:128 * ci + 128],
                           W2l[:, 128 * ci:128 * ci + 128],
                           fmh[:, GT * ci:GT * ci + GT],
                           fml[:, GT * ci:GT * ci + GT],
                           ci == 0, ci == 7)
            return last
        p.op("tensor", kv_mm,
             reads=W2h.k(0, 1024) + W2l.k(0, 1024) + fm_keys(0, NCH * GT),
             writes=ps[2].k())
        p.op("vector", lambda e: e.tensor_scalar_add(PTb.ap, ps[2].ap, biaskv.ap),
             reads=ps[2].k() + biaskv.k(), writes=PTb.k())
        ktok = kcb + 512 * grel
        p.dma("gpsimd", kcache[64:128, ktok:ktok + 512], PTb[0:64, :],
              reads=PTb.k(), writes=kcache.k(ktok, ktok + 512))
        p.op("vector", lambda e, ktok=ktok: e.tensor_copy(
            kcache[0:64, ktok:ktok + 512], PTb[0:64, :]),
            reads=PTb.k(), writes=kcache.k(ktok, ktok + 512))
        for j in range(NT):
            p.op("tensor", lambda e, j=j: e.transpose(
                ps[3][:, 64 * j:64 * j + 64], PTb[64:128, 128 * j:128 * j + 128],
                eyemod.ap[64:128, :]),
                reads=PTb.k(128 * j, 128 * j + 128) + eyemod.k(),
                writes=ps[3].k(), waw_ok=True)
        vtok = vb + 256 * grel
        p.op("vector", lambda e, vtok=vtok: e.tensor_copy(
            vcache[:, vtok:vtok + 256], ps[3][:, 0:256]),
            reads=ps[3].k(), writes=vcache.k(vtok, vtok + 256))

        loadw(W1h, W1l, "wq")
        for eo in range(NCH):
            bank = ps[4 + (eo % 2)]
            def q_mm(e, eo=eo, bank=bank):
                for ci in range(NCH):
                    last = mm3(e, bank.ap,
                               W1h[:, 1024 * ci + 128 * eo:1024 * ci + 128 * eo + 128],
                               W1l[:, 1024 * ci + 128 * eo:1024 * ci + 128 * eo + 128],
                               fmh[:, GT * ci:GT * ci + GT],
                               fml[:, GT * ci:GT * ci + GT],
                               ci == 0, ci == 7)
                return last
            p.op("tensor", q_mm, reads=W1h.k() + W1l.k() + fm_keys(0, NCH * GT),
                 writes=bank.k())
            p.op("vector", lambda e, eo=eo, bank=bank: e.tensor_scalar_add(
                buf1[:, GT * eo:GT * eo + GT], bank.ap, biasq[:, eo:eo + 1]),
                reads=bank.k() + biasq.k(), writes=buf1.k(GT * eo, GT * eo + GT))

        for h in range(NH):
            hrow = 64 * (h % 2)
            hch = h // 2
            obank = ps[7] if h % 2 == 0 else ps[5]
            for jt in range(NT):
                ig = 4 * grel + jt
                nk = (ig + 1) * 128
                nchunk = (nk + 511) // 512
                lq0 = GT * hch + 128 * jt
                lhs_q = buf1[hrow:hrow + 64, lq0:lq0 + 128]
                for c in range(nchunk):
                    w = min(512, nk - 512 * c)
                    p.op("tensor", lambda e, c=c, w=w, lhs_q=lhs_q, hrow=hrow:
                         e.matmul(
                             ps[c][:, 0:w], lhs_q,
                             kcache[hrow:hrow + 64, kcb + 512 * c:kcb + 512 * c + w],
                             start=True, stop=True),
                         reads=buf1.k(lq0, lq0 + 128) +
                               kcache.k(kcb + 512 * c, kcb + 512 * c + w),
                         writes=ps[c].k())
                dc, doff = (ig * 128) // 512, (ig * 128) % 512
                p.op("vector", lambda e, dc=dc, doff=doff: e.tensor_add(
                    ps[dc][:, doff:doff + 128], ps[dc][:, doff:doff + 128],
                    trimask.ap),
                    reads=ps[dc].k() + trimask.k(), writes=ps[dc].k())
                for c in range(nchunk):
                    w = min(512, nk - 512 * c)
                    p.op("vector", lambda e, c=c, w=w: e.reduce_max(
                        mx4[:, c:c + 1], ps[c][:, 0:w], axis=AX.X, negate=True),
                        reads=ps[c].k(), writes=mx4.k(c, c + 1))
                if nchunk > 1:
                    p.op("vector", lambda e, n=nchunk: e.tensor_reduce(
                        mx4[:, 0:1], mx4[:, 0:n], AX.X, ALU.min),
                        reads=mx4.k(0, nchunk), writes=mx4.k(0, 1))
                for c in range(nchunk):
                    w = min(512, nk - 512 * c)
                    p.op("scalar", lambda e, c=c, w=w: e.activation(
                        Pb[:, 512 * c:512 * c + w], ps[c][:, 0:w], AF.Exp,
                        bias=mx4[:, 0:1], accum_out=sums4[:, c:c + 1]),
                        reads=ps[c].k() + mx4.k(0, 1),
                        writes=Pb.k(512 * c, 512 * c + w) + sums4.k(c, c + 1))
                p.op("vector", lambda e, n=nchunk: e.tensor_reduce(
                    rsum.ap, sums4[:, 0:n], AX.X, ALU.add),
                    reads=sums4.k(0, nchunk), writes=rsum.k())
                p.op("vector", lambda e: e.reciprocal(rsum.ap, rsum.ap),
                     reads=rsum.k(), writes=rsum.k())
                for c in range(nchunk):
                    w = min(512, nk - 512 * c)
                    p.op("scalar", lambda e, c=c, w=w: e.activation(
                        Pb[:, 512 * c:512 * c + w], Pb[:, 512 * c:512 * c + w],
                        AF.Identity, scale=rsum.ap),
                        reads=Pb.k(512 * c, 512 * c + w) + rsum.k(),
                        writes=Pb.k(512 * c, 512 * c + w))
                nkb = ig + 1
                for b0 in range(0, nkb, 4):
                    bcnt = min(4, nkb - b0)
                    ptbank = ps[6] if (b0 // 4) % 2 == 0 else ps[4]
                    for bi in range(bcnt):
                        kb = b0 + bi
                        p.op("tensor", lambda e, kb=kb, bi=bi, ptbank=ptbank:
                             e.transpose(
                                 ptbank[:, 128 * bi:128 * bi + 128],
                                 Pb[:, 128 * kb:128 * kb + 128], ident.ap),
                             reads=Pb.k(128 * kb, 128 * kb + 128) + ident.k(),
                             writes=ptbank.k(), waw_ok=True)
                    eng = evac_eng()
                    p.op(eng, copy_op(eng, PTb[:, 0:128 * bcnt],
                                      ptbank[:, 0:128 * bcnt]),
                         reads=ptbank.k(), writes=PTb.k(0, 128 * bcnt))
                    def pv(e, b0=b0, bcnt=bcnt, jt=jt, nkb=nkb, vb=vb,
                           obank=obank, hrow=hrow):
                        tp = None if hrow == 0 else (0, 64)
                        for bi in range(bcnt):
                            kb = b0 + bi
                            last = e.matmul(
                                obank[hrow:hrow + 64, 128 * jt:128 * jt + 128],
                                vcache[:, vb + 64 * kb:vb + 64 * kb + 64],
                                PTb[:, 128 * bi:128 * bi + 128],
                                start=(kb == 0), stop=(kb == nkb - 1),
                                tile_position=tp)
                        return last
                    p.op("tensor", pv,
                         reads=PTb.k(0, 128 * bcnt) + vcache.k(vb, vb + 64 * nkb),
                         writes=obank.k(), waw_ok=True)
                p.op("scalar", lambda e, hrow=hrow, lq0=lq0, jt=jt, obank=obank:
                     e.copy(b2h[hrow:hrow + 64, lq0:lq0 + 128],
                            obank[hrow:hrow + 64, 128 * jt:128 * jt + 128]),
                     reads=obank.k(), writes=b2h.k(lq0, lq0 + 128))
                p.op("vector", lambda e, hrow=hrow, lq0=lq0, jt=jt, obank=obank:
                     e.tensor_sub(b2l[hrow:hrow + 64, lq0:lq0 + 128],
                                  obank[hrow:hrow + 64, 128 * jt:128 * jt + 128],
                                  b2h[hrow:hrow + 64, lq0:lq0 + 128]),
                     reads=obank.k() + b2h.k(lq0, lq0 + 128),
                     writes=b2l.k(lq0, lq0 + 128))

        p.dma("sync", Pb[:, 0:1024], hbm["og_rep"], writes=Pb.k(0, 1024))
        p.dma("sync", Pb[:, 1024:2048], hbm["ob_rep"], writes=Pb.k(1024, 2048))
        loadw(W2h, W2l, "wout")
        for jt in range(NT):
            for half in range(2):
                bank = ps[half]
                def o_mm(e, jt=jt, half=half, bank=bank):
                    for ci in range(NCH):
                        last = mm3(
                            e, bank.ap,
                            b2h[:, GT * ci + 128 * jt:GT * ci + 128 * jt + 128],
                            b2l[:, GT * ci + 128 * jt:GT * ci + 128 * jt + 128],
                            W2h[:, 1024 * ci + 512 * half:1024 * ci + 512 * half + 512],
                            W2l[:, 1024 * ci + 512 * half:1024 * ci + 512 * half + 512],
                            ci == 0, ci == 7)
                    return last
                p.op("tensor", o_mm,
                     reads=W2h.k() + W2l.k() +
                           [k for ci in range(NCH) for k in
                            b2h.k(GT * ci + 128 * jt, GT * ci + 128 * jt + 128) +
                            b2l.k(GT * ci + 128 * jt, GT * ci + 128 * jt + 128)],
                     writes=bank.k())
            s0 = 12 * jt
            p.op("vector", lambda e, s0=s0: e.bn_stats(st6[:, s0:s0 + 6], ps[0].ap),
                 reads=ps[0].k(), writes=st6.k(s0, s0 + 6))
            p.op("vector", lambda e, s0=s0: e.bn_stats(st6[:, s0 + 6:s0 + 12],
                                                       ps[1].ap),
                 reads=ps[1].k(), writes=st6.k(s0 + 6, s0 + 12))
            p.op("vector", lambda e, jt=jt, s0=s0: e.bn_aggr(
                mv[:, 2 * jt:2 * jt + 2], st6[:, s0:s0 + 12]),
                reads=st6.k(s0, s0 + 12), writes=mv.k(2 * jt, 2 * jt + 2))
            p.op("scalar", lambda e, jt=jt: e.activation(
                rr[:, jt:jt + 1], mv[:, 2 * jt + 1:2 * jt + 2], AF.Ln,
                bias=epsc.ap),
                reads=mv.k(2 * jt + 1, 2 * jt + 2) + epsc.k(),
                writes=rr.k(jt, jt + 1))
            p.op("scalar", lambda e, jt=jt: e.activation(
                rr[:, jt:jt + 1], rr[:, jt:jt + 1], AF.Exp, scale=nhc.ap),
                reads=rr.k(jt, jt + 1) + nhc.k(), writes=rr.k(jt, jt + 1))
            p.op("vector", lambda e, jt=jt: e.scalar_tensor_tensor(
                ss[:, jt:jt + 1], mv[:, 2 * jt:2 * jt + 1], -1.0, rr[:, jt:jt + 1],
                ALU.mult, ALU.mult),
                reads=mv.k(2 * jt, 2 * jt + 1) + rr.k(jt, jt + 1),
                writes=ss.k(jt, jt + 1))
            for half in range(2):
                bank = ps[half]
                a0 = D * jt + 512 * half
                g0 = 512 * half
                p.op("scalar", lambda e, bank=bank, a0=a0, jt=jt: e.activation(
                    xn[:, a0:a0 + 512], bank.ap, AF.Identity, bias=ss[:, jt:jt + 1],
                    scale=rr[:, jt:jt + 1]),
                    reads=bank.k() + rr.k(jt, jt + 1) + ss.k(jt, jt + 1),
                    writes=xn.k(a0, a0 + 512))
                p.op("vector", lambda e, a0=a0, g0=g0: e.tensor_mul(
                    xn[:, a0:a0 + 512], xn[:, a0:a0 + 512], Pb[:, g0:g0 + 512]),
                    reads=xn.k(a0, a0 + 512) + Pb.k(g0, g0 + 512),
                    writes=xn.k(a0, a0 + 512))
                p.op("vector", lambda e, a0=a0, g0=g0: e.tensor_add(
                    xn[:, a0:a0 + 512], xn[:, a0:a0 + 512],
                    Pb[:, 1024 + g0:1024 + g0 + 512]),
                    reads=xn.k(a0, a0 + 512) + Pb.k(1024 + g0, 1024 + g0 + 512),
                    writes=xn.k(a0, a0 + 512))
                p.op("vector", lambda e, a0=a0: e.tensor_add(
                    x[:, a0:a0 + 512], x[:, a0:a0 + 512], xn[:, a0:a0 + 512]),
                    reads=xn.k(a0, a0 + 512) + x.k(a0, a0 + 512),
                    writes=x.k(a0, a0 + 512))
            cb(jt)

    def ffn(g, cb):
        transpose_split_fm(xn)
        for jt in range(NT):
            def r_mm(e, jt=jt):
                for ci in range(NCH):
                    last = mm3(e, ps[2][:, 0:8],
                               fmh[:, GT * ci + 128 * jt:GT * ci + 128 * jt + 128],
                               fml[:, GT * ci + 128 * jt:GT * ci + 128 * jt + 128],
                               rwh[:, 8 * ci:8 * ci + 8],
                               rwl[:, 8 * ci:8 * ci + 8],
                               ci == 0, ci == 7)
                return last
            p.op("tensor", r_mm,
                 reads=[k for ci in range(NCH) for k in
                        fm_keys(GT * ci + 128 * jt, GT * ci + 128 * jt + 128)] +
                       rwh.k() + rwl.k(),
                 writes=ps[2].k())
            p.op("vector", lambda e: e.tensor_add(lgt.ap, ps[2][:, 0:8], rbr.ap),
                 reads=ps[2].k() + rbr.k(), writes=lgt.k())
            p.op("vector", lambda e: e.reduce_max(l1t.ap, lgt.ap, axis=AX.X),
                 reads=lgt.k(), writes=l1t.k())
            p.op("vector", lambda e: e.reduce_max(nl1t.ap, lgt.ap, axis=AX.X,
                                                  negate=True),
                 reads=lgt.k(), writes=nl1t.k())
            p.op("vector", lambda e: e.tensor_scalar(
                m1t.ap, lgt.ap, l1t.ap, None, ALU.is_ge),
                reads=lgt.k() + l1t.k(), writes=m1t.k())
            p.op("vector", lambda e: e.scalar_tensor_tensor(
                m1t.ap, m1t.ap, NEG, lgt.ap, ALU.mult, ALU.add),
                reads=m1t.k() + lgt.k(), writes=m1t.k())
            p.op("vector", lambda e: e.reduce_max(l2t.ap, m1t.ap, axis=AX.X),
                 reads=m1t.k(), writes=l2t.k())
            p.op("vector", lambda e: e.tensor_scalar(
                ge2t.ap, lgt.ap, l2t.ap, None, ALU.is_ge),
                reads=lgt.k() + l2t.k(), writes=ge2t.k())
            p.op("scalar", lambda e: e.activation(lgt.ap, lgt.ap, AF.Exp,
                                                  bias=nl1t.ap),
                 reads=lgt.k() + nl1t.k(), writes=lgt.k())
            p.op("vector", lambda e: e.tensor_mul(lgt.ap, lgt.ap, ge2t.ap),
                 reads=lgt.k() + ge2t.k(), writes=lgt.k())
            p.op("vector", lambda e: e.reduce_sum(l1t.ap, lgt.ap, axis=AX.X),
                 reads=lgt.k(), writes=l1t.k())
            p.op("vector", lambda e: e.reciprocal(l1t.ap, l1t.ap),
                 reads=l1t.k(), writes=l1t.k())
            p.op("vector", lambda e, jt=jt: e.tensor_scalar_mul(
                gates[:, 8 * jt:8 * jt + 8], lgt.ap, l1t.ap),
                reads=lgt.k() + l1t.k(), writes=gates.k(8 * jt, 8 * jt + 8))

        def split_h(d0):
            p.op("scalar", lambda e, d0=d0: e.copy(
                b2h[:, d0:d0 + 512], PTb.ap),
                reads=PTb.k(), writes=b2h.k(d0, d0 + 512))
            p.op("vector", lambda e, d0=d0: e.tensor_sub(
                b2l[:, d0:d0 + 512], PTb.ap, b2h[:, d0:d0 + 512]),
                reads=PTb.k() + b2h.k(d0, d0 + 512), writes=b2l.k(d0, d0 + 512))

        def expert_h(e_idx, hbase):
            loadw(W1h, W1l, "e1", 0, 4096, eslice=e_idx)
            loadw(W1h, W1l, "e3", 4096, 4096, eslice=e_idx)
            for hc in range(4):
                bank1 = ps[hc % 2]
                bank3 = ps[2 + hc % 2]
                def h1_mm(e, hc=hc, bank1=bank1):
                    for ci in range(NCH):
                        last = mm3(
                            e, bank1.ap,
                            W1h[:, 512 * ci + 128 * hc:512 * ci + 128 * hc + 128],
                            W1l[:, 512 * ci + 128 * hc:512 * ci + 128 * hc + 128],
                            fmh[:, GT * ci:GT * ci + GT],
                            fml[:, GT * ci:GT * ci + GT],
                            ci == 0, ci == 7)
                    return last
                def h3_mm(e, hc=hc, bank3=bank3):
                    for ci in range(NCH):
                        last = mm3(
                            e, bank3.ap,
                            W1h[:, 4096 + 512 * ci + 128 * hc:4096 + 512 * ci + 128 * hc + 128],
                            W1l[:, 4096 + 512 * ci + 128 * hc:4096 + 512 * ci + 128 * hc + 128],
                            fmh[:, GT * ci:GT * ci + GT],
                            fml[:, GT * ci:GT * ci + GT],
                            ci == 0, ci == 7)
                    return last
                p.op("tensor", h1_mm,
                     reads=W1h.k(0, 4096) + W1l.k(0, 4096) + fm_keys(0, NCH * GT),
                     writes=bank1.k())
                p.op("tensor", h3_mm,
                     reads=W1h.k(4096, 8192) + W1l.k(4096, 8192) +
                           fm_keys(0, NCH * GT),
                     writes=bank3.k())
                p.op("scalar", lambda e, bank1=bank1: e.activation(
                    tmp5.ap, bank1.ap, AF.Silu),
                    reads=bank1.k(), writes=tmp5.k())
                p.op("vector", lambda e, bank3=bank3: e.tensor_mul(
                    PTb.ap, tmp5.ap, bank3.ap),
                    reads=tmp5.k() + bank3.k(), writes=PTb.k())
                split_h(hbase + 512 * hc)

        def y_hc(e, hbase, wcol, jt, half, bank, hc, start, stop):
            a0 = hbase + 512 * hc + 128 * jt
            w0 = wcol + 1024 * hc + 512 * half
            return mm3(e, bank.ap, b2h[:, a0:a0 + 128], b2l[:, a0:a0 + 128],
                       W2h[:, w0:w0 + 512], W2l[:, w0:w0 + 512], start, stop)

        def y_ops(hbase, wcol, jt, half, bank, start, stop=True):
            wk0 = wcol
            for hc in range(4):
                a0 = hbase + 512 * hc + 128 * jt
                w0 = wcol + 1024 * hc + 512 * half
                p.op("tensor",
                     lambda e, hc=hc, jt=jt, half=half, bank=bank:
                     y_hc(e, hbase, wcol, jt, half, bank, hc,
                          start and hc == 0, stop and hc == 3),
                     reads=b2h.k(a0, a0 + 128) + b2l.k(a0, a0 + 128) +
                           W2h.k(w0, w0 + 512) + W2l.k(w0, w0 + 512) +
                           ([] if (start and hc == 0) else [("cont", 0)]),
                     writes=bank.k())

        expert_h(8, 0)
        expert_h(9, 2048)
        loadw(W2h, W2l, "e2", 0, 4096, eslice=8)
        loadw(W2h, W2l, "e2", 4096, 4096, eslice=9)
        for jt in range(NT):
            for half in range(2):
                bank = ps[4 + (2 * jt + half) % 4]
                y_ops(0, 0, jt, half, bank, True, stop=False)
                y_ops(2048, 4096, jt, half, bank, False)
                a0 = D * jt + 512 * half
                p.op("vector", lambda e, a0=a0, bank=bank: e.tensor_add(
                    x[:, a0:a0 + 512], x[:, a0:a0 + 512], bank.ap),
                    reads=bank.k() + x.k(a0, a0 + 512), writes=x.k(a0, a0 + 512))

        for e_idx in range(8):
            expert_h(e_idx, 0)
            loadw(W2h, W2l, "e2", 0, 4096, eslice=e_idx)
            for jt in range(NT):
                for half in range(2):
                    bank = ps[4 + (2 * jt + half) % 4]
                    y_ops(0, 0, jt, half, bank, True)
                    a0 = D * jt + 512 * half
                    p.op("vector",
                         lambda e, a0=a0, bank=bank, jt=jt, e_idx=e_idx:
                         e.scalar_tensor_tensor(
                             x[:, a0:a0 + 512], bank.ap,
                             gates[:, 8 * jt + e_idx:8 * jt + e_idx + 1],
                             x[:, a0:a0 + 512], ALU.mult, ALU.add),
                         reads=bank.k() +
                               gates.k(8 * jt + e_idx, 8 * jt + e_idx + 1) +
                               x.k(a0, a0 + 512),
                         writes=x.k(a0, a0 + 512))
                    if e_idx == 7 and half == 1:
                        cb(jt)

    def attn_b(g, cb):
        transpose_split_fm(xn)
        for wname, dsth, dstl in (("wbq", b1h, b1l), ("wbk", b2h, b2l)):
            loadw(W1h, W1l, wname)
            for eo in range(NCH):
                bank = ps[eo % 2]
                def p_mm(e, eo=eo, bank=bank):
                    for ci in range(NCH):
                        last = mm3(
                            e, bank.ap,
                            W1h[:, 1024 * ci + 128 * eo:1024 * ci + 128 * eo + 128],
                            W1l[:, 1024 * ci + 128 * eo:1024 * ci + 128 * eo + 128],
                            fmh[:, GT * ci:GT * ci + GT],
                            fml[:, GT * ci:GT * ci + GT],
                            ci == 0, ci == 7)
                    return last
                p.op("tensor", p_mm,
                     reads=W1h.k() + W1l.k() + fm_keys(0, NCH * GT),
                     writes=bank.k())
                split_psum(bank, GT, dsth, dstl, GT * eo)
        for jt in range(NT):
            def st_mm(e, jt=jt):
                for ci in range(NCH):
                    a0 = GT * ci + 128 * jt
                    last = mm3(e, ps[2][:, 128 * jt:128 * jt + 128],
                               b2h[:, a0:a0 + 128], b2l[:, a0:a0 + 128],
                               b1h[:, a0:a0 + 128], b1l[:, a0:a0 + 128],
                               ci == 0, ci == 7)
                return last
            p.op("tensor", st_mm,
                 reads=[k for ci in range(NCH) for k in
                        b2h.k(GT * ci + 128 * jt, GT * ci + 128 * jt + 128) +
                        b2l.k(GT * ci + 128 * jt, GT * ci + 128 * jt + 128) +
                        b1h.k(GT * ci + 128 * jt, GT * ci + 128 * jt + 128) +
                        b1l.k(GT * ci + 128 * jt, GT * ci + 128 * jt + 128)],
                 writes=ps[2].k())
            p.op("vector", lambda e, jt=jt: e.tensor_mul(
                PTb[:, 0:128], ps[2][:, 128 * jt:128 * jt + 128], maskT.ap),
                reads=ps[2].k() + maskT.k(), writes=PTb.k(0, 128))
            p.op("scalar", lambda e, jt=jt: e.copy(
                stmh[:, 128 * jt:128 * jt + 128], PTb[:, 0:128]),
                reads=PTb.k(0, 128), writes=stmh.k(128 * jt, 128 * jt + 128))
            p.op("vector", lambda e, jt=jt: e.tensor_sub(
                stml[:, 128 * jt:128 * jt + 128], PTb[:, 0:128],
                stmh[:, 128 * jt:128 * jt + 128]),
                reads=PTb.k(0, 128) + stmh.k(128 * jt, 128 * jt + 128),
                writes=stml.k(128 * jt, 128 * jt + 128))
        loadw(W1h, W1l, "wbv")
        loadw(W2h, W2l, "wbg")
        for jt in range(NT):
            for half in range(2):
                vbank = ps[3 + half]
                def v_mm(e, jt=jt, half=half, vbank=vbank):
                    for ci in range(NCH):
                        a0 = GT * ci + 128 * jt
                        w0 = 1024 * ci + 512 * half
                        last = mm3(e, vbank.ap,
                                   fmh[:, a0:a0 + 128], fml[:, a0:a0 + 128],
                                   W1h[:, w0:w0 + 512], W1l[:, w0:w0 + 512],
                                   ci == 0, ci == 7)
                    return last
                p.op("tensor", v_mm,
                     reads=W1h.k() + W1l.k() +
                           [k for ci in range(NCH) for k in
                            fm_keys(GT * ci + 128 * jt, GT * ci + 128 * jt + 128)],
                     writes=vbank.k())
                split_psum(vbank, 512, b1h, b1l, D * jt + 512 * half)
            for half in range(2):
                gbank = ps[5 + half]
                obank = ps[3 + half]
                def g_mm(e, jt=jt, half=half, gbank=gbank):
                    for ci in range(NCH):
                        a0 = GT * ci + 128 * jt
                        w0 = 1024 * ci + 512 * half
                        last = mm3(e, gbank.ap,
                                   fmh[:, a0:a0 + 128], fml[:, a0:a0 + 128],
                                   W2h[:, w0:w0 + 512], W2l[:, w0:w0 + 512],
                                   ci == 0, ci == 7)
                    return last
                p.op("tensor", g_mm,
                     reads=W2h.k() + W2l.k() +
                           [k for ci in range(NCH) for k in
                            fm_keys(GT * ci + 128 * jt, GT * ci + 128 * jt + 128)],
                     writes=gbank.k())
                p.op("scalar", lambda e, gbank=gbank: e.activation(
                    tmp5.ap, gbank.ap, AF.Sigmoid),
                    reads=gbank.k(), writes=tmp5.k())
                a0 = D * jt + 512 * half
                def o_mm(e, jt=jt, a0=a0, obank=obank):
                    return mm3(e, obank.ap,
                               stmh[:, 128 * jt:128 * jt + 128],
                               stml[:, 128 * jt:128 * jt + 128],
                               b1h[:, a0:a0 + 512], b1l[:, a0:a0 + 512],
                               True, True)
                p.op("tensor", o_mm,
                     reads=stmh.k(128 * jt, 128 * jt + 128) +
                           stml.k(128 * jt, 128 * jt + 128) +
                           b1h.k(a0, a0 + 512) + b1l.k(a0, a0 + 512),
                     writes=obank.k())
                p.op("vector", lambda e, a0=a0, obank=obank: e.tensor_mul(
                    buf1[:, a0:a0 + 512], obank.ap, tmp5.ap),
                    reads=obank.k() + tmp5.k(), writes=buf1.k(a0, a0 + 512))
        transpose_split_fm(buf1)
        loadw(W1h, W1l, "wbo")
        for jt in range(NT):
            for half in range(2):
                bank = ps[half]
                def ob_mm(e, jt=jt, half=half, bank=bank):
                    for ci in range(NCH):
                        a0 = GT * ci + 128 * jt
                        w0 = 1024 * ci + 512 * half
                        last = mm3(e, bank.ap,
                                   fmh[:, a0:a0 + 128], fml[:, a0:a0 + 128],
                                   W1h[:, w0:w0 + 512], W1l[:, w0:w0 + 512],
                                   ci == 0, ci == 7)
                    return last
                p.op("tensor", ob_mm,
                     reads=W1h.k() + W1l.k() +
                           [k for ci in range(NCH) for k in
                            fm_keys(GT * ci + 128 * jt, GT * ci + 128 * jt + 128)],
                     writes=bank.k())
                a0 = D * jt + 512 * half
                p.op("vector", lambda e, a0=a0, bank=bank: e.tensor_add(
                    xn[:, a0:a0 + 512], xn[:, a0:a0 + 512], bank.ap),
                    reads=xn.k(a0, a0 + 512) + bank.k(), writes=xn.k(a0, a0 + 512))
                p.op("vector", lambda e, a0=a0: e.tensor_add(
                    x[:, a0:a0 + 512], x[:, a0:a0 + 512], xn[:, a0:a0 + 512]),
                    reads=xn.k(a0, a0 + 512) + x.k(a0, a0 + 512),
                    writes=x.k(a0, a0 + 512))
                if half == 1:
                    cb(jt)

    def store_tile(g, j):
        row0 = GT * g
        p.dma("gpsimd", x_out[row0 + 128 * j:row0 + 128 * j + 128, :],
              x[:, D * j:D * j + D], reads=x.k(D * j, D * j + D))

    for g in range(n_groups):
        row0 = GT * g
        for j in range(NT):
            p.dma("sync", x[:, D * j:D * j + D],
                  x_in[row0 + 128 * j:row0 + 128 * j + 128, :],
                  writes=x.k(D * j, D * j + D))
            ln_tile(j)
        for si, s in enumerate(subs):
            last = si == len(subs) - 1
            cb = ((lambda j, g=g: store_tile(g, j)) if last
                  else (lambda j: ln_tile(j)))
            if s == "a":
                attn_a(g, cb)
            elif s == "b":
                attn_b(g, cb)
            else:
                ffn(g, cb)

    p.dma_barrier("gpsimd", "gpsimd")
    p.finalize()
    n_ins = sum(len(v) for v in p.thunks.values())
    return nc, n_ins



def _img(wT):
    k, m = wT.shape
    return np.ascontiguousarray(
        wT.reshape(k // 128, 128, m).transpose(1, 0, 2).reshape(128, (k // 128) * m))


def make_inputs(inputs, core):
    f = np.ascontiguousarray
    g = lambda n: np.asarray(inputs[n])
    x = g("L0_attn_a_x")
    d = {
        "x_in": f(x[2 * core:2 * core + 2].reshape(CORE_TOK, D)),
        "wqT": _img(g("L0_attn_a_w_q").T),
        "wkvT": _img(g("L0_attn_a_w_kv").T),
        "woutT": _img(g("L0_attn_a_w_out").T),
        "ag_col": f(g("L0_attn_a_ln_g").reshape(NCH, 128).T),
        "ab_col": f(g("L0_attn_a_ln_b").reshape(NCH, 128).T),
        "og_rep": f(np.broadcast_to(g("L0_attn_a_lno_g"), (128, D))),
        "ob_rep": f(np.broadcast_to(g("L0_attn_a_lno_b"), (128, D))),
        "ld0": f(g("L0_attn_b_log_decay")[0:1].reshape(1, 1)),
        "r_w": _img(g("L0_ffn_router_w")),
        "rb_rep": f(np.broadcast_to(g("L0_ffn_router_b"), (128, 8))),
        "ew1": np.stack([_img(w) for w in np.concatenate(
            [g("L0_ffn_routed_w1"), g("L0_ffn_shared_w1")], 0)]),
        "ew3": np.stack([_img(w) for w in np.concatenate(
            [g("L0_ffn_routed_w3"), g("L0_ffn_shared_w3")], 0)]),
        "ew2": np.stack([_img(w) for w in np.concatenate(
            [g("L0_ffn_routed_w2"), g("L0_ffn_shared_w2")], 0)]),
        "ident": np.eye(128, dtype=np.float32),
        "eyemod": (np.arange(128)[:, None] % 64 == np.arange(64)[None, :]
                   ).astype(np.float32),
        "trimask": np.triu(np.full((128, 128), NEG, np.float32), 1),
        "ones128": np.ones((1, 128), np.float32),
    }
    for w in ("q", "k", "v", "g", "o"):
        d["wb%sT" % w] = _img(g("L0_attn_b_W_" + w).T)
    u = np.arange(128)[:, None]
    q = np.arange(128)[None, :]
    low = ((u // 32) == (q // 32)) & (q >= u)
    d["diffT"] = ((q - u) * low).astype(np.float32)
    d["indT"] = low.astype(np.float32)
    return d


def kernel(**inputs):
    nc, _ = build_core()
    in_maps = [make_inputs(inputs, c) for c in range(8)]
    res = run_bass_kernel_spmd(nc, in_maps, core_ids=list(range(8)))
    out = np.concatenate([r["x_out"].reshape(2, SEQ, D) for r in res.results], 0)
    return out.astype(np.float32)


# version 05
# speedup vs baseline: 1.0090x; 1.0090x over previous
import math
import numpy as np
import concourse.bass as bass
import concourse.mybir as mybir
from concourse.bass_utils import run_bass_kernel_spmd

F32 = mybir.dt.float32
BF16 = mybir.dt.bfloat16
AF = mybir.ActivationFunctionType
ALU = mybir.AluOpType
AX = mybir.AxisListType

D = 1024
NH = 16
NCH = 8
GT = 512
NT = 4
CORE_TOK = 4096
SEQ = 2048
EPS = 1e-5
QSCALE = (64 ** -0.5) * math.sqrt(64 ** -0.5)
KSCALE = math.sqrt(64 ** -0.5)
NEG = -1.0e30

ENGS = ("sync", "gpsimd", "scalar", "vector", "tensor")


class Buf:

    def __init__(self, ap, name, gran):
        self.ap = ap
        self.name = name
        self.gran = gran

    def __getitem__(self, sl):
        return self.ap[sl]

    def k(self, c0=None, c1=None):
        if c0 is None:
            c0, c1 = 0, self.ap.shape[-1]
        lo = c0 // self.gran
        hi = (c1 + self.gran - 1) // self.gran
        return [(self.name, b) for b in range(lo, hi)]


class Prog:

    def __init__(self, nc):
        self.nc = nc
        self.thunks = {e: [] for e in ENGS}
        self.sems = {}
        self.cnt = {}
        self.stepv = {}
        self.waited = {}
        self.seq = {e: 0 for e in ENGS}
        self.lastdrain = {e: 0 for e in ENGS}
        self.state = {}
        self.engsem = {"tensor": "tsem", "vector": "vsem",
                       "scalar": "ssem", "gpsimd": "gsem"}
        for s in ("tsem", "vsem", "ssem", "gsem"):
            self._mksem(s, 1)
        self.pool = {"sync": ["q%d" % i for i in range(16)],
                     "gpsimd": ["gq%d" % i for i in range(8)]}
        for q in self.pool["sync"] + self.pool["gpsimd"]:
            self._mksem(q, 16)
        self.pool_i = {"sync": 0, "gpsimd": 0}
        self.ndrain = {e: 0 for e in ENGS}

    def _mksem(self, name, step):
        self.sems[name] = self.nc.alloc_semaphore(name=name)
        self.cnt[name] = 0
        self.stepv[name] = step

    def _wait(self, eng, sem, val):
        if val <= self.waited.get((eng, sem), 0):
            return
        h = self.sems[sem]
        self.thunks[eng].append(lambda e, s=h, v=val: e.wait_ge(s, v))
        self.waited[(eng, sem)] = val

    def _drain(self, eng):
        self.thunks[eng].append(lambda e: e.drain())
        self.seq[eng] += 1
        self.lastdrain[eng] = self.seq[eng]
        self.ndrain[eng] += 1

    def _deps(self, eng, reads, writes, waw_ok):
        need = []
        for k in reads:
            st = self.state.get(k)
            if st and st[0] is not None:
                need.append((st[0], False))
        for k in writes:
            st = self.state.get(k)
            if st:
                if st[0] is not None:
                    need.append((st[0], waw_ok))
                for r in st[1]:
                    need.append((r, False))
        drained = False
        for (deng, dsem, dval, dseq), skip_same in need:
            if deng == eng:
                if (eng != "tensor" and not skip_same
                        and dseq > self.lastdrain[eng] and not drained):
                    self._drain(eng)
                    drained = True
            else:
                self._wait(eng, dsem, dval)

    def _record(self, acc, reads, writes):
        for k in writes:
            self.state[k] = [acc, []]
        for k in reads:
            self.state.setdefault(k, [None, []])[1].append(acc)

    def op(self, eng, fn, reads=(), writes=(), waw_ok=False):
        self._deps(eng, reads, writes, waw_ok or eng == "tensor")
        sem = self.engsem[eng]
        h, step = self.sems[sem], self.stepv[sem]

        def thunk(e, fn=fn, h=h, step=step):
            fn(e).then_inc(h, step)
        self.thunks[eng].append(thunk)
        self.cnt[sem] += step
        self.seq[eng] += 1
        acc = (eng, sem, self.cnt[sem], self.seq[eng])
        self._record(acc, reads, writes)
        return acc

    def dma(self, eng, out, in_, reads=(), writes=()):
        self._deps(eng, reads, writes, False)
        sems = self.pool[eng]
        sem = sems[self.pool_i[eng] % len(sems)]
        self.pool_i[eng] += 1
        if self.cnt[sem]:
            self._wait(eng, sem, self.cnt[sem])
        h = self.sems[sem]

        def thunk(e, o=out, i=in_, h=h):
            e.dma_start(out=o, in_=i).then_inc(h, 16)
        self.thunks[eng].append(thunk)
        self.cnt[sem] += 16
        self.seq[eng] += 1
        acc = ("dma:" + sem, sem, self.cnt[sem], self.seq[eng])
        self._record(acc, reads, writes)
        return acc

    def dma_barrier(self, eng, queue):
        for sem in self.pool[queue]:
            if self.cnt[sem]:
                self._wait(eng, sem, self.cnt[sem])

    def finalize(self):
        with self.nc.Block() as block:
            block.sync(lambda e: [t(e) for t in self.thunks["sync"]])
            block.gpsimd(lambda e: [t(e) for t in self.thunks["gpsimd"]])
            block.scalar(lambda e: [t(e) for t in self.thunks["scalar"]])
            block.vector(lambda e: [t(e) for t in self.thunks["vector"]])
            block.tensor(lambda e: [t(e) for t in self.thunks["tensor"]])


def build_core(cfg=None):
    cfg = cfg or {}
    n_groups = cfg.get("n_groups", 8)
    subs = cfg.get("subs", ("a", "f", "b", "f"))

    nc = bass.Bass("TRN2", target_bir_lowering=False, debug=False)
    p = Prog(nc)

    hbm = {}

    def hin(name, shape):
        hbm[name] = nc.dram_tensor(name, list(shape), F32, kind="ExternalInput").ap()
        return hbm[name]

    x_in = hin("x_in", (CORE_TOK, D))
    x_out = nc.dram_tensor("x_out", [CORE_TOK, D], F32, kind="ExternalOutput").ap()
    hin("wqT", (128, 8192))
    hin("wkvT", (128, 1024))
    hin("woutT", (128, 8192))
    hin("ag_col", (128, NCH))
    hin("ab_col", (128, NCH))
    hin("og_rep", (128, D))
    hin("ob_rep", (128, D))
    for w in ("q", "k", "v", "g", "o"):
        hin("wb%sT" % w, (128, 8192))
    hin("ld0", (1, 1))
    hin("r_w", (128, 64))
    hin("rb_rep", (128, 8))
    ew1 = hin("ew1", (10, 128, 4096))
    ew3 = hin("ew3", (10, 128, 4096))
    ew2 = hin("ew2", (10, 128, 4096))
    hin("ident", (128, 128))
    hin("eyemod", (128, 64))
    hin("trimask", (128, 128))
    hin("diffT", (128, 128))
    hin("indT", (128, 128))
    hin("ones128", (1, 128))

    wsplit = {}
    for name, cols in (("wq", 8192), ("wkv", 1024), ("wout", 8192),
                       ("wbq", 8192), ("wbk", 8192), ("wbv", 8192),
                       ("wbg", 8192), ("wbo", 8192)):
        wsplit[name] = (
            nc.dram_tensor("s_%s_h" % name, [128, cols], BF16, kind="Internal").ap(),
            nc.dram_tensor("s_%s_l" % name, [128, cols], BF16, kind="Internal").ap())
    for name in ("e1", "e3", "e2"):
        wsplit[name] = (
            nc.dram_tensor("s_%s_h" % name, [10, 128, 4096], BF16,
                           kind="Internal").ap(),
            nc.dram_tensor("s_%s_l" % name, [10, 128, 4096], BF16,
                           kind="Internal").ap())

    def sb(n, cols, gran, rows=128, dt=F32):
        ap = nc.alloc_sbuf_tensor(n, [rows, cols], dt).ap()
        return Buf(ap, n, gran)

    x = sb("x", NT * D, 1024)
    xn = sb("xn", NT * D, 512)
    fmh = sb("fmh", NCH * GT, 128, dt=BF16)
    fml = sb("fml", NCH * GT, 128, dt=BF16)
    buf1 = sb("buf1", NCH * GT, 128)
    b1h = sb("b1h", NCH * GT, 128, dt=BF16)
    b1l = sb("b1l", NCH * GT, 128, dt=BF16)
    b2h = sb("b2h", NCH * GT, 128, dt=BF16)
    b2l = sb("b2l", NCH * GT, 128, dt=BF16)
    Pb = sb("P", SEQ, 512)
    PTb = sb("PT", GT, 128)
    stmh = sb("stmh", NT * 128, 128, dt=BF16)
    stml = sb("stml", NT * 128, 128, dt=BF16)
    kcache = sb("kcache", 2 * SEQ, 512)
    vcache = sb("vcache", SEQ, 64)
    W1h = sb("W1h", 8192, 512, dt=BF16)
    W1l = sb("W1l", 8192, 512, dt=BF16)
    W2h = sb("W2h", 8192, 512, dt=BF16)
    W2l = sb("W2l", 8192, 512, dt=BF16)
    tmp5 = sb("tmp5", GT, 512)
    ident = sb("identS", 128, 128)
    eyemod = sb("eyemodS", 64, 64)
    trimask = sb("trimaskS", 128, 128)
    maskT = sb("maskTS", 128, 128)
    ones128 = sb("ones128S", 128, 128, rows=1)
    agc = sb("agc", NCH, 1)
    abc = sb("abc", NCH, 1)
    rbr = sb("rbr", 8, 8)
    rwh = sb("rwh", NCH * 8, 8, dt=BF16)
    rwl = sb("rwl", NCH * 8, 8, dt=BF16)
    st6 = sb("st6", 48, 6)
    mv = sb("mv", 2 * NT, 2)
    rr = sb("rr", NT, 1)
    ss = sb("ss", NT, 1)
    mx4 = sb("mx4", 8, 1)
    sums4 = sb("sums4", 8, 1)
    rsum = sb("rsum", 2, 1)
    biasq = sb("biasq", NCH, 8)
    biaskv = sb("biaskv", 1, 1)
    gates = sb("gates", NT * 8, 8)
    lgt = sb("lgt", 8, 8)
    l1t = sb("l1t", 1, 1)
    nl1t = sb("nl1t", 1, 1)
    l2t = sb("l2t", 1, 1)
    m1t = sb("m1t", 8, 8)
    ge2t = sb("ge2t", 8, 8)
    epsc = sb("epsc", 1, 1)
    nhc = sb("nhc", 1, 1)
    dect = sb("dect", 1, 1, rows=1)
    lnd128 = sb("lnd128", 1, 1)

    ps = [Buf(nc.alloc_psum_tensor("ps%d" % i, [128, 512], F32).ap(),
              "ps%d" % i, 512) for i in range(8)]

    for name, dst in (("ident", ident), ("eyemod", eyemod),
                      ("trimask", trimask), ("ones128", ones128),
                      ("ag_col", agc), ("ab_col", abc), ("rb_rep", rbr),
                      ("ld0", dect),):
        p.dma("sync", dst.ap, hbm[name], writes=dst.k())
    p.op("vector", lambda e: e.memset(epsc.ap, float(EPS)), writes=epsc.k())
    p.op("vector", lambda e: e.memset(nhc.ap, -0.5), writes=nhc.k())

    p.dma("sync", PTb[:, 0:128], hbm["diffT"], writes=PTb.k(0, 128))
    p.dma("sync", tmp5[:, 0:128], hbm["indT"], writes=tmp5.k(0, 128))
    p.op("scalar", lambda e: e.activation(dect.ap, dect.ap, AF.Sigmoid),
         reads=dect.k(), writes=dect.k())
    p.op("scalar", lambda e: e.activation(dect.ap, dect.ap, AF.Ln),
         reads=dect.k(), writes=dect.k())
    p.op("tensor", lambda e: e.matmul(ps[0][:, 0:1], ones128.ap, dect.ap,
                                      start=True, stop=True),
         reads=dect.k() + ones128.k(), writes=ps[0].k())
    p.op("vector", lambda e: e.tensor_copy(lnd128.ap, ps[0][:, 0:1]),
         reads=ps[0].k(), writes=lnd128.k())
    p.op("scalar", lambda e: e.activation(maskT.ap, PTb[:, 0:128], AF.Exp,
                                          scale=lnd128.ap),
         reads=PTb.k(0, 128) + lnd128.k(), writes=maskT.k())
    p.op("vector", lambda e: e.tensor_mul(maskT.ap, maskT.ap, tmp5[:, 0:128]),
         reads=maskT.k() + tmp5.k(0, 128), writes=maskT.k())

    def wprep(src_h, cols, dst_pair, prep=None, eslice=None):
        npc = (cols + 2047) // 2048
        for pc in range(npc):
            w = min(2048, cols - 2048 * pc)
            src = src_h[eslice] if eslice is not None else src_h
            p.dma("sync", Pb[:, 0:w], src[:, 2048 * pc:2048 * pc + w],
                  writes=Pb.k(0, w))
            if prep is not None:
                prep(pc, w)
            p.op("scalar", lambda e, w=w: e.copy(b2h[:, 0:w], Pb[:, 0:w]),
                 reads=Pb.k(0, w), writes=b2h.k(0, w))
            p.op("vector", lambda e, w=w: e.tensor_sub(
                b2l[:, 0:w], Pb[:, 0:w], b2h[:, 0:w]),
                reads=Pb.k(0, w) + b2h.k(0, w), writes=b2l.k(0, w))
            dh, dl = dst_pair
            dsth = dh[eslice] if eslice is not None else dh
            dstl = dl[eslice] if eslice is not None else dl
            p.dma("gpsimd", dsth[:, 2048 * pc:2048 * pc + w], b2h[:, 0:w],
                  reads=b2h.k(0, w))
            p.dma("gpsimd", dstl[:, 2048 * pc:2048 * pc + w], b2l[:, 0:w],
                  reads=b2l.k(0, w))

    for pc in range(4):
        p.dma("sync", Pb[:, 0:2048], hbm["wqT"][:, 2048 * pc:2048 * pc + 2048],
              writes=Pb.k(0, 2048))
        def bq_mm(e, pc=pc):
            for eo in range(NCH):
                for i, ci in enumerate(range(2 * pc, 2 * pc + 2)):
                    lo = 1024 * (ci - 2 * pc)
                    last = e.matmul(
                        ps[4][:, eo:eo + 1],
                        Pb[:, lo + 128 * eo:lo + 128 * eo + 128],
                        abc[:, ci:ci + 1],
                        start=(i == 0), stop=(i == 1))
            return last
        p.op("tensor", bq_mm, reads=Pb.k(0, 2048) + abc.k(), writes=ps[4].k(),
             waw_ok=True)
        if pc == 0:
            p.op("vector", lambda e: e.tensor_copy(biasq.ap, ps[4][:, 0:NCH]),
                 reads=ps[4].k(), writes=biasq.k())
        else:
            p.op("vector", lambda e: e.tensor_add(
                biasq.ap, biasq.ap, ps[4][:, 0:NCH]),
                reads=ps[4].k() + biasq.k(), writes=biasq.k())
    p.op("vector", lambda e: e.tensor_scalar_mul(
        biasq.ap, biasq.ap, float(QSCALE)),
        reads=biasq.k(), writes=biasq.k())

    def prep_wq(pc, w):
        for ci in range(2 * pc, 2 * pc + w // 1024):
            lo = 1024 * (ci - 2 * pc)
            p.op("vector", lambda e, lo=lo, ci=ci: e.tensor_scalar(
                Pb[:, lo:lo + 1024], Pb[:, lo:lo + 1024],
                agc[:, ci:ci + 1], float(QSCALE), ALU.mult, ALU.mult),
                reads=Pb.k(lo, lo + 1024) + agc.k(),
                writes=Pb.k(lo, lo + 1024))
    wprep(hbm["wqT"], 8192, wsplit["wq"], prep=prep_wq)

    p.dma("sync", Pb[:, 0:1024], hbm["wkvT"], writes=Pb.k(0, 1024))
    def bkv_mm(e):
        for ci in range(NCH):
            last = e.matmul(ps[2][:, 0:1], Pb[:, 128 * ci:128 * ci + 128],
                            abc[:, ci:ci + 1], start=(ci == 0), stop=(ci == 7))
        return last
    p.op("tensor", bkv_mm, reads=Pb.k(0, 1024) + abc.k(), writes=ps[2].k())
    p.op("vector", lambda e: e.tensor_copy(biaskv.ap, ps[2][:, 0:1]),
         reads=ps[2].k(), writes=biaskv.k())
    p.op("vector", lambda e: e.tensor_scalar_mul(
        biaskv[0:64, :], biaskv[0:64, :], float(KSCALE)),
        reads=biaskv.k(), writes=biaskv.k())

    def prep_wkv(pc, w):
        for ci in range(NCH):
            p.op("vector", lambda e, ci=ci: e.tensor_scalar_mul(
                Pb[:, 128 * ci:128 * ci + 128], Pb[:, 128 * ci:128 * ci + 128],
                agc[:, ci:ci + 1]),
                reads=Pb.k(128 * ci, 128 * ci + 128) + agc.k(),
                writes=Pb.k(128 * ci, 128 * ci + 128))
            p.op("vector", lambda e, ci=ci: e.tensor_scalar_mul(
                Pb[:, 128 * ci:128 * ci + 64], Pb[:, 128 * ci:128 * ci + 64],
                float(KSCALE)),
                reads=Pb.k(128 * ci, 128 * ci + 64),
                writes=Pb.k(128 * ci, 128 * ci + 64))
    wprep(hbm["wkvT"], 1024, wsplit["wkv"], prep=prep_wkv)

    wprep(hbm["woutT"], 8192, wsplit["wout"])
    for ei in (8, 9, 0, 1, 2, 3, 4, 5, 6, 7):
        wprep(ew1, 4096, wsplit["e1"], eslice=ei)
        wprep(ew3, 4096, wsplit["e3"], eslice=ei)
        wprep(ew2, 4096, wsplit["e2"], eslice=ei)
    for w in ("q", "k", "v", "g", "o"):
        wprep(hbm["wb%sT" % w], 8192, wsplit["wb" + w])
    p.dma("sync", PTb[:, 0:64], hbm["r_w"], writes=PTb.k(0, 64))
    p.op("scalar", lambda e: e.copy(rwh.ap, PTb[:, 0:64]),
         reads=PTb.k(0, 64), writes=rwh.k())
    p.op("vector", lambda e: e.tensor_sub(rwl.ap, PTb[:, 0:64], rwh.ap),
         reads=PTb.k(0, 64) + rwh.k(), writes=rwl.k())

    evac_tgl = [0]

    def evac_eng():
        evac_tgl[0] ^= 1
        return "vector" if evac_tgl[0] else "scalar"

    def copy_op(eng, dst, src):
        if eng == "vector":
            return lambda e: e.tensor_copy(dst, src)
        return lambda e: e.copy(dst, src)

    def split_psum(bank, w, dsth, dstl, d0):
        p.op("scalar", lambda e: e.copy(dsth[:, d0:d0 + w], bank[:, 0:w]),
             reads=bank.k(), writes=dsth.k(d0, d0 + w))
        p.op("vector", lambda e: e.tensor_sub(
            dstl[:, d0:d0 + w], bank[:, 0:w], dsth[:, d0:d0 + w]),
            reads=bank.k() + dsth.k(d0, d0 + w), writes=dstl.k(d0, d0 + w))

    def ln_tile(j, src=None, dst=None):
        src = src if src is not None else x
        dst = dst if dst is not None else xn
        sl = src[:, D * j:D * j + D]
        s0 = 12 * j
        p.op("vector", lambda e, sl=sl, s0=s0: e.bn_stats(
            st6[:, s0:s0 + 6], sl[:, 0:512]),
            reads=src.k(D * j, D * j + 512), writes=st6.k(s0, s0 + 6))
        p.op("vector", lambda e, sl=sl, s0=s0: e.bn_stats(
            st6[:, s0 + 6:s0 + 12], sl[:, 512:1024]),
            reads=src.k(D * j + 512, D * j + D), writes=st6.k(s0 + 6, s0 + 12))
        p.op("vector", lambda e, j=j, s0=s0: e.bn_aggr(
            mv[:, 2 * j:2 * j + 2], st6[:, s0:s0 + 12]),
            reads=st6.k(s0, s0 + 12), writes=mv.k(2 * j, 2 * j + 2))
        p.op("scalar", lambda e, j=j: e.activation(
            rr[:, j:j + 1], mv[:, 2 * j + 1:2 * j + 2], AF.Ln, bias=epsc.ap),
            reads=mv.k(2 * j + 1, 2 * j + 2) + epsc.k(), writes=rr.k(j, j + 1))
        p.op("scalar", lambda e, j=j: e.activation(
            rr[:, j:j + 1], rr[:, j:j + 1], AF.Exp, scale=nhc.ap),
            reads=rr.k(j, j + 1) + nhc.k(), writes=rr.k(j, j + 1))
        p.op("vector", lambda e, j=j: e.scalar_tensor_tensor(
            ss[:, j:j + 1], mv[:, 2 * j:2 * j + 1], -1.0, rr[:, j:j + 1],
            ALU.mult, ALU.mult),
            reads=mv.k(2 * j, 2 * j + 1) + rr.k(j, j + 1), writes=ss.k(j, j + 1))
        p.op("scalar", lambda e, j=j: e.activation(
            dst[:, D * j:D * j + D], src[:, D * j:D * j + D], AF.Identity,
            bias=ss[:, j:j + 1], scale=rr[:, j:j + 1]),
            reads=src.k(D * j, D * j + D) + rr.k(j, j + 1) + ss.k(j, j + 1),
            writes=dst.k(D * j, D * j + D))

    def ln_pass(src, dst):
        for j in range(NT):
            ln_tile(j, src, dst)

    def transpose_split_fm(src):
        for ci in range(NCH):
            bank = ps[ci % 2]
            for j in range(NT):
                p.op("tensor", lambda e, ci=ci, j=j, bank=bank: e.transpose(
                    bank[:, 128 * j:128 * j + 128],
                    src[:, D * j + 128 * ci:D * j + 128 * ci + 128], ident.ap),
                    reads=src.k(D * j + 128 * ci, D * j + 128 * ci + 128) + ident.k(),
                    writes=bank.k(), waw_ok=True)
            split_psum(bank, GT, fmh, fml, GT * ci)

    def loadw(slot_h, slot_l, name, col0=0, cols=8192, eslice=None):
        sh, sl_ = wsplit[name]
        if eslice is not None:
            sh, sl_ = sh[eslice], sl_[eslice]
        p.dma("sync", slot_h[:, col0:col0 + cols], sh[:, 0:cols],
              writes=slot_h.k(col0, col0 + cols))
        p.dma("sync", slot_l[:, col0:col0 + cols], sl_[:, 0:cols],
              writes=slot_l.k(col0, col0 + cols))

    def mm3(e, bank_ap, wh, wl, xh, xl, start, stop):
        e.matmul(bank_ap, wh, xh, start=start, stop=False)
        e.matmul(bank_ap, wh, xl, start=False, stop=False)
        return e.matmul(bank_ap, wl, xh, start=False, stop=stop)

    def fm_keys(c0, c1):
        return fmh.k(c0, c1) + fml.k(c0, c1)

    def attn_a(g, cb):
        grel = g % 4
        seqh = g // 4
        kcb = SEQ * seqh
        vb = 1024 * seqh
        ln_pass(xn, xn)
        transpose_split_fm(xn)

        loadw(W2h, W2l, "wkv", 0, 1024)
        def kv_mm(e):
            for ci in range(NCH):
                last = mm3(e, ps[2].ap,
                           W2h[:, 128 * ci:128 * ci + 128],
                           W2l[:, 128 * ci:128 * ci + 128],
                           fmh[:, GT * ci:GT * ci + GT],
                           fml[:, GT * ci:GT * ci + GT],
                           ci == 0, ci == 7)
            return last
        p.op("tensor", kv_mm,
             reads=W2h.k(0, 1024) + W2l.k(0, 1024) + fm_keys(0, NCH * GT),
             writes=ps[2].k())
        p.op("vector", lambda e: e.tensor_scalar_add(PTb.ap, ps[2].ap, biaskv.ap),
             reads=ps[2].k() + biaskv.k(), writes=PTb.k())
        ktok = kcb + 512 * grel
        p.dma("gpsimd", kcache[64:128, ktok:ktok + 512], PTb[0:64, :],
              reads=PTb.k(), writes=kcache.k(ktok, ktok + 512))
        p.op("vector", lambda e, ktok=ktok: e.tensor_copy(
            kcache[0:64, ktok:ktok + 512], PTb[0:64, :]),
            reads=PTb.k(), writes=kcache.k(ktok, ktok + 512))
        for j in range(NT):
            p.op("tensor", lambda e, j=j: e.transpose(
                ps[3][:, 64 * j:64 * j + 64], PTb[64:128, 128 * j:128 * j + 128],
                eyemod.ap[64:128, :]),
                reads=PTb.k(128 * j, 128 * j + 128) + eyemod.k(),
                writes=ps[3].k(), waw_ok=True)
        vtok = vb + 256 * grel
        p.op("vector", lambda e, vtok=vtok: e.tensor_copy(
            vcache[:, vtok:vtok + 256], ps[3][:, 0:256]),
            reads=ps[3].k(), writes=vcache.k(vtok, vtok + 256))

        loadw(W1h, W1l, "wq")
        for eo in range(NCH):
            bank = ps[4 + (eo % 2)]
            def q_mm(e, eo=eo, bank=bank):
                for ci in range(NCH):
                    last = mm3(e, bank.ap,
                               W1h[:, 1024 * ci + 128 * eo:1024 * ci + 128 * eo + 128],
                               W1l[:, 1024 * ci + 128 * eo:1024 * ci + 128 * eo + 128],
                               fmh[:, GT * ci:GT * ci + GT],
                               fml[:, GT * ci:GT * ci + GT],
                               ci == 0, ci == 7)
                return last
            p.op("tensor", q_mm, reads=W1h.k() + W1l.k() + fm_keys(0, NCH * GT),
                 writes=bank.k())
            p.op("vector", lambda e, eo=eo, bank=bank: e.tensor_scalar_add(
                buf1[:, GT * eo:GT * eo + GT], bank.ap, biasq[:, eo:eo + 1]),
                reads=bank.k() + biasq.k(), writes=buf1.k(GT * eo, GT * eo + GT))

        pairs = [(h, jt) for jt in range(NT) for h in range(NH)]

        def att_S(n):
            h, jt = pairs[n]
            hrow = 64 * (h % 2)
            ig = 4 * grel + jt
            nk = (ig + 1) * 128
            nchunk = (nk + 511) // 512
            lq0 = GT * (h // 2) + 128 * jt
            lhs_q = buf1[hrow:hrow + 64, lq0:lq0 + 128]
            for c in range(nchunk):
                w = min(512, nk - 512 * c)
                p.op("tensor", lambda e, c=c, w=w, lhs_q=lhs_q, hrow=hrow:
                     e.matmul(
                         ps[c][:, 0:w], lhs_q,
                         kcache[hrow:hrow + 64, kcb + 512 * c:kcb + 512 * c + w],
                         start=True, stop=True),
                     reads=buf1.k(lq0, lq0 + 128) +
                           kcache.k(kcb + 512 * c, kcb + 512 * c + w),
                     writes=ps[c].k())
            dc, doff = (ig * 128) // 512, (ig * 128) % 512
            p.op("vector", lambda e, dc=dc, doff=doff: e.tensor_add(
                ps[dc][:, doff:doff + 128], ps[dc][:, doff:doff + 128],
                trimask.ap),
                reads=ps[dc].k() + trimask.k(), writes=ps[dc].k())

        def att_p1(n):
            h, jt = pairs[n]
            ig = 4 * grel + jt
            nk = (ig + 1) * 128
            nchunk = (nk + 511) // 512
            s4 = 4 * (n % 2)
            for c in range(nchunk):
                w = min(512, nk - 512 * c)
                p.op("vector", lambda e, c=c, w=w, s4=s4: e.reduce_max(
                    mx4[:, s4 + c:s4 + c + 1], ps[c][:, 0:w], axis=AX.X,
                    negate=True),
                    reads=ps[c].k(), writes=mx4.k(s4 + c, s4 + c + 1))
            if nchunk > 1:
                p.op("vector", lambda e, n_=nchunk, s4=s4: e.tensor_reduce(
                    mx4[:, s4:s4 + 1], mx4[:, s4:s4 + n_], AX.X, ALU.min),
                    reads=mx4.k(s4, s4 + nchunk), writes=mx4.k(s4, s4 + 1))
            for c in range(nchunk):
                w = min(512, nk - 512 * c)
                p.op("scalar", lambda e, c=c, w=w, s4=s4: e.activation(
                    Pb[:, 512 * c:512 * c + w], ps[c][:, 0:w], AF.Exp,
                    bias=mx4[:, s4:s4 + 1], accum_out=sums4[:, s4 + c:s4 + c + 1]),
                    reads=ps[c].k() + mx4.k(s4, s4 + 1),
                    writes=Pb.k(512 * c, 512 * c + w) + sums4.k(s4 + c, s4 + c + 1))

        def att_p2v(n):
            h, jt = pairs[n]
            nk = (4 * grel + jt + 1) * 128
            nchunk = (nk + 511) // 512
            s4 = 4 * (n % 2)
            r0 = n % 2
            p.op("vector", lambda e, n_=nchunk, s4=s4, r0=r0: e.tensor_reduce(
                rsum[:, r0:r0 + 1], sums4[:, s4:s4 + n_], AX.X, ALU.add),
                reads=sums4.k(s4, s4 + nchunk), writes=rsum.k(r0, r0 + 1))
            p.op("vector", lambda e, r0=r0: e.reciprocal(
                rsum[:, r0:r0 + 1], rsum[:, r0:r0 + 1]),
                reads=rsum.k(r0, r0 + 1), writes=rsum.k(r0, r0 + 1))

        def att_p2s(n):
            h, jt = pairs[n]
            nk = (4 * grel + jt + 1) * 128
            nchunk = (nk + 511) // 512
            r0 = n % 2
            for c in range(nchunk):
                w = min(512, nk - 512 * c)
                p.op("scalar", lambda e, c=c, w=w, r0=r0: e.activation(
                    Pb[:, 512 * c:512 * c + w], Pb[:, 512 * c:512 * c + w],
                    AF.Identity, scale=rsum[:, r0:r0 + 1]),
                    reads=Pb.k(512 * c, 512 * c + w) + rsum.k(r0, r0 + 1),
                    writes=Pb.k(512 * c, 512 * c + w))

        def att_ptpv(n):
            h, jt = pairs[n]
            hrow = 64 * (h % 2)
            obank = ps[7] if h % 2 == 0 else ps[5]
            ig = 4 * grel + jt
            lq0 = GT * (h // 2) + 128 * jt
            nkb = ig + 1
            for b0 in range(0, nkb, 4):
                bcnt = min(4, nkb - b0)
                ptbank = ps[6] if (b0 // 4) % 2 == 0 else ps[4]
                for bi in range(bcnt):
                    kb = b0 + bi
                    p.op("tensor", lambda e, kb=kb, bi=bi, ptbank=ptbank:
                         e.transpose(
                             ptbank[:, 128 * bi:128 * bi + 128],
                             Pb[:, 128 * kb:128 * kb + 128], ident.ap),
                         reads=Pb.k(128 * kb, 128 * kb + 128) + ident.k(),
                         writes=ptbank.k(), waw_ok=True)
                eng = evac_eng()
                p.op(eng, copy_op(eng, PTb[:, 0:128 * bcnt],
                                  ptbank[:, 0:128 * bcnt]),
                     reads=ptbank.k(), writes=PTb.k(0, 128 * bcnt))
                def pv(e, b0=b0, bcnt=bcnt, jt=jt, nkb=nkb, vb=vb,
                       obank=obank, hrow=hrow):
                    tp = None if hrow == 0 else (0, 64)
                    for bi in range(bcnt):
                        kb = b0 + bi
                        last = e.matmul(
                            obank[hrow:hrow + 64, 128 * jt:128 * jt + 128],
                            vcache[:, vb + 64 * kb:vb + 64 * kb + 64],
                            PTb[:, 128 * bi:128 * bi + 128],
                            start=(kb == 0), stop=(kb == nkb - 1),
                            tile_position=tp)
                    return last
                p.op("tensor", pv,
                     reads=PTb.k(0, 128 * bcnt) + vcache.k(vb, vb + 64 * nkb),
                     writes=obank.k(), waw_ok=True)
            p.op("scalar", lambda e, hrow=hrow, lq0=lq0, jt=jt, obank=obank:
                 e.copy(b2h[hrow:hrow + 64, lq0:lq0 + 128],
                        obank[hrow:hrow + 64, 128 * jt:128 * jt + 128]),
                 reads=obank.k(), writes=b2h.k(lq0, lq0 + 128))
            p.op("vector", lambda e, hrow=hrow, lq0=lq0, jt=jt, obank=obank:
                 e.tensor_sub(b2l[hrow:hrow + 64, lq0:lq0 + 128],
                              obank[hrow:hrow + 64, 128 * jt:128 * jt + 128],
                              b2h[hrow:hrow + 64, lq0:lq0 + 128]),
                 reads=obank.k() + b2h.k(lq0, lq0 + 128),
                 writes=b2l.k(lq0, lq0 + 128))

        att_S(0)
        for n in range(len(pairs)):
            att_p1(n)
            att_p2v(n)
            if n + 1 < len(pairs):
                att_S(n + 1)
            att_p2s(n)
            att_ptpv(n)

        p.dma("sync", Pb[:, 0:1024], hbm["og_rep"], writes=Pb.k(0, 1024))
        p.dma("sync", Pb[:, 1024:2048], hbm["ob_rep"], writes=Pb.k(1024, 2048))
        loadw(W2h, W2l, "wout")
        for jt in range(NT):
            for half in range(2):
                bank = ps[half]
                def o_mm(e, jt=jt, half=half, bank=bank):
                    for ci in range(NCH):
                        last = mm3(
                            e, bank.ap,
                            b2h[:, GT * ci + 128 * jt:GT * ci + 128 * jt + 128],
                            b2l[:, GT * ci + 128 * jt:GT * ci + 128 * jt + 128],
                            W2h[:, 1024 * ci + 512 * half:1024 * ci + 512 * half + 512],
                            W2l[:, 1024 * ci + 512 * half:1024 * ci + 512 * half + 512],
                            ci == 0, ci == 7)
                    return last
                p.op("tensor", o_mm,
                     reads=W2h.k() + W2l.k() +
                           [k for ci in range(NCH) for k in
                            b2h.k(GT * ci + 128 * jt, GT * ci + 128 * jt + 128) +
                            b2l.k(GT * ci + 128 * jt, GT * ci + 128 * jt + 128)],
                     writes=bank.k())
            s0 = 12 * jt
            p.op("vector", lambda e, s0=s0: e.bn_stats(st6[:, s0:s0 + 6], ps[0].ap),
                 reads=ps[0].k(), writes=st6.k(s0, s0 + 6))
            p.op("vector", lambda e, s0=s0: e.bn_stats(st6[:, s0 + 6:s0 + 12],
                                                       ps[1].ap),
                 reads=ps[1].k(), writes=st6.k(s0 + 6, s0 + 12))
            p.op("vector", lambda e, jt=jt, s0=s0: e.bn_aggr(
                mv[:, 2 * jt:2 * jt + 2], st6[:, s0:s0 + 12]),
                reads=st6.k(s0, s0 + 12), writes=mv.k(2 * jt, 2 * jt + 2))
            p.op("scalar", lambda e, jt=jt: e.activation(
                rr[:, jt:jt + 1], mv[:, 2 * jt + 1:2 * jt + 2], AF.Ln,
                bias=epsc.ap),
                reads=mv.k(2 * jt + 1, 2 * jt + 2) + epsc.k(),
                writes=rr.k(jt, jt + 1))
            p.op("scalar", lambda e, jt=jt: e.activation(
                rr[:, jt:jt + 1], rr[:, jt:jt + 1], AF.Exp, scale=nhc.ap),
                reads=rr.k(jt, jt + 1) + nhc.k(), writes=rr.k(jt, jt + 1))
            p.op("vector", lambda e, jt=jt: e.scalar_tensor_tensor(
                ss[:, jt:jt + 1], mv[:, 2 * jt:2 * jt + 1], -1.0, rr[:, jt:jt + 1],
                ALU.mult, ALU.mult),
                reads=mv.k(2 * jt, 2 * jt + 1) + rr.k(jt, jt + 1),
                writes=ss.k(jt, jt + 1))
            for half in range(2):
                bank = ps[half]
                a0 = D * jt + 512 * half
                g0 = 512 * half
                p.op("scalar", lambda e, bank=bank, a0=a0, jt=jt: e.activation(
                    xn[:, a0:a0 + 512], bank.ap, AF.Identity, bias=ss[:, jt:jt + 1],
                    scale=rr[:, jt:jt + 1]),
                    reads=bank.k() + rr.k(jt, jt + 1) + ss.k(jt, jt + 1),
                    writes=xn.k(a0, a0 + 512))
                p.op("vector", lambda e, a0=a0, g0=g0: e.tensor_mul(
                    xn[:, a0:a0 + 512], xn[:, a0:a0 + 512], Pb[:, g0:g0 + 512]),
                    reads=xn.k(a0, a0 + 512) + Pb.k(g0, g0 + 512),
                    writes=xn.k(a0, a0 + 512))
                p.op("vector", lambda e, a0=a0, g0=g0: e.tensor_add(
                    xn[:, a0:a0 + 512], xn[:, a0:a0 + 512],
                    Pb[:, 1024 + g0:1024 + g0 + 512]),
                    reads=xn.k(a0, a0 + 512) + Pb.k(1024 + g0, 1024 + g0 + 512),
                    writes=xn.k(a0, a0 + 512))
                p.op("vector", lambda e, a0=a0: e.tensor_add(
                    x[:, a0:a0 + 512], x[:, a0:a0 + 512], xn[:, a0:a0 + 512]),
                    reads=xn.k(a0, a0 + 512) + x.k(a0, a0 + 512),
                    writes=x.k(a0, a0 + 512))
            cb(jt)

    def ffn(g, cb):
        transpose_split_fm(xn)
        for jt in range(NT):
            def r_mm(e, jt=jt):
                for ci in range(NCH):
                    last = mm3(e, ps[2][:, 0:8],
                               fmh[:, GT * ci + 128 * jt:GT * ci + 128 * jt + 128],
                               fml[:, GT * ci + 128 * jt:GT * ci + 128 * jt + 128],
                               rwh[:, 8 * ci:8 * ci + 8],
                               rwl[:, 8 * ci:8 * ci + 8],
                               ci == 0, ci == 7)
                return last
            p.op("tensor", r_mm,
                 reads=[k for ci in range(NCH) for k in
                        fm_keys(GT * ci + 128 * jt, GT * ci + 128 * jt + 128)] +
                       rwh.k() + rwl.k(),
                 writes=ps[2].k())
            p.op("vector", lambda e: e.tensor_add(lgt.ap, ps[2][:, 0:8], rbr.ap),
                 reads=ps[2].k() + rbr.k(), writes=lgt.k())
            p.op("vector", lambda e: e.reduce_max(l1t.ap, lgt.ap, axis=AX.X),
                 reads=lgt.k(), writes=l1t.k())
            p.op("vector", lambda e: e.reduce_max(nl1t.ap, lgt.ap, axis=AX.X,
                                                  negate=True),
                 reads=lgt.k(), writes=nl1t.k())
            p.op("vector", lambda e: e.tensor_scalar(
                m1t.ap, lgt.ap, l1t.ap, None, ALU.is_ge),
                reads=lgt.k() + l1t.k(), writes=m1t.k())
            p.op("vector", lambda e: e.scalar_tensor_tensor(
                m1t.ap, m1t.ap, NEG, lgt.ap, ALU.mult, ALU.add),
                reads=m1t.k() + lgt.k(), writes=m1t.k())
            p.op("vector", lambda e: e.reduce_max(l2t.ap, m1t.ap, axis=AX.X),
                 reads=m1t.k(), writes=l2t.k())
            p.op("vector", lambda e: e.tensor_scalar(
                ge2t.ap, lgt.ap, l2t.ap, None, ALU.is_ge),
                reads=lgt.k() + l2t.k(), writes=ge2t.k())
            p.op("scalar", lambda e: e.activation(lgt.ap, lgt.ap, AF.Exp,
                                                  bias=nl1t.ap),
                 reads=lgt.k() + nl1t.k(), writes=lgt.k())
            p.op("vector", lambda e: e.tensor_mul(lgt.ap, lgt.ap, ge2t.ap),
                 reads=lgt.k() + ge2t.k(), writes=lgt.k())
            p.op("vector", lambda e: e.reduce_sum(l1t.ap, lgt.ap, axis=AX.X),
                 reads=lgt.k(), writes=l1t.k())
            p.op("vector", lambda e: e.reciprocal(l1t.ap, l1t.ap),
                 reads=l1t.k(), writes=l1t.k())
            p.op("vector", lambda e, jt=jt: e.tensor_scalar_mul(
                gates[:, 8 * jt:8 * jt + 8], lgt.ap, l1t.ap),
                reads=lgt.k() + l1t.k(), writes=gates.k(8 * jt, 8 * jt + 8))

        def split_h(d0):
            p.op("scalar", lambda e, d0=d0: e.copy(
                b2h[:, d0:d0 + 512], PTb.ap),
                reads=PTb.k(), writes=b2h.k(d0, d0 + 512))
            p.op("vector", lambda e, d0=d0: e.tensor_sub(
                b2l[:, d0:d0 + 512], PTb.ap, b2h[:, d0:d0 + 512]),
                reads=PTb.k() + b2h.k(d0, d0 + 512), writes=b2l.k(d0, d0 + 512))

        def expert_h(e_idx, hbase):
            loadw(W1h, W1l, "e1", 0, 4096, eslice=e_idx)
            loadw(W1h, W1l, "e3", 4096, 4096, eslice=e_idx)
            for hc in range(4):
                bank1 = ps[hc % 2]
                bank3 = ps[2 + hc % 2]
                def h1_mm(e, hc=hc, bank1=bank1):
                    for ci in range(NCH):
                        last = mm3(
                            e, bank1.ap,
                            W1h[:, 512 * ci + 128 * hc:512 * ci + 128 * hc + 128],
                            W1l[:, 512 * ci + 128 * hc:512 * ci + 128 * hc + 128],
                            fmh[:, GT * ci:GT * ci + GT],
                            fml[:, GT * ci:GT * ci + GT],
                            ci == 0, ci == 7)
                    return last
                def h3_mm(e, hc=hc, bank3=bank3):
                    for ci in range(NCH):
                        last = mm3(
                            e, bank3.ap,
                            W1h[:, 4096 + 512 * ci + 128 * hc:4096 + 512 * ci + 128 * hc + 128],
                            W1l[:, 4096 + 512 * ci + 128 * hc:4096 + 512 * ci + 128 * hc + 128],
                            fmh[:, GT * ci:GT * ci + GT],
                            fml[:, GT * ci:GT * ci + GT],
                            ci == 0, ci == 7)
                    return last
                p.op("tensor", h1_mm,
                     reads=W1h.k(0, 4096) + W1l.k(0, 4096) + fm_keys(0, NCH * GT),
                     writes=bank1.k())
                p.op("tensor", h3_mm,
                     reads=W1h.k(4096, 8192) + W1l.k(4096, 8192) +
                           fm_keys(0, NCH * GT),
                     writes=bank3.k())
                p.op("scalar", lambda e, bank1=bank1: e.activation(
                    tmp5.ap, bank1.ap, AF.Silu),
                    reads=bank1.k(), writes=tmp5.k())
                p.op("vector", lambda e, bank3=bank3: e.tensor_mul(
                    PTb.ap, tmp5.ap, bank3.ap),
                    reads=tmp5.k() + bank3.k(), writes=PTb.k())
                split_h(hbase + 512 * hc)

        def y_hc(e, hbase, wcol, jt, half, bank, hc, start, stop):
            a0 = hbase + 512 * hc + 128 * jt
            w0 = wcol + 1024 * hc + 512 * half
            return mm3(e, bank.ap, b2h[:, a0:a0 + 128], b2l[:, a0:a0 + 128],
                       W2h[:, w0:w0 + 512], W2l[:, w0:w0 + 512], start, stop)

        def y_ops(hbase, wcol, jt, half, bank, start, stop=True):
            wk0 = wcol
            for hc in range(4):
                a0 = hbase + 512 * hc + 128 * jt
                w0 = wcol + 1024 * hc + 512 * half
                p.op("tensor",
                     lambda e, hc=hc, jt=jt, half=half, bank=bank:
                     y_hc(e, hbase, wcol, jt, half, bank, hc,
                          start and hc == 0, stop and hc == 3),
                     reads=b2h.k(a0, a0 + 128) + b2l.k(a0, a0 + 128) +
                           W2h.k(w0, w0 + 512) + W2l.k(w0, w0 + 512) +
                           ([] if (start and hc == 0) else [("cont", 0)]),
                     writes=bank.k())

        expert_h(8, 0)
        expert_h(9, 2048)
        loadw(W2h, W2l, "e2", 0, 4096, eslice=8)
        loadw(W2h, W2l, "e2", 4096, 4096, eslice=9)
        for jt in range(NT):
            for half in range(2):
                bank = ps[4 + (2 * jt + half) % 4]
                y_ops(0, 0, jt, half, bank, True, stop=False)
                y_ops(2048, 4096, jt, half, bank, False)
                a0 = D * jt + 512 * half
                p.op("vector", lambda e, a0=a0, bank=bank: e.tensor_add(
                    x[:, a0:a0 + 512], x[:, a0:a0 + 512], bank.ap),
                    reads=bank.k() + x.k(a0, a0 + 512), writes=x.k(a0, a0 + 512))

        for e_idx in range(8):
            expert_h(e_idx, 0)
            loadw(W2h, W2l, "e2", 0, 4096, eslice=e_idx)
            for jt in range(NT):
                for half in range(2):
                    bank = ps[4 + (2 * jt + half) % 4]
                    y_ops(0, 0, jt, half, bank, True)
                    a0 = D * jt + 512 * half
                    p.op("vector",
                         lambda e, a0=a0, bank=bank, jt=jt, e_idx=e_idx:
                         e.scalar_tensor_tensor(
                             x[:, a0:a0 + 512], bank.ap,
                             gates[:, 8 * jt + e_idx:8 * jt + e_idx + 1],
                             x[:, a0:a0 + 512], ALU.mult, ALU.add),
                         reads=bank.k() +
                               gates.k(8 * jt + e_idx, 8 * jt + e_idx + 1) +
                               x.k(a0, a0 + 512),
                         writes=x.k(a0, a0 + 512))
                    if e_idx == 7 and half == 1:
                        cb(jt)

    def attn_b(g, cb):
        transpose_split_fm(xn)
        for wname, dsth, dstl in (("wbq", b1h, b1l), ("wbk", b2h, b2l)):
            loadw(W1h, W1l, wname)
            for eo in range(NCH):
                bank = ps[eo % 2]
                def p_mm(e, eo=eo, bank=bank):
                    for ci in range(NCH):
                        last = mm3(
                            e, bank.ap,
                            W1h[:, 1024 * ci + 128 * eo:1024 * ci + 128 * eo + 128],
                            W1l[:, 1024 * ci + 128 * eo:1024 * ci + 128 * eo + 128],
                            fmh[:, GT * ci:GT * ci + GT],
                            fml[:, GT * ci:GT * ci + GT],
                            ci == 0, ci == 7)
                    return last
                p.op("tensor", p_mm,
                     reads=W1h.k() + W1l.k() + fm_keys(0, NCH * GT),
                     writes=bank.k())
                split_psum(bank, GT, dsth, dstl, GT * eo)
        for jt in range(NT):
            def st_mm(e, jt=jt):
                for ci in range(NCH):
                    a0 = GT * ci + 128 * jt
                    last = mm3(e, ps[2][:, 128 * jt:128 * jt + 128],
                               b2h[:, a0:a0 + 128], b2l[:, a0:a0 + 128],
                               b1h[:, a0:a0 + 128], b1l[:, a0:a0 + 128],
                               ci == 0, ci == 7)
                return last
            p.op("tensor", st_mm,
                 reads=[k for ci in range(NCH) for k in
                        b2h.k(GT * ci + 128 * jt, GT * ci + 128 * jt + 128) +
                        b2l.k(GT * ci + 128 * jt, GT * ci + 128 * jt + 128) +
                        b1h.k(GT * ci + 128 * jt, GT * ci + 128 * jt + 128) +
                        b1l.k(GT * ci + 128 * jt, GT * ci + 128 * jt + 128)],
                 writes=ps[2].k())
            p.op("vector", lambda e, jt=jt: e.tensor_mul(
                PTb[:, 0:128], ps[2][:, 128 * jt:128 * jt + 128], maskT.ap),
                reads=ps[2].k() + maskT.k(), writes=PTb.k(0, 128))
            p.op("scalar", lambda e, jt=jt: e.copy(
                stmh[:, 128 * jt:128 * jt + 128], PTb[:, 0:128]),
                reads=PTb.k(0, 128), writes=stmh.k(128 * jt, 128 * jt + 128))
            p.op("vector", lambda e, jt=jt: e.tensor_sub(
                stml[:, 128 * jt:128 * jt + 128], PTb[:, 0:128],
                stmh[:, 128 * jt:128 * jt + 128]),
                reads=PTb.k(0, 128) + stmh.k(128 * jt, 128 * jt + 128),
                writes=stml.k(128 * jt, 128 * jt + 128))
        loadw(W1h, W1l, "wbv")
        loadw(W2h, W2l, "wbg")
        for jt in range(NT):
            for half in range(2):
                vbank = ps[3 + half]
                def v_mm(e, jt=jt, half=half, vbank=vbank):
                    for ci in range(NCH):
                        a0 = GT * ci + 128 * jt
                        w0 = 1024 * ci + 512 * half
                        last = mm3(e, vbank.ap,
                                   fmh[:, a0:a0 + 128], fml[:, a0:a0 + 128],
                                   W1h[:, w0:w0 + 512], W1l[:, w0:w0 + 512],
                                   ci == 0, ci == 7)
                    return last
                p.op("tensor", v_mm,
                     reads=W1h.k() + W1l.k() +
                           [k for ci in range(NCH) for k in
                            fm_keys(GT * ci + 128 * jt, GT * ci + 128 * jt + 128)],
                     writes=vbank.k())
                split_psum(vbank, 512, b1h, b1l, D * jt + 512 * half)
            for half in range(2):
                gbank = ps[5 + half]
                obank = ps[3 + half]
                def g_mm(e, jt=jt, half=half, gbank=gbank):
                    for ci in range(NCH):
                        a0 = GT * ci + 128 * jt
                        w0 = 1024 * ci + 512 * half
                        last = mm3(e, gbank.ap,
                                   fmh[:, a0:a0 + 128], fml[:, a0:a0 + 128],
                                   W2h[:, w0:w0 + 512], W2l[:, w0:w0 + 512],
                                   ci == 0, ci == 7)
                    return last
                p.op("tensor", g_mm,
                     reads=W2h.k() + W2l.k() +
                           [k for ci in range(NCH) for k in
                            fm_keys(GT * ci + 128 * jt, GT * ci + 128 * jt + 128)],
                     writes=gbank.k())
                p.op("scalar", lambda e, gbank=gbank: e.activation(
                    tmp5.ap, gbank.ap, AF.Sigmoid),
                    reads=gbank.k(), writes=tmp5.k())
                a0 = D * jt + 512 * half
                def o_mm(e, jt=jt, a0=a0, obank=obank):
                    return mm3(e, obank.ap,
                               stmh[:, 128 * jt:128 * jt + 128],
                               stml[:, 128 * jt:128 * jt + 128],
                               b1h[:, a0:a0 + 512], b1l[:, a0:a0 + 512],
                               True, True)
                p.op("tensor", o_mm,
                     reads=stmh.k(128 * jt, 128 * jt + 128) +
                           stml.k(128 * jt, 128 * jt + 128) +
                           b1h.k(a0, a0 + 512) + b1l.k(a0, a0 + 512),
                     writes=obank.k())
                p.op("vector", lambda e, a0=a0, obank=obank: e.tensor_mul(
                    buf1[:, a0:a0 + 512], obank.ap, tmp5.ap),
                    reads=obank.k() + tmp5.k(), writes=buf1.k(a0, a0 + 512))
        transpose_split_fm(buf1)
        loadw(W1h, W1l, "wbo")
        for jt in range(NT):
            for half in range(2):
                bank = ps[half]
                def ob_mm(e, jt=jt, half=half, bank=bank):
                    for ci in range(NCH):
                        a0 = GT * ci + 128 * jt
                        w0 = 1024 * ci + 512 * half
                        last = mm3(e, bank.ap,
                                   fmh[:, a0:a0 + 128], fml[:, a0:a0 + 128],
                                   W1h[:, w0:w0 + 512], W1l[:, w0:w0 + 512],
                                   ci == 0, ci == 7)
                    return last
                p.op("tensor", ob_mm,
                     reads=W1h.k() + W1l.k() +
                           [k for ci in range(NCH) for k in
                            fm_keys(GT * ci + 128 * jt, GT * ci + 128 * jt + 128)],
                     writes=bank.k())
                a0 = D * jt + 512 * half
                p.op("vector", lambda e, a0=a0, bank=bank: e.tensor_add(
                    xn[:, a0:a0 + 512], xn[:, a0:a0 + 512], bank.ap),
                    reads=xn.k(a0, a0 + 512) + bank.k(), writes=xn.k(a0, a0 + 512))
                p.op("vector", lambda e, a0=a0: e.tensor_add(
                    x[:, a0:a0 + 512], x[:, a0:a0 + 512], xn[:, a0:a0 + 512]),
                    reads=xn.k(a0, a0 + 512) + x.k(a0, a0 + 512),
                    writes=x.k(a0, a0 + 512))
                if half == 1:
                    cb(jt)

    def store_tile(g, j):
        row0 = GT * g
        p.dma("gpsimd", x_out[row0 + 128 * j:row0 + 128 * j + 128, :],
              x[:, D * j:D * j + D], reads=x.k(D * j, D * j + D))

    for g in range(n_groups):
        row0 = GT * g
        for j in range(NT):
            p.dma("sync", x[:, D * j:D * j + D],
                  x_in[row0 + 128 * j:row0 + 128 * j + 128, :],
                  writes=x.k(D * j, D * j + D))
            ln_tile(j)
        for si, s in enumerate(subs):
            last = si == len(subs) - 1
            cb = ((lambda j, g=g: store_tile(g, j)) if last
                  else (lambda j: ln_tile(j)))
            if s == "a":
                attn_a(g, cb)
            elif s == "b":
                attn_b(g, cb)
            else:
                ffn(g, cb)

    p.dma_barrier("gpsimd", "gpsimd")
    p.finalize()
    n_ins = sum(len(v) for v in p.thunks.values())
    return nc, n_ins



def _img(wT):
    k, m = wT.shape
    return np.ascontiguousarray(
        wT.reshape(k // 128, 128, m).transpose(1, 0, 2).reshape(128, (k // 128) * m))


def make_inputs(inputs, core):
    f = np.ascontiguousarray
    g = lambda n: np.asarray(inputs[n])
    x = g("L0_attn_a_x")
    d = {
        "x_in": f(x[2 * core:2 * core + 2].reshape(CORE_TOK, D)),
        "wqT": _img(g("L0_attn_a_w_q").T),
        "wkvT": _img(g("L0_attn_a_w_kv").T),
        "woutT": _img(g("L0_attn_a_w_out").T),
        "ag_col": f(g("L0_attn_a_ln_g").reshape(NCH, 128).T),
        "ab_col": f(g("L0_attn_a_ln_b").reshape(NCH, 128).T),
        "og_rep": f(np.broadcast_to(g("L0_attn_a_lno_g"), (128, D))),
        "ob_rep": f(np.broadcast_to(g("L0_attn_a_lno_b"), (128, D))),
        "ld0": f(g("L0_attn_b_log_decay")[0:1].reshape(1, 1)),
        "r_w": _img(g("L0_ffn_router_w")),
        "rb_rep": f(np.broadcast_to(g("L0_ffn_router_b"), (128, 8))),
        "ew1": np.stack([_img(w) for w in np.concatenate(
            [g("L0_ffn_routed_w1"), g("L0_ffn_shared_w1")], 0)]),
        "ew3": np.stack([_img(w) for w in np.concatenate(
            [g("L0_ffn_routed_w3"), g("L0_ffn_shared_w3")], 0)]),
        "ew2": np.stack([_img(w) for w in np.concatenate(
            [g("L0_ffn_routed_w2"), g("L0_ffn_shared_w2")], 0)]),
        "ident": np.eye(128, dtype=np.float32),
        "eyemod": (np.arange(128)[:, None] % 64 == np.arange(64)[None, :]
                   ).astype(np.float32),
        "trimask": np.triu(np.full((128, 128), NEG, np.float32), 1),
        "ones128": np.ones((1, 128), np.float32),
    }
    for w in ("q", "k", "v", "g", "o"):
        d["wb%sT" % w] = _img(g("L0_attn_b_W_" + w).T)
    u = np.arange(128)[:, None]
    q = np.arange(128)[None, :]
    low = ((u // 32) == (q // 32)) & (q >= u)
    d["diffT"] = ((q - u) * low).astype(np.float32)
    d["indT"] = low.astype(np.float32)
    return d


def kernel(**inputs):
    nc, _ = build_core()
    in_maps = [make_inputs(inputs, c) for c in range(8)]
    res = run_bass_kernel_spmd(nc, in_maps, core_ids=list(range(8)))
    out = np.concatenate([r["x_out"].reshape(2, SEQ, D) for r in res.results], 0)
    return out.astype(np.float32)


# version 10
# speedup vs baseline: 1.0135x; 1.0044x over previous
import math
import numpy as np
import concourse.bass as bass
import concourse.mybir as mybir
from concourse.bass_utils import run_bass_kernel_spmd

F32 = mybir.dt.float32
BF16 = mybir.dt.bfloat16
AF = mybir.ActivationFunctionType
ALU = mybir.AluOpType
AX = mybir.AxisListType

D = 1024
NH = 16
NCH = 8
GT = 512
NT = 4
CORE_TOK = 4096
SEQ = 2048
EPS = 1e-5
QSCALE = (64 ** -0.5) * math.sqrt(64 ** -0.5)
KSCALE = math.sqrt(64 ** -0.5)
NEG = -1.0e30

ENGS = ("sync", "gpsimd", "scalar", "vector", "tensor")


class Buf:

    def __init__(self, ap, name, gran):
        self.ap = ap
        self.name = name
        self.gran = gran

    def __getitem__(self, sl):
        return self.ap[sl]

    def k(self, c0=None, c1=None):
        if c0 is None:
            c0, c1 = 0, self.ap.shape[-1]
        lo = c0 // self.gran
        hi = (c1 + self.gran - 1) // self.gran
        return [(self.name, b) for b in range(lo, hi)]


class Prog:

    def __init__(self, nc):
        self.nc = nc
        self.thunks = {e: [] for e in ENGS}
        self.sems = {}
        self.cnt = {}
        self.stepv = {}
        self.waited = {}
        self.seq = {e: 0 for e in ENGS}
        self.lastdrain = {e: 0 for e in ENGS}
        self.state = {}
        self.engsem = {"tensor": "tsem", "vector": "vsem",
                       "scalar": "ssem", "gpsimd": "gsem"}
        for s in ("tsem", "vsem", "ssem", "gsem"):
            self._mksem(s, 1)
        self.pool = {"sync": ["q%d" % i for i in range(16)],
                     "gpsimd": ["gq%d" % i for i in range(8)]}
        for q in self.pool["sync"] + self.pool["gpsimd"]:
            self._mksem(q, 16)
        self.pool_i = {"sync": 0, "gpsimd": 0}
        self.ndrain = {e: 0 for e in ENGS}

    def _mksem(self, name, step):
        self.sems[name] = self.nc.alloc_semaphore(name=name)
        self.cnt[name] = 0
        self.stepv[name] = step

    def _wait(self, eng, sem, val):
        if val <= self.waited.get((eng, sem), 0):
            return
        h = self.sems[sem]
        self.thunks[eng].append(lambda e, s=h, v=val: e.wait_ge(s, v))
        self.waited[(eng, sem)] = val

    def _drain(self, eng):
        self.thunks[eng].append(lambda e: e.drain())
        self.seq[eng] += 1
        self.lastdrain[eng] = self.seq[eng]
        self.ndrain[eng] += 1

    def _deps(self, eng, reads, writes, waw_ok):
        need = []
        for k in reads:
            st = self.state.get(k)
            if st and st[0] is not None:
                need.append((st[0], False))
        for k in writes:
            st = self.state.get(k)
            if st:
                if st[0] is not None:
                    need.append((st[0], waw_ok))
                for r in st[1]:
                    need.append((r, False))
        drained = False
        for (deng, dsem, dval, dseq), skip_same in need:
            if deng == eng:
                if (eng != "tensor" and not skip_same
                        and dseq > self.lastdrain[eng] and not drained):
                    self._drain(eng)
                    drained = True
            else:
                self._wait(eng, dsem, dval)

    def _record(self, acc, reads, writes):
        for k in writes:
            self.state[k] = [acc, []]
        for k in reads:
            self.state.setdefault(k, [None, []])[1].append(acc)

    def op(self, eng, fn, reads=(), writes=(), waw_ok=False):
        self._deps(eng, reads, writes, waw_ok or eng == "tensor")
        sem = self.engsem[eng]
        h, step = self.sems[sem], self.stepv[sem]

        def thunk(e, fn=fn, h=h, step=step):
            fn(e).then_inc(h, step)
        self.thunks[eng].append(thunk)
        self.cnt[sem] += step
        self.seq[eng] += 1
        acc = (eng, sem, self.cnt[sem], self.seq[eng])
        self._record(acc, reads, writes)
        return acc

    def dma(self, eng, out, in_, reads=(), writes=()):
        self._deps(eng, reads, writes, False)
        sems = self.pool[eng]
        sem = sems[self.pool_i[eng] % len(sems)]
        self.pool_i[eng] += 1
        if self.cnt[sem]:
            self._wait(eng, sem, self.cnt[sem])
        h = self.sems[sem]

        def thunk(e, o=out, i=in_, h=h):
            e.dma_start(out=o, in_=i).then_inc(h, 16)
        self.thunks[eng].append(thunk)
        self.cnt[sem] += 16
        self.seq[eng] += 1
        acc = ("dma:" + sem, sem, self.cnt[sem], self.seq[eng])
        self._record(acc, reads, writes)
        return acc

    def dma_barrier(self, eng, queue):
        for sem in self.pool[queue]:
            if self.cnt[sem]:
                self._wait(eng, sem, self.cnt[sem])

    def finalize(self):
        with self.nc.Block() as block:
            block.sync(lambda e: [t(e) for t in self.thunks["sync"]])
            block.gpsimd(lambda e: [t(e) for t in self.thunks["gpsimd"]])
            block.scalar(lambda e: [t(e) for t in self.thunks["scalar"]])
            block.vector(lambda e: [t(e) for t in self.thunks["vector"]])
            block.tensor(lambda e: [t(e) for t in self.thunks["tensor"]])


def build_core(cfg=None):
    cfg = cfg or {}
    n_groups = cfg.get("n_groups", 8)
    subs = cfg.get("subs", ("a", "f", "b", "f"))

    nc = bass.Bass("TRN2", target_bir_lowering=False, debug=False)
    p = Prog(nc)

    hbm = {}

    def hin(name, shape):
        hbm[name] = nc.dram_tensor(name, list(shape), F32, kind="ExternalInput").ap()
        return hbm[name]

    x_in = hin("x_in", (CORE_TOK, D))
    x_out = nc.dram_tensor("x_out", [CORE_TOK, D], F32, kind="ExternalOutput").ap()
    hin("wqT", (128, 8192))
    hin("wkvT", (128, 1024))
    hin("woutT", (128, 8192))
    hin("ag_col", (128, NCH))
    hin("ab_col", (128, NCH))
    hin("og_rep", (128, D))
    hin("ob_rep", (128, D))
    for w in ("q", "k", "v", "g", "o"):
        hin("wb%sT" % w, (128, 8192))
    hin("ld0", (1, 1))
    hin("r_w", (128, 64))
    hin("rb_rep", (128, 8))
    ew1 = hin("ew1", (10, 128, 4096))
    ew3 = hin("ew3", (10, 128, 4096))
    ew2 = hin("ew2", (10, 128, 4096))
    hin("ident", (128, 128))
    hin("eyemod", (128, 64))
    hin("trimask", (128, 128))
    hin("diffT", (128, 128))
    hin("indT", (128, 128))
    hin("ones128", (1, 128))

    wsplit = {}
    for name, cols in (("wq", 8192), ("wkv", 1024), ("wout", 8192),
                       ("wbq", 8192), ("wbk", 8192), ("wbv", 8192),
                       ("wbg", 8192), ("wbo", 8192)):
        wsplit[name] = (
            nc.dram_tensor("s_%s_h" % name, [128, cols], BF16, kind="Internal").ap(),
            nc.dram_tensor("s_%s_l" % name, [128, cols], BF16, kind="Internal").ap())
    for name in ("e1", "e3", "e2"):
        wsplit[name] = (
            nc.dram_tensor("s_%s_h" % name, [10, 128, 4096], BF16,
                           kind="Internal").ap(),
            nc.dram_tensor("s_%s_l" % name, [10, 128, 4096], BF16,
                           kind="Internal").ap())

    def sb(n, cols, gran, rows=128, dt=F32):
        ap = nc.alloc_sbuf_tensor(n, [rows, cols], dt).ap()
        return Buf(ap, n, gran)

    x = sb("x", NT * D, 1024)
    xn = sb("xn", NT * D, 512)
    fmh = sb("fmh", NCH * GT, 128, dt=BF16)
    fml = sb("fml", NCH * GT, 128, dt=BF16)
    buf1 = sb("buf1", NCH * GT, 128)
    b1h = sb("b1h", NCH * GT, 128, dt=BF16)
    b1l = sb("b1l", NCH * GT, 128, dt=BF16)
    b2h = sb("b2h", NCH * GT, 128, dt=BF16)
    b2l = sb("b2l", NCH * GT, 128, dt=BF16)
    Pb = sb("P", SEQ, 512)
    PTb = sb("PT", GT, 128)
    stmh = sb("stmh", NT * 128, 128, dt=BF16)
    stml = sb("stml", NT * 128, 128, dt=BF16)
    kcache = sb("kcache", 2 * SEQ, 512)
    vcache = sb("vcache", SEQ, 64)
    W1h = sb("W1h", 8192, 512, dt=BF16)
    W1l = sb("W1l", 8192, 512, dt=BF16)
    W2h = sb("W2h", 8192, 512, dt=BF16)
    W2l = sb("W2l", 8192, 512, dt=BF16)
    tmp5 = sb("tmp5", GT, 512)
    ident = sb("identS", 128, 128)
    eyemod = sb("eyemodS", 64, 64)
    trimask = sb("trimaskS", 128, 128)
    maskT = sb("maskTS", 128, 128)
    ones128 = sb("ones128S", 128, 128, rows=1)
    agc = sb("agc", NCH, 1)
    abc = sb("abc", NCH, 1)
    rbr = sb("rbr", 8, 8)
    rwh = sb("rwh", NCH * 8, 8, dt=BF16)
    rwl = sb("rwl", NCH * 8, 8, dt=BF16)
    st6 = sb("st6", 48, 6)
    mv = sb("mv", 2 * NT, 2)
    rr = sb("rr", NT, 1)
    ss = sb("ss", NT, 1)
    mx4 = sb("mx4", 8, 1)
    sums4 = sb("sums4", 8, 1)
    rsum = sb("rsum", 2, 1)
    biasq = sb("biasq", NCH, 8)
    biaskv = sb("biaskv", 1, 1)
    gates = sb("gates", NT * 8, 8)
    lgt = sb("lgt", 8, 8)
    l1t = sb("l1t", 1, 1)
    nl1t = sb("nl1t", 1, 1)
    l2t = sb("l2t", 1, 1)
    m1t = sb("m1t", 8, 8)
    ge2t = sb("ge2t", 8, 8)
    epsc = sb("epsc", 1, 1)
    nhc = sb("nhc", 1, 1)
    dect = sb("dect", 1, 1, rows=1)
    lnd128 = sb("lnd128", 1, 1)

    ps = [Buf(nc.alloc_psum_tensor("ps%d" % i, [128, 512], F32).ap(),
              "ps%d" % i, 512) for i in range(8)]

    for name, dst in (("ident", ident), ("eyemod", eyemod),
                      ("trimask", trimask), ("ones128", ones128),
                      ("ag_col", agc), ("ab_col", abc), ("rb_rep", rbr),
                      ("ld0", dect),):
        p.dma("sync", dst.ap, hbm[name], writes=dst.k())
    p.op("vector", lambda e: e.memset(epsc.ap, float(EPS)), writes=epsc.k())
    p.op("vector", lambda e: e.memset(nhc.ap, -0.5), writes=nhc.k())

    p.dma("sync", PTb[:, 0:128], hbm["diffT"], writes=PTb.k(0, 128))
    p.dma("sync", tmp5[:, 0:128], hbm["indT"], writes=tmp5.k(0, 128))
    p.op("scalar", lambda e: e.activation(dect.ap, dect.ap, AF.Sigmoid),
         reads=dect.k(), writes=dect.k())
    p.op("scalar", lambda e: e.activation(dect.ap, dect.ap, AF.Ln),
         reads=dect.k(), writes=dect.k())
    p.op("tensor", lambda e: e.matmul(ps[0][:, 0:1], ones128.ap, dect.ap,
                                      start=True, stop=True),
         reads=dect.k() + ones128.k(), writes=ps[0].k())
    p.op("vector", lambda e: e.tensor_copy(lnd128.ap, ps[0][:, 0:1]),
         reads=ps[0].k(), writes=lnd128.k())
    p.op("scalar", lambda e: e.activation(maskT.ap, PTb[:, 0:128], AF.Exp,
                                          scale=lnd128.ap),
         reads=PTb.k(0, 128) + lnd128.k(), writes=maskT.k())
    p.op("vector", lambda e: e.tensor_mul(maskT.ap, maskT.ap, tmp5[:, 0:128]),
         reads=maskT.k() + tmp5.k(0, 128), writes=maskT.k())

    def wprep(src_h, cols, dst_pair, prep=None, eslice=None):
        npc = (cols + 2047) // 2048
        for pc in range(npc):
            w = min(2048, cols - 2048 * pc)
            src = src_h[eslice] if eslice is not None else src_h
            p.dma("sync", Pb[:, 0:w], src[:, 2048 * pc:2048 * pc + w],
                  writes=Pb.k(0, w))
            if prep is not None:
                prep(pc, w)
            p.op("scalar", lambda e, w=w: e.copy(b2h[:, 0:w], Pb[:, 0:w]),
                 reads=Pb.k(0, w), writes=b2h.k(0, w))
            p.op("vector", lambda e, w=w: e.tensor_sub(
                b2l[:, 0:w], Pb[:, 0:w], b2h[:, 0:w]),
                reads=Pb.k(0, w) + b2h.k(0, w), writes=b2l.k(0, w))
            dh, dl = dst_pair
            dsth = dh[eslice] if eslice is not None else dh
            dstl = dl[eslice] if eslice is not None else dl
            p.dma("gpsimd", dsth[:, 2048 * pc:2048 * pc + w], b2h[:, 0:w],
                  reads=b2h.k(0, w))
            p.dma("gpsimd", dstl[:, 2048 * pc:2048 * pc + w], b2l[:, 0:w],
                  reads=b2l.k(0, w))

    for pc in range(4):
        p.dma("sync", Pb[:, 0:2048], hbm["wqT"][:, 2048 * pc:2048 * pc + 2048],
              writes=Pb.k(0, 2048))
        def bq_mm(e, pc=pc):
            for eo in range(NCH):
                for i, ci in enumerate(range(2 * pc, 2 * pc + 2)):
                    lo = 1024 * (ci - 2 * pc)
                    last = e.matmul(
                        ps[4][:, eo:eo + 1],
                        Pb[:, lo + 128 * eo:lo + 128 * eo + 128],
                        abc[:, ci:ci + 1],
                        start=(i == 0), stop=(i == 1))
            return last
        p.op("tensor", bq_mm, reads=Pb.k(0, 2048) + abc.k(), writes=ps[4].k(),
             waw_ok=True)
        if pc == 0:
            p.op("vector", lambda e: e.tensor_copy(biasq.ap, ps[4][:, 0:NCH]),
                 reads=ps[4].k(), writes=biasq.k())
        else:
            p.op("vector", lambda e: e.tensor_add(
                biasq.ap, biasq.ap, ps[4][:, 0:NCH]),
                reads=ps[4].k() + biasq.k(), writes=biasq.k())
    p.op("vector", lambda e: e.tensor_scalar_mul(
        biasq.ap, biasq.ap, float(QSCALE)),
        reads=biasq.k(), writes=biasq.k())

    def prep_wq(pc, w):
        for ci in range(2 * pc, 2 * pc + w // 1024):
            lo = 1024 * (ci - 2 * pc)
            p.op("vector", lambda e, lo=lo, ci=ci: e.tensor_scalar(
                Pb[:, lo:lo + 1024], Pb[:, lo:lo + 1024],
                agc[:, ci:ci + 1], float(QSCALE), ALU.mult, ALU.mult),
                reads=Pb.k(lo, lo + 1024) + agc.k(),
                writes=Pb.k(lo, lo + 1024))
    wprep(hbm["wqT"], 8192, wsplit["wq"], prep=prep_wq)

    p.dma("sync", Pb[:, 0:1024], hbm["wkvT"], writes=Pb.k(0, 1024))
    def bkv_mm(e):
        for ci in range(NCH):
            last = e.matmul(ps[2][:, 0:1], Pb[:, 128 * ci:128 * ci + 128],
                            abc[:, ci:ci + 1], start=(ci == 0), stop=(ci == 7))
        return last
    p.op("tensor", bkv_mm, reads=Pb.k(0, 1024) + abc.k(), writes=ps[2].k())
    p.op("vector", lambda e: e.tensor_copy(biaskv.ap, ps[2][:, 0:1]),
         reads=ps[2].k(), writes=biaskv.k())
    p.op("vector", lambda e: e.tensor_scalar_mul(
        biaskv[0:64, :], biaskv[0:64, :], float(KSCALE)),
        reads=biaskv.k(), writes=biaskv.k())

    def prep_wkv(pc, w):
        for ci in range(NCH):
            p.op("vector", lambda e, ci=ci: e.tensor_scalar_mul(
                Pb[:, 128 * ci:128 * ci + 128], Pb[:, 128 * ci:128 * ci + 128],
                agc[:, ci:ci + 1]),
                reads=Pb.k(128 * ci, 128 * ci + 128) + agc.k(),
                writes=Pb.k(128 * ci, 128 * ci + 128))
            p.op("vector", lambda e, ci=ci: e.tensor_scalar_mul(
                Pb[:, 128 * ci:128 * ci + 64], Pb[:, 128 * ci:128 * ci + 64],
                float(KSCALE)),
                reads=Pb.k(128 * ci, 128 * ci + 64),
                writes=Pb.k(128 * ci, 128 * ci + 64))
    wprep(hbm["wkvT"], 1024, wsplit["wkv"], prep=prep_wkv)

    wprep(hbm["woutT"], 8192, wsplit["wout"])
    for ei in (8, 9, 0, 1, 2, 3, 4, 5, 6, 7):
        wprep(ew1, 4096, wsplit["e1"], eslice=ei)
        wprep(ew3, 4096, wsplit["e3"], eslice=ei)
        wprep(ew2, 4096, wsplit["e2"], eslice=ei)
    for w in ("q", "k", "v", "g", "o"):
        wprep(hbm["wb%sT" % w], 8192, wsplit["wb" + w])
    p.dma("sync", PTb[:, 0:64], hbm["r_w"], writes=PTb.k(0, 64))
    p.op("scalar", lambda e: e.copy(rwh.ap, PTb[:, 0:64]),
         reads=PTb.k(0, 64), writes=rwh.k())
    p.op("vector", lambda e: e.tensor_sub(rwl.ap, PTb[:, 0:64], rwh.ap),
         reads=PTb.k(0, 64) + rwh.k(), writes=rwl.k())

    evac_tgl = [0]

    def evac_eng():
        evac_tgl[0] ^= 1
        return "vector" if evac_tgl[0] else "scalar"

    def copy_op(eng, dst, src):
        if eng == "vector":
            return lambda e: e.tensor_copy(dst, src)
        return lambda e: e.copy(dst, src)

    def split_psum(bank, w, dsth, dstl, d0):
        p.op("scalar", lambda e: e.copy(dsth[:, d0:d0 + w], bank[:, 0:w]),
             reads=bank.k(), writes=dsth.k(d0, d0 + w))
        p.op("vector", lambda e: e.tensor_sub(
            dstl[:, d0:d0 + w], bank[:, 0:w], dsth[:, d0:d0 + w]),
            reads=bank.k() + dsth.k(d0, d0 + w), writes=dstl.k(d0, d0 + w))

    def ln_tile(j, src=None, dst=None):
        src = src if src is not None else x
        dst = dst if dst is not None else xn
        sl = src[:, D * j:D * j + D]
        s0 = 12 * j
        p.op("vector", lambda e, sl=sl, s0=s0: e.bn_stats(
            st6[:, s0:s0 + 6], sl[:, 0:512]),
            reads=src.k(D * j, D * j + 512), writes=st6.k(s0, s0 + 6))
        p.op("vector", lambda e, sl=sl, s0=s0: e.bn_stats(
            st6[:, s0 + 6:s0 + 12], sl[:, 512:1024]),
            reads=src.k(D * j + 512, D * j + D), writes=st6.k(s0 + 6, s0 + 12))
        p.op("vector", lambda e, j=j, s0=s0: e.bn_aggr(
            mv[:, 2 * j:2 * j + 2], st6[:, s0:s0 + 12]),
            reads=st6.k(s0, s0 + 12), writes=mv.k(2 * j, 2 * j + 2))
        p.op("scalar", lambda e, j=j: e.activation(
            rr[:, j:j + 1], mv[:, 2 * j + 1:2 * j + 2], AF.Ln, bias=epsc.ap),
            reads=mv.k(2 * j + 1, 2 * j + 2) + epsc.k(), writes=rr.k(j, j + 1))
        p.op("scalar", lambda e, j=j: e.activation(
            rr[:, j:j + 1], rr[:, j:j + 1], AF.Exp, scale=nhc.ap),
            reads=rr.k(j, j + 1) + nhc.k(), writes=rr.k(j, j + 1))
        p.op("vector", lambda e, j=j: e.scalar_tensor_tensor(
            ss[:, j:j + 1], mv[:, 2 * j:2 * j + 1], -1.0, rr[:, j:j + 1],
            ALU.mult, ALU.mult),
            reads=mv.k(2 * j, 2 * j + 1) + rr.k(j, j + 1), writes=ss.k(j, j + 1))
        p.op("scalar", lambda e, j=j: e.activation(
            dst[:, D * j:D * j + D], src[:, D * j:D * j + D], AF.Identity,
            bias=ss[:, j:j + 1], scale=rr[:, j:j + 1]),
            reads=src.k(D * j, D * j + D) + rr.k(j, j + 1) + ss.k(j, j + 1),
            writes=dst.k(D * j, D * j + D))

    def ln_pass(src, dst):
        for j in range(NT):
            ln_tile(j, src, dst)

    def transpose_split_fm(src):
        for ci in range(NCH):
            bank = ps[ci % 2]
            for j in range(NT):
                p.op("tensor", lambda e, ci=ci, j=j, bank=bank: e.transpose(
                    bank[:, 128 * j:128 * j + 128],
                    src[:, D * j + 128 * ci:D * j + 128 * ci + 128], ident.ap),
                    reads=src.k(D * j + 128 * ci, D * j + 128 * ci + 128) + ident.k(),
                    writes=bank.k(), waw_ok=True)
            split_psum(bank, GT, fmh, fml, GT * ci)

    def loadw(slot_h, slot_l, name, col0=0, cols=8192, eslice=None):
        sh, sl_ = wsplit[name]
        if eslice is not None:
            sh, sl_ = sh[eslice], sl_[eslice]
        p.dma("sync", slot_h[:, col0:col0 + cols], sh[:, 0:cols],
              writes=slot_h.k(col0, col0 + cols))
        p.dma("sync", slot_l[:, col0:col0 + cols], sl_[:, 0:cols],
              writes=slot_l.k(col0, col0 + cols))

    def mm3(e, bank_ap, wh, wl, xh, xl, start, stop):
        e.matmul(bank_ap, wh, xh, start=start, stop=False)
        e.matmul(bank_ap, wh, xl, start=False, stop=False)
        return e.matmul(bank_ap, wl, xh, start=False, stop=stop)

    def fm_keys(c0, c1):
        return fmh.k(c0, c1) + fml.k(c0, c1)

    def attn_a(g, cb):
        grel = g % 4
        seqh = g // 4
        kcb = SEQ * seqh
        vb = 1024 * seqh
        ln_pass(xn, xn)
        transpose_split_fm(xn)

        loadw(W2h, W2l, "wkv", 0, 1024)
        def kv_mm(e):
            for ci in range(NCH):
                last = mm3(e, ps[2].ap,
                           W2h[:, 128 * ci:128 * ci + 128],
                           W2l[:, 128 * ci:128 * ci + 128],
                           fmh[:, GT * ci:GT * ci + GT],
                           fml[:, GT * ci:GT * ci + GT],
                           ci == 0, ci == 7)
            return last
        p.op("tensor", kv_mm,
             reads=W2h.k(0, 1024) + W2l.k(0, 1024) + fm_keys(0, NCH * GT),
             writes=ps[2].k())
        p.op("vector", lambda e: e.tensor_scalar_add(PTb.ap, ps[2].ap, biaskv.ap),
             reads=ps[2].k() + biaskv.k(), writes=PTb.k())
        ktok = kcb + 512 * grel
        p.dma("gpsimd", kcache[64:128, ktok:ktok + 512], PTb[0:64, :],
              reads=PTb.k(), writes=kcache.k(ktok, ktok + 512))
        p.op("vector", lambda e, ktok=ktok: e.tensor_copy(
            kcache[0:64, ktok:ktok + 512], PTb[0:64, :]),
            reads=PTb.k(), writes=kcache.k(ktok, ktok + 512))
        for j in range(NT):
            p.op("tensor", lambda e, j=j: e.transpose(
                ps[3][:, 64 * j:64 * j + 64], PTb[64:128, 128 * j:128 * j + 128],
                eyemod.ap[64:128, :]),
                reads=PTb.k(128 * j, 128 * j + 128) + eyemod.k(),
                writes=ps[3].k(), waw_ok=True)
        vtok = vb + 256 * grel
        p.op("vector", lambda e, vtok=vtok: e.tensor_copy(
            vcache[:, vtok:vtok + 256], ps[3][:, 0:256]),
            reads=ps[3].k(), writes=vcache.k(vtok, vtok + 256))

        loadw(W1h, W1l, "wq")
        for eo in range(NCH):
            bank = ps[4 + (eo % 2)]
            def q_mm(e, eo=eo, bank=bank):
                for ci in range(NCH):
                    last = mm3(e, bank.ap,
                               W1h[:, 1024 * ci + 128 * eo:1024 * ci + 128 * eo + 128],
                               W1l[:, 1024 * ci + 128 * eo:1024 * ci + 128 * eo + 128],
                               fmh[:, GT * ci:GT * ci + GT],
                               fml[:, GT * ci:GT * ci + GT],
                               ci == 0, ci == 7)
                return last
            p.op("tensor", q_mm, reads=W1h.k() + W1l.k() + fm_keys(0, NCH * GT),
                 writes=bank.k())
            p.op("vector", lambda e, eo=eo, bank=bank: e.tensor_scalar_add(
                buf1[:, GT * eo:GT * eo + GT], bank.ap, biasq[:, eo:eo + 1]),
                reads=bank.k() + biasq.k(), writes=buf1.k(GT * eo, GT * eo + GT))

        pairs = [(h, jt) for jt in range(NT) for h in range(NH)]

        def sbase(n):
            h, jt = pairs[n]
            nchunk = ((4 * grel + jt + 1) * 128 + 511) // 512
            return 2 * (n % 2) if nchunk <= 2 else 0

        def att_S(n):
            h, jt = pairs[n]
            hrow = 64 * (h % 2)
            ig = 4 * grel + jt
            nk = (ig + 1) * 128
            nchunk = (nk + 511) // 512
            sb0 = sbase(n)
            lq0 = GT * (h // 2) + 128 * jt
            lhs_q = buf1[hrow:hrow + 64, lq0:lq0 + 128]
            for c in range(nchunk):
                w = min(512, nk - 512 * c)
                p.op("tensor", lambda e, c=c, w=w, lhs_q=lhs_q, hrow=hrow,
                     sb0=sb0:
                     e.matmul(
                         ps[sb0 + c][:, 0:w], lhs_q,
                         kcache[hrow:hrow + 64, kcb + 512 * c:kcb + 512 * c + w],
                         start=True, stop=True),
                     reads=buf1.k(lq0, lq0 + 128) +
                           kcache.k(kcb + 512 * c, kcb + 512 * c + w),
                     writes=ps[sb0 + c].k())
            dc, doff = (ig * 128) // 512, (ig * 128) % 512
            p.op("vector", lambda e, dc=dc, doff=doff, sb0=sb0: e.tensor_add(
                ps[sb0 + dc][:, doff:doff + 128],
                ps[sb0 + dc][:, doff:doff + 128], trimask.ap),
                reads=ps[sb0 + dc].k() + trimask.k(),
                writes=ps[sb0 + dc].k())

        def att_p1(n):
            h, jt = pairs[n]
            ig = 4 * grel + jt
            nk = (ig + 1) * 128
            nchunk = (nk + 511) // 512
            sb0 = sbase(n)
            s4 = 4 * (n % 2)
            for c in range(nchunk):
                w = min(512, nk - 512 * c)
                p.op("vector", lambda e, c=c, w=w, s4=s4, sb0=sb0: e.reduce_max(
                    mx4[:, s4 + c:s4 + c + 1], ps[sb0 + c][:, 0:w], axis=AX.X,
                    negate=True),
                    reads=ps[sb0 + c].k(), writes=mx4.k(s4 + c, s4 + c + 1))
            if nchunk > 1:
                p.op("vector", lambda e, n_=nchunk, s4=s4: e.tensor_reduce(
                    mx4[:, s4:s4 + 1], mx4[:, s4:s4 + n_], AX.X, ALU.min),
                    reads=mx4.k(s4, s4 + nchunk), writes=mx4.k(s4, s4 + 1))
            for c in range(nchunk):
                w = min(512, nk - 512 * c)
                p.op("scalar", lambda e, c=c, w=w, s4=s4, sb0=sb0: e.activation(
                    Pb[:, 512 * c:512 * c + w], ps[sb0 + c][:, 0:w], AF.Exp,
                    bias=mx4[:, s4:s4 + 1], accum_out=sums4[:, s4 + c:s4 + c + 1]),
                    reads=ps[sb0 + c].k() + mx4.k(s4, s4 + 1),
                    writes=Pb.k(512 * c, 512 * c + w) + sums4.k(s4 + c, s4 + c + 1))

        def att_p2v(n):
            h, jt = pairs[n]
            nk = (4 * grel + jt + 1) * 128
            nchunk = (nk + 511) // 512
            s4 = 4 * (n % 2)
            r0 = n % 2
            p.op("vector", lambda e, n_=nchunk, s4=s4, r0=r0: e.tensor_reduce(
                rsum[:, r0:r0 + 1], sums4[:, s4:s4 + n_], AX.X, ALU.add),
                reads=sums4.k(s4, s4 + nchunk), writes=rsum.k(r0, r0 + 1))
            p.op("vector", lambda e, r0=r0: e.reciprocal(
                rsum[:, r0:r0 + 1], rsum[:, r0:r0 + 1]),
                reads=rsum.k(r0, r0 + 1), writes=rsum.k(r0, r0 + 1))

        def att_p2s(n):
            h, jt = pairs[n]
            nk = (4 * grel + jt + 1) * 128
            nchunk = (nk + 511) // 512
            r0 = n % 2
            for c in range(nchunk):
                w = min(512, nk - 512 * c)
                p.op("scalar", lambda e, c=c, w=w, r0=r0: e.activation(
                    Pb[:, 512 * c:512 * c + w], Pb[:, 512 * c:512 * c + w],
                    AF.Identity, scale=rsum[:, r0:r0 + 1]),
                    reads=Pb.k(512 * c, 512 * c + w) + rsum.k(r0, r0 + 1),
                    writes=Pb.k(512 * c, 512 * c + w))

        def att_ptpv(n):
            h, jt = pairs[n]
            hrow = 64 * (h % 2)
            obank = ps[7] if h % 2 == 0 else ps[5]
            ig = 4 * grel + jt
            lq0 = GT * (h // 2) + 128 * jt
            nkb = ig + 1
            for b0 in range(0, nkb, 4):
                bcnt = min(4, nkb - b0)
                ptbank = ps[6] if (b0 // 4) % 2 == 0 else ps[4]
                for bi in range(bcnt):
                    kb = b0 + bi
                    p.op("tensor", lambda e, kb=kb, bi=bi, ptbank=ptbank:
                         e.transpose(
                             ptbank[:, 128 * bi:128 * bi + 128],
                             Pb[:, 128 * kb:128 * kb + 128], ident.ap),
                         reads=Pb.k(128 * kb, 128 * kb + 128) + ident.k(),
                         writes=ptbank.k(), waw_ok=True)
                eng = evac_eng()
                p.op(eng, copy_op(eng, PTb[:, 0:128 * bcnt],
                                  ptbank[:, 0:128 * bcnt]),
                     reads=ptbank.k(), writes=PTb.k(0, 128 * bcnt))
                def pv(e, b0=b0, bcnt=bcnt, jt=jt, nkb=nkb, vb=vb,
                       obank=obank, hrow=hrow):
                    tp = None if hrow == 0 else (0, 64)
                    for bi in range(bcnt):
                        kb = b0 + bi
                        last = e.matmul(
                            obank[hrow:hrow + 64, 128 * jt:128 * jt + 128],
                            vcache[:, vb + 64 * kb:vb + 64 * kb + 64],
                            PTb[:, 128 * bi:128 * bi + 128],
                            start=(kb == 0), stop=(kb == nkb - 1),
                            tile_position=tp)
                    return last
                p.op("tensor", pv,
                     reads=PTb.k(0, 128 * bcnt) + vcache.k(vb, vb + 64 * nkb),
                     writes=obank.k(), waw_ok=True)
            p.op("scalar", lambda e, hrow=hrow, lq0=lq0, jt=jt, obank=obank:
                 e.copy(b2h[hrow:hrow + 64, lq0:lq0 + 128],
                        obank[hrow:hrow + 64, 128 * jt:128 * jt + 128]),
                 reads=obank.k(), writes=b2h.k(lq0, lq0 + 128))
            p.op("vector", lambda e, hrow=hrow, lq0=lq0, jt=jt, obank=obank:
                 e.tensor_sub(b2l[hrow:hrow + 64, lq0:lq0 + 128],
                              obank[hrow:hrow + 64, 128 * jt:128 * jt + 128],
                              b2h[hrow:hrow + 64, lq0:lq0 + 128]),
                 reads=obank.k() + b2h.k(lq0, lq0 + 128),
                 writes=b2l.k(lq0, lq0 + 128))

        att_S(0)
        for n in range(len(pairs)):
            att_p1(n)
            att_p2v(n)
            if n + 1 < len(pairs):
                att_S(n + 1)
            att_p2s(n)
            att_ptpv(n)

        p.dma("sync", Pb[:, 0:1024], hbm["og_rep"], writes=Pb.k(0, 1024))
        p.dma("sync", Pb[:, 1024:2048], hbm["ob_rep"], writes=Pb.k(1024, 2048))
        loadw(W2h, W2l, "wout")
        for jt in range(NT):
            for half in range(2):
                bank = ps[half]
                def o_mm(e, jt=jt, half=half, bank=bank):
                    for ci in range(NCH):
                        last = mm3(
                            e, bank.ap,
                            b2h[:, GT * ci + 128 * jt:GT * ci + 128 * jt + 128],
                            b2l[:, GT * ci + 128 * jt:GT * ci + 128 * jt + 128],
                            W2h[:, 1024 * ci + 512 * half:1024 * ci + 512 * half + 512],
                            W2l[:, 1024 * ci + 512 * half:1024 * ci + 512 * half + 512],
                            ci == 0, ci == 7)
                    return last
                p.op("tensor", o_mm,
                     reads=W2h.k() + W2l.k() +
                           [k for ci in range(NCH) for k in
                            b2h.k(GT * ci + 128 * jt, GT * ci + 128 * jt + 128) +
                            b2l.k(GT * ci + 128 * jt, GT * ci + 128 * jt + 128)],
                     writes=bank.k())
            s0 = 12 * jt
            p.op("vector", lambda e, s0=s0: e.bn_stats(st6[:, s0:s0 + 6], ps[0].ap),
                 reads=ps[0].k(), writes=st6.k(s0, s0 + 6))
            p.op("vector", lambda e, s0=s0: e.bn_stats(st6[:, s0 + 6:s0 + 12],
                                                       ps[1].ap),
                 reads=ps[1].k(), writes=st6.k(s0 + 6, s0 + 12))
            p.op("vector", lambda e, jt=jt, s0=s0: e.bn_aggr(
                mv[:, 2 * jt:2 * jt + 2], st6[:, s0:s0 + 12]),
                reads=st6.k(s0, s0 + 12), writes=mv.k(2 * jt, 2 * jt + 2))
            p.op("scalar", lambda e, jt=jt: e.activation(
                rr[:, jt:jt + 1], mv[:, 2 * jt + 1:2 * jt + 2], AF.Ln,
                bias=epsc.ap),
                reads=mv.k(2 * jt + 1, 2 * jt + 2) + epsc.k(),
                writes=rr.k(jt, jt + 1))
            p.op("scalar", lambda e, jt=jt: e.activation(
                rr[:, jt:jt + 1], rr[:, jt:jt + 1], AF.Exp, scale=nhc.ap),
                reads=rr.k(jt, jt + 1) + nhc.k(), writes=rr.k(jt, jt + 1))
            p.op("vector", lambda e, jt=jt: e.scalar_tensor_tensor(
                ss[:, jt:jt + 1], mv[:, 2 * jt:2 * jt + 1], -1.0, rr[:, jt:jt + 1],
                ALU.mult, ALU.mult),
                reads=mv.k(2 * jt, 2 * jt + 1) + rr.k(jt, jt + 1),
                writes=ss.k(jt, jt + 1))
            for half in range(2):
                bank = ps[half]
                a0 = D * jt + 512 * half
                g0 = 512 * half
                p.op("scalar", lambda e, bank=bank, a0=a0, jt=jt: e.activation(
                    xn[:, a0:a0 + 512], bank.ap, AF.Identity, bias=ss[:, jt:jt + 1],
                    scale=rr[:, jt:jt + 1]),
                    reads=bank.k() + rr.k(jt, jt + 1) + ss.k(jt, jt + 1),
                    writes=xn.k(a0, a0 + 512))
                p.op("vector", lambda e, a0=a0, g0=g0: e.tensor_mul(
                    xn[:, a0:a0 + 512], xn[:, a0:a0 + 512], Pb[:, g0:g0 + 512]),
                    reads=xn.k(a0, a0 + 512) + Pb.k(g0, g0 + 512),
                    writes=xn.k(a0, a0 + 512))
                p.op("vector", lambda e, a0=a0, g0=g0: e.tensor_add(
                    xn[:, a0:a0 + 512], xn[:, a0:a0 + 512],
                    Pb[:, 1024 + g0:1024 + g0 + 512]),
                    reads=xn.k(a0, a0 + 512) + Pb.k(1024 + g0, 1024 + g0 + 512),
                    writes=xn.k(a0, a0 + 512))
                p.op("vector", lambda e, a0=a0: e.tensor_add(
                    x[:, a0:a0 + 512], x[:, a0:a0 + 512], xn[:, a0:a0 + 512]),
                    reads=xn.k(a0, a0 + 512) + x.k(a0, a0 + 512),
                    writes=x.k(a0, a0 + 512))
            cb(jt)

    def ffn(g, cb):
        transpose_split_fm(xn)
        for jt in range(NT):
            def r_mm(e, jt=jt):
                for ci in range(NCH):
                    last = mm3(e, ps[2][:, 0:8],
                               fmh[:, GT * ci + 128 * jt:GT * ci + 128 * jt + 128],
                               fml[:, GT * ci + 128 * jt:GT * ci + 128 * jt + 128],
                               rwh[:, 8 * ci:8 * ci + 8],
                               rwl[:, 8 * ci:8 * ci + 8],
                               ci == 0, ci == 7)
                return last
            p.op("tensor", r_mm,
                 reads=[k for ci in range(NCH) for k in
                        fm_keys(GT * ci + 128 * jt, GT * ci + 128 * jt + 128)] +
                       rwh.k() + rwl.k(),
                 writes=ps[2].k())
            p.op("vector", lambda e: e.tensor_add(lgt.ap, ps[2][:, 0:8], rbr.ap),
                 reads=ps[2].k() + rbr.k(), writes=lgt.k())
            p.op("vector", lambda e: e.reduce_max(l1t.ap, lgt.ap, axis=AX.X),
                 reads=lgt.k(), writes=l1t.k())
            p.op("vector", lambda e: e.reduce_max(nl1t.ap, lgt.ap, axis=AX.X,
                                                  negate=True),
                 reads=lgt.k(), writes=nl1t.k())
            p.op("vector", lambda e: e.tensor_scalar(
                m1t.ap, lgt.ap, l1t.ap, None, ALU.is_ge),
                reads=lgt.k() + l1t.k(), writes=m1t.k())
            p.op("vector", lambda e: e.scalar_tensor_tensor(
                m1t.ap, m1t.ap, NEG, lgt.ap, ALU.mult, ALU.add),
                reads=m1t.k() + lgt.k(), writes=m1t.k())
            p.op("vector", lambda e: e.reduce_max(l2t.ap, m1t.ap, axis=AX.X),
                 reads=m1t.k(), writes=l2t.k())
            p.op("vector", lambda e: e.tensor_scalar(
                ge2t.ap, lgt.ap, l2t.ap, None, ALU.is_ge),
                reads=lgt.k() + l2t.k(), writes=ge2t.k())
            p.op("scalar", lambda e: e.activation(lgt.ap, lgt.ap, AF.Exp,
                                                  bias=nl1t.ap),
                 reads=lgt.k() + nl1t.k(), writes=lgt.k())
            p.op("vector", lambda e: e.tensor_mul(lgt.ap, lgt.ap, ge2t.ap),
                 reads=lgt.k() + ge2t.k(), writes=lgt.k())
            p.op("vector", lambda e: e.reduce_sum(l1t.ap, lgt.ap, axis=AX.X),
                 reads=lgt.k(), writes=l1t.k())
            p.op("vector", lambda e: e.reciprocal(l1t.ap, l1t.ap),
                 reads=l1t.k(), writes=l1t.k())
            p.op("vector", lambda e, jt=jt: e.tensor_scalar_mul(
                gates[:, 8 * jt:8 * jt + 8], lgt.ap, l1t.ap),
                reads=lgt.k() + l1t.k(), writes=gates.k(8 * jt, 8 * jt + 8))

        def split_h(d0):
            p.op("scalar", lambda e, d0=d0: e.copy(
                b2h[:, d0:d0 + 512], PTb.ap),
                reads=PTb.k(), writes=b2h.k(d0, d0 + 512))
            p.op("vector", lambda e, d0=d0: e.tensor_sub(
                b2l[:, d0:d0 + 512], PTb.ap, b2h[:, d0:d0 + 512]),
                reads=PTb.k() + b2h.k(d0, d0 + 512), writes=b2l.k(d0, d0 + 512))

        def expert_h(e_idx, hbase):
            loadw(W1h, W1l, "e1", 0, 4096, eslice=e_idx)
            loadw(W1h, W1l, "e3", 4096, 4096, eslice=e_idx)
            for hc in range(4):
                bank1 = ps[hc % 2]
                bank3 = ps[2 + hc % 2]
                def h1_mm(e, hc=hc, bank1=bank1):
                    for ci in range(NCH):
                        last = mm3(
                            e, bank1.ap,
                            W1h[:, 512 * ci + 128 * hc:512 * ci + 128 * hc + 128],
                            W1l[:, 512 * ci + 128 * hc:512 * ci + 128 * hc + 128],
                            fmh[:, GT * ci:GT * ci + GT],
                            fml[:, GT * ci:GT * ci + GT],
                            ci == 0, ci == 7)
                    return last
                def h3_mm(e, hc=hc, bank3=bank3):
                    for ci in range(NCH):
                        last = mm3(
                            e, bank3.ap,
                            W1h[:, 4096 + 512 * ci + 128 * hc:4096 + 512 * ci + 128 * hc + 128],
                            W1l[:, 4096 + 512 * ci + 128 * hc:4096 + 512 * ci + 128 * hc + 128],
                            fmh[:, GT * ci:GT * ci + GT],
                            fml[:, GT * ci:GT * ci + GT],
                            ci == 0, ci == 7)
                    return last
                p.op("tensor", h1_mm,
                     reads=W1h.k(0, 4096) + W1l.k(0, 4096) + fm_keys(0, NCH * GT),
                     writes=bank1.k())
                p.op("tensor", h3_mm,
                     reads=W1h.k(4096, 8192) + W1l.k(4096, 8192) +
                           fm_keys(0, NCH * GT),
                     writes=bank3.k())
                p.op("scalar", lambda e, bank1=bank1: e.activation(
                    tmp5.ap, bank1.ap, AF.Silu),
                    reads=bank1.k(), writes=tmp5.k())
                p.op("vector", lambda e, bank3=bank3: e.tensor_mul(
                    PTb.ap, tmp5.ap, bank3.ap),
                    reads=tmp5.k() + bank3.k(), writes=PTb.k())
                split_h(hbase + 512 * hc)

        def y_hc(e, hbase, wcol, jt, half, bank, hc, start, stop):
            a0 = hbase + 512 * hc + 128 * jt
            w0 = wcol + 1024 * hc + 512 * half
            return mm3(e, bank.ap, b2h[:, a0:a0 + 128], b2l[:, a0:a0 + 128],
                       W2h[:, w0:w0 + 512], W2l[:, w0:w0 + 512], start, stop)

        def y_ops(hbase, wcol, jt, half, bank, start, stop=True):
            wk0 = wcol
            for hc in range(4):
                a0 = hbase + 512 * hc + 128 * jt
                w0 = wcol + 1024 * hc + 512 * half
                p.op("tensor",
                     lambda e, hc=hc, jt=jt, half=half, bank=bank:
                     y_hc(e, hbase, wcol, jt, half, bank, hc,
                          start and hc == 0, stop and hc == 3),
                     reads=b2h.k(a0, a0 + 128) + b2l.k(a0, a0 + 128) +
                           W2h.k(w0, w0 + 512) + W2l.k(w0, w0 + 512) +
                           ([] if (start and hc == 0) else [("cont", 0)]),
                     writes=bank.k())

        expert_h(8, 0)
        expert_h(9, 2048)
        loadw(W2h, W2l, "e2", 0, 4096, eslice=8)
        loadw(W2h, W2l, "e2", 4096, 4096, eslice=9)
        for jt in range(NT):
            for half in range(2):
                bank = ps[4 + (2 * jt + half) % 4]
                y_ops(0, 0, jt, half, bank, True, stop=False)
                y_ops(2048, 4096, jt, half, bank, False)
                a0 = D * jt + 512 * half
                p.op("vector", lambda e, a0=a0, bank=bank: e.tensor_add(
                    x[:, a0:a0 + 512], x[:, a0:a0 + 512], bank.ap),
                    reads=bank.k() + x.k(a0, a0 + 512), writes=x.k(a0, a0 + 512))

        for e_idx in range(8):
            expert_h(e_idx, 0)
            loadw(W2h, W2l, "e2", 0, 4096, eslice=e_idx)
            for jt in range(NT):
                for half in range(2):
                    bank = ps[4 + (2 * jt + half) % 4]
                    y_ops(0, 0, jt, half, bank, True)
                    a0 = D * jt + 512 * half
                    p.op("vector",
                         lambda e, a0=a0, bank=bank, jt=jt, e_idx=e_idx:
                         e.scalar_tensor_tensor(
                             x[:, a0:a0 + 512], bank.ap,
                             gates[:, 8 * jt + e_idx:8 * jt + e_idx + 1],
                             x[:, a0:a0 + 512], ALU.mult, ALU.add),
                         reads=bank.k() +
                               gates.k(8 * jt + e_idx, 8 * jt + e_idx + 1) +
                               x.k(a0, a0 + 512),
                         writes=x.k(a0, a0 + 512))
                    if e_idx == 7 and half == 1:
                        cb(jt)

    def attn_b(g, cb):
        transpose_split_fm(xn)
        for wname, dsth, dstl in (("wbq", b1h, b1l), ("wbk", b2h, b2l)):
            loadw(W1h, W1l, wname)
            for eo in range(NCH):
                bank = ps[eo % 2]
                def p_mm(e, eo=eo, bank=bank):
                    for ci in range(NCH):
                        last = mm3(
                            e, bank.ap,
                            W1h[:, 1024 * ci + 128 * eo:1024 * ci + 128 * eo + 128],
                            W1l[:, 1024 * ci + 128 * eo:1024 * ci + 128 * eo + 128],
                            fmh[:, GT * ci:GT * ci + GT],
                            fml[:, GT * ci:GT * ci + GT],
                            ci == 0, ci == 7)
                    return last
                p.op("tensor", p_mm,
                     reads=W1h.k() + W1l.k() + fm_keys(0, NCH * GT),
                     writes=bank.k())
                split_psum(bank, GT, dsth, dstl, GT * eo)
        for jt in range(NT):
            def st_mm(e, jt=jt):
                for ci in range(NCH):
                    a0 = GT * ci + 128 * jt
                    last = mm3(e, ps[2][:, 128 * jt:128 * jt + 128],
                               b2h[:, a0:a0 + 128], b2l[:, a0:a0 + 128],
                               b1h[:, a0:a0 + 128], b1l[:, a0:a0 + 128],
                               ci == 0, ci == 7)
                return last
            p.op("tensor", st_mm,
                 reads=[k for ci in range(NCH) for k in
                        b2h.k(GT * ci + 128 * jt, GT * ci + 128 * jt + 128) +
                        b2l.k(GT * ci + 128 * jt, GT * ci + 128 * jt + 128) +
                        b1h.k(GT * ci + 128 * jt, GT * ci + 128 * jt + 128) +
                        b1l.k(GT * ci + 128 * jt, GT * ci + 128 * jt + 128)],
                 writes=ps[2].k())
            p.op("vector", lambda e, jt=jt: e.tensor_mul(
                PTb[:, 0:128], ps[2][:, 128 * jt:128 * jt + 128], maskT.ap),
                reads=ps[2].k() + maskT.k(), writes=PTb.k(0, 128))
            p.op("scalar", lambda e, jt=jt: e.copy(
                stmh[:, 128 * jt:128 * jt + 128], PTb[:, 0:128]),
                reads=PTb.k(0, 128), writes=stmh.k(128 * jt, 128 * jt + 128))
            p.op("vector", lambda e, jt=jt: e.tensor_sub(
                stml[:, 128 * jt:128 * jt + 128], PTb[:, 0:128],
                stmh[:, 128 * jt:128 * jt + 128]),
                reads=PTb.k(0, 128) + stmh.k(128 * jt, 128 * jt + 128),
                writes=stml.k(128 * jt, 128 * jt + 128))
        loadw(W1h, W1l, "wbv")
        loadw(W2h, W2l, "wbg")
        for jt in range(NT):
            for half in range(2):
                vbank = ps[3 + half]
                def v_mm(e, jt=jt, half=half, vbank=vbank):
                    for ci in range(NCH):
                        a0 = GT * ci + 128 * jt
                        w0 = 1024 * ci + 512 * half
                        last = mm3(e, vbank.ap,
                                   fmh[:, a0:a0 + 128], fml[:, a0:a0 + 128],
                                   W1h[:, w0:w0 + 512], W1l[:, w0:w0 + 512],
                                   ci == 0, ci == 7)
                    return last
                p.op("tensor", v_mm,
                     reads=W1h.k() + W1l.k() +
                           [k for ci in range(NCH) for k in
                            fm_keys(GT * ci + 128 * jt, GT * ci + 128 * jt + 128)],
                     writes=vbank.k())
                split_psum(vbank, 512, b1h, b1l, D * jt + 512 * half)
            for half in range(2):
                gbank = ps[5 + half]
                obank = ps[3 + half]
                def g_mm(e, jt=jt, half=half, gbank=gbank):
                    for ci in range(NCH):
                        a0 = GT * ci + 128 * jt
                        w0 = 1024 * ci + 512 * half
                        last = mm3(e, gbank.ap,
                                   fmh[:, a0:a0 + 128], fml[:, a0:a0 + 128],
                                   W2h[:, w0:w0 + 512], W2l[:, w0:w0 + 512],
                                   ci == 0, ci == 7)
                    return last
                p.op("tensor", g_mm,
                     reads=W2h.k() + W2l.k() +
                           [k for ci in range(NCH) for k in
                            fm_keys(GT * ci + 128 * jt, GT * ci + 128 * jt + 128)],
                     writes=gbank.k())
                p.op("scalar", lambda e, gbank=gbank: e.activation(
                    tmp5.ap, gbank.ap, AF.Sigmoid),
                    reads=gbank.k(), writes=tmp5.k())
                a0 = D * jt + 512 * half
                def o_mm(e, jt=jt, a0=a0, obank=obank):
                    return mm3(e, obank.ap,
                               stmh[:, 128 * jt:128 * jt + 128],
                               stml[:, 128 * jt:128 * jt + 128],
                               b1h[:, a0:a0 + 512], b1l[:, a0:a0 + 512],
                               True, True)
                p.op("tensor", o_mm,
                     reads=stmh.k(128 * jt, 128 * jt + 128) +
                           stml.k(128 * jt, 128 * jt + 128) +
                           b1h.k(a0, a0 + 512) + b1l.k(a0, a0 + 512),
                     writes=obank.k())
                p.op("vector", lambda e, a0=a0, obank=obank: e.tensor_mul(
                    buf1[:, a0:a0 + 512], obank.ap, tmp5.ap),
                    reads=obank.k() + tmp5.k(), writes=buf1.k(a0, a0 + 512))
        transpose_split_fm(buf1)
        loadw(W1h, W1l, "wbo")
        for jt in range(NT):
            for half in range(2):
                bank = ps[half]
                def ob_mm(e, jt=jt, half=half, bank=bank):
                    for ci in range(NCH):
                        a0 = GT * ci + 128 * jt
                        w0 = 1024 * ci + 512 * half
                        last = mm3(e, bank.ap,
                                   fmh[:, a0:a0 + 128], fml[:, a0:a0 + 128],
                                   W1h[:, w0:w0 + 512], W1l[:, w0:w0 + 512],
                                   ci == 0, ci == 7)
                    return last
                p.op("tensor", ob_mm,
                     reads=W1h.k() + W1l.k() +
                           [k for ci in range(NCH) for k in
                            fm_keys(GT * ci + 128 * jt, GT * ci + 128 * jt + 128)],
                     writes=bank.k())
                a0 = D * jt + 512 * half
                p.op("vector", lambda e, a0=a0, bank=bank: e.tensor_add(
                    xn[:, a0:a0 + 512], xn[:, a0:a0 + 512], bank.ap),
                    reads=xn.k(a0, a0 + 512) + bank.k(), writes=xn.k(a0, a0 + 512))
                p.op("vector", lambda e, a0=a0: e.tensor_add(
                    x[:, a0:a0 + 512], x[:, a0:a0 + 512], xn[:, a0:a0 + 512]),
                    reads=xn.k(a0, a0 + 512) + x.k(a0, a0 + 512),
                    writes=x.k(a0, a0 + 512))
                if half == 1:
                    cb(jt)

    def store_tile(g, j):
        row0 = GT * g
        p.dma("gpsimd", x_out[row0 + 128 * j:row0 + 128 * j + 128, :],
              x[:, D * j:D * j + D], reads=x.k(D * j, D * j + D))

    for g in range(n_groups):
        row0 = GT * g
        for j in range(NT):
            p.dma("sync", x[:, D * j:D * j + D],
                  x_in[row0 + 128 * j:row0 + 128 * j + 128, :],
                  writes=x.k(D * j, D * j + D))
            ln_tile(j)
        for si, s in enumerate(subs):
            last = si == len(subs) - 1
            cb = ((lambda j, g=g: store_tile(g, j)) if last
                  else (lambda j: ln_tile(j)))
            if s == "a":
                attn_a(g, cb)
            elif s == "b":
                attn_b(g, cb)
            else:
                ffn(g, cb)

    p.dma_barrier("gpsimd", "gpsimd")
    p.finalize()
    n_ins = sum(len(v) for v in p.thunks.values())
    return nc, n_ins



def _img(wT):
    k, m = wT.shape
    return np.ascontiguousarray(
        wT.reshape(k // 128, 128, m).transpose(1, 0, 2).reshape(128, (k // 128) * m))


def make_inputs(inputs, core):
    f = np.ascontiguousarray
    g = lambda n: np.asarray(inputs[n])
    x = g("L0_attn_a_x")
    d = {
        "x_in": f(x[2 * core:2 * core + 2].reshape(CORE_TOK, D)),
        "wqT": _img(g("L0_attn_a_w_q").T),
        "wkvT": _img(g("L0_attn_a_w_kv").T),
        "woutT": _img(g("L0_attn_a_w_out").T),
        "ag_col": f(g("L0_attn_a_ln_g").reshape(NCH, 128).T),
        "ab_col": f(g("L0_attn_a_ln_b").reshape(NCH, 128).T),
        "og_rep": f(np.broadcast_to(g("L0_attn_a_lno_g"), (128, D))),
        "ob_rep": f(np.broadcast_to(g("L0_attn_a_lno_b"), (128, D))),
        "ld0": f(g("L0_attn_b_log_decay")[0:1].reshape(1, 1)),
        "r_w": _img(g("L0_ffn_router_w")),
        "rb_rep": f(np.broadcast_to(g("L0_ffn_router_b"), (128, 8))),
        "ew1": np.stack([_img(w) for w in np.concatenate(
            [g("L0_ffn_routed_w1"), g("L0_ffn_shared_w1")], 0)]),
        "ew3": np.stack([_img(w) for w in np.concatenate(
            [g("L0_ffn_routed_w3"), g("L0_ffn_shared_w3")], 0)]),
        "ew2": np.stack([_img(w) for w in np.concatenate(
            [g("L0_ffn_routed_w2"), g("L0_ffn_shared_w2")], 0)]),
        "ident": np.eye(128, dtype=np.float32),
        "eyemod": (np.arange(128)[:, None] % 64 == np.arange(64)[None, :]
                   ).astype(np.float32),
        "trimask": np.triu(np.full((128, 128), NEG, np.float32), 1),
        "ones128": np.ones((1, 128), np.float32),
    }
    for w in ("q", "k", "v", "g", "o"):
        d["wb%sT" % w] = _img(g("L0_attn_b_W_" + w).T)
    u = np.arange(128)[:, None]
    q = np.arange(128)[None, :]
    low = ((u // 32) == (q // 32)) & (q >= u)
    d["diffT"] = ((q - u) * low).astype(np.float32)
    d["indT"] = low.astype(np.float32)
    return d


def kernel(**inputs):
    nc, _ = build_core()
    in_maps = [make_inputs(inputs, c) for c in range(8)]
    res = run_bass_kernel_spmd(nc, in_maps, core_ids=list(range(8)))
    out = np.concatenate([r["x_out"].reshape(2, SEQ, D) for r in res.results], 0)
    return out.astype(np.float32)


# version 11
# speedup vs baseline: 1.0254x; 1.0118x over previous
import math
import numpy as np
import concourse.bass as bass
import concourse.mybir as mybir
from concourse.bass_utils import run_bass_kernel_spmd

F32 = mybir.dt.float32
BF16 = mybir.dt.bfloat16
AF = mybir.ActivationFunctionType
ALU = mybir.AluOpType
AX = mybir.AxisListType

D = 1024
NH = 16
NCH = 8
GT = 512
NT = 4
CORE_TOK = 4096
SEQ = 2048
EPS = 1e-5
QSCALE = (64 ** -0.5) * math.sqrt(64 ** -0.5)
KSCALE = math.sqrt(64 ** -0.5)
NEG = -1.0e30

ENGS = ("sync", "gpsimd", "scalar", "vector", "tensor")


class Buf:

    def __init__(self, ap, name, gran):
        self.ap = ap
        self.name = name
        self.gran = gran

    def __getitem__(self, sl):
        return self.ap[sl]

    def k(self, c0=None, c1=None):
        if c0 is None:
            c0, c1 = 0, self.ap.shape[-1]
        lo = c0 // self.gran
        hi = (c1 + self.gran - 1) // self.gran
        return [(self.name, b) for b in range(lo, hi)]


class Prog:

    def __init__(self, nc):
        self.nc = nc
        self.thunks = {e: [] for e in ENGS}
        self.sems = {}
        self.cnt = {}
        self.stepv = {}
        self.waited = {}
        self.seq = {e: 0 for e in ENGS}
        self.lastdrain = {e: 0 for e in ENGS}
        self.state = {}
        self.engsem = {"tensor": "tsem", "vector": "vsem",
                       "scalar": "ssem", "gpsimd": "gsem"}
        for s in ("tsem", "vsem", "ssem", "gsem"):
            self._mksem(s, 1)
        self.pool = {"sync": ["q%d" % i for i in range(16)],
                     "gpsimd": ["gq%d" % i for i in range(8)]}
        for q in self.pool["sync"] + self.pool["gpsimd"]:
            self._mksem(q, 16)
        self.pool_i = {"sync": 0, "gpsimd": 0}
        self.ndrain = {e: 0 for e in ENGS}

    def _mksem(self, name, step):
        self.sems[name] = self.nc.alloc_semaphore(name=name)
        self.cnt[name] = 0
        self.stepv[name] = step

    def _wait(self, eng, sem, val):
        if val <= self.waited.get((eng, sem), 0):
            return
        h = self.sems[sem]
        self.thunks[eng].append(lambda e, s=h, v=val: e.wait_ge(s, v))
        self.waited[(eng, sem)] = val

    def _drain(self, eng):
        self.thunks[eng].append(lambda e: e.drain())
        self.seq[eng] += 1
        self.lastdrain[eng] = self.seq[eng]
        self.ndrain[eng] += 1

    def _deps(self, eng, reads, writes, waw_ok):
        need = []
        for k in reads:
            st = self.state.get(k)
            if st and st[0] is not None:
                need.append((st[0], False))
        for k in writes:
            st = self.state.get(k)
            if st:
                if st[0] is not None:
                    need.append((st[0], waw_ok))
                for r in st[1]:
                    need.append((r, False))
        drained = False
        for (deng, dsem, dval, dseq), skip_same in need:
            if deng == eng:
                if (eng != "tensor" and not skip_same
                        and dseq > self.lastdrain[eng] and not drained):
                    self._drain(eng)
                    drained = True
            else:
                self._wait(eng, dsem, dval)

    def _record(self, acc, reads, writes):
        for k in writes:
            self.state[k] = [acc, []]
        for k in reads:
            self.state.setdefault(k, [None, []])[1].append(acc)

    def op(self, eng, fn, reads=(), writes=(), waw_ok=False):
        self._deps(eng, reads, writes, waw_ok or eng == "tensor")
        sem = self.engsem[eng]
        h, step = self.sems[sem], self.stepv[sem]

        def thunk(e, fn=fn, h=h, step=step):
            fn(e).then_inc(h, step)
        self.thunks[eng].append(thunk)
        self.cnt[sem] += step
        self.seq[eng] += 1
        acc = (eng, sem, self.cnt[sem], self.seq[eng])
        self._record(acc, reads, writes)
        return acc

    def dma(self, eng, out, in_, reads=(), writes=()):
        self._deps(eng, reads, writes, False)
        sems = self.pool[eng]
        sem = sems[self.pool_i[eng] % len(sems)]
        self.pool_i[eng] += 1
        if self.cnt[sem]:
            self._wait(eng, sem, self.cnt[sem])
        h = self.sems[sem]

        def thunk(e, o=out, i=in_, h=h):
            e.dma_start(out=o, in_=i).then_inc(h, 16)
        self.thunks[eng].append(thunk)
        self.cnt[sem] += 16
        self.seq[eng] += 1
        acc = ("dma:" + sem, sem, self.cnt[sem], self.seq[eng])
        self._record(acc, reads, writes)
        return acc

    def dma_barrier(self, eng, queue):
        for sem in self.pool[queue]:
            if self.cnt[sem]:
                self._wait(eng, sem, self.cnt[sem])

    def finalize(self):
        with self.nc.Block() as block:
            block.sync(lambda e: [t(e) for t in self.thunks["sync"]])
            block.gpsimd(lambda e: [t(e) for t in self.thunks["gpsimd"]])
            block.scalar(lambda e: [t(e) for t in self.thunks["scalar"]])
            block.vector(lambda e: [t(e) for t in self.thunks["vector"]])
            block.tensor(lambda e: [t(e) for t in self.thunks["tensor"]])


def build_core(cfg=None):
    cfg = cfg or {}
    n_groups = cfg.get("n_groups", 8)
    subs = cfg.get("subs", ("a", "f", "b", "f"))

    nc = bass.Bass("TRN2", target_bir_lowering=False, debug=False)
    p = Prog(nc)

    hbm = {}

    def hin(name, shape):
        hbm[name] = nc.dram_tensor(name, list(shape), F32, kind="ExternalInput").ap()
        return hbm[name]

    x_in = hin("x_in", (CORE_TOK, D))
    x_out = nc.dram_tensor("x_out", [CORE_TOK, D], F32, kind="ExternalOutput").ap()
    hin("wqT", (128, 8192))
    hin("wkvT", (128, 1024))
    hin("woutT", (128, 8192))
    hin("ag_col", (128, NCH))
    hin("ab_col", (128, NCH))
    hin("og_rep", (128, D))
    hin("ob_rep", (128, D))
    for w in ("q", "k", "v", "g", "o"):
        hin("wb%sT" % w, (128, 8192))
    hin("ld0", (1, 1))
    hin("r_w", (128, 64))
    hin("rb_rep", (128, 8))
    ew1 = hin("ew1", (10, 128, 4096))
    ew3 = hin("ew3", (10, 128, 4096))
    ew2 = hin("ew2", (10, 128, 4096))
    hin("ident", (128, 128))
    hin("eyemod", (128, 64))
    hin("trimask", (128, 128))
    hin("diffT", (128, 128))
    hin("indT", (128, 128))
    hin("ones128", (1, 128))

    wsplit = {}
    for name, cols in (("wq", 8192), ("wkv", 1024), ("wout", 8192),
                       ("wbq", 8192), ("wbk", 8192), ("wbv", 8192),
                       ("wbg", 8192), ("wbo", 8192)):
        wsplit[name] = (
            nc.dram_tensor("s_%s_h" % name, [128, cols], BF16, kind="Internal").ap(),
            nc.dram_tensor("s_%s_l" % name, [128, cols], BF16, kind="Internal").ap())
    for name in ("e1", "e3", "e2"):
        wsplit[name] = (
            nc.dram_tensor("s_%s_h" % name, [10, 128, 4096], BF16,
                           kind="Internal").ap(),
            nc.dram_tensor("s_%s_l" % name, [10, 128, 4096], BF16,
                           kind="Internal").ap())

    def sb(n, cols, gran, rows=128, dt=F32):
        ap = nc.alloc_sbuf_tensor(n, [rows, cols], dt).ap()
        return Buf(ap, n, gran)

    x = sb("x", NT * D, 1024)
    xn = sb("xn", NT * D, 512)
    fmh = sb("fmh", NCH * GT, 128, dt=BF16)
    fml = sb("fml", NCH * GT, 128, dt=BF16)
    buf1 = sb("buf1", NCH * GT, 128)
    b1h = sb("b1h", NCH * GT, 128, dt=BF16)
    b1l = sb("b1l", NCH * GT, 128, dt=BF16)
    b2h = sb("b2h", NCH * GT, 128, dt=BF16)
    b2l = sb("b2l", NCH * GT, 128, dt=BF16)
    Pb = sb("P", SEQ, 512)
    PTb = sb("PT", GT, 128)
    stmh = sb("stmh", NT * 128, 128, dt=BF16)
    stml = sb("stml", NT * 128, 128, dt=BF16)
    kcache = sb("kcache", 2 * SEQ, 512)
    vcache = sb("vcache", SEQ, 64)
    W1h = sb("W1h", 8192, 512, dt=BF16)
    W1l = sb("W1l", 8192, 512, dt=BF16)
    W2h = sb("W2h", 8192, 512, dt=BF16)
    W2l = sb("W2l", 8192, 512, dt=BF16)
    tmp5 = sb("tmp5", GT, 512)
    ident = sb("identS", 128, 128)
    eyemod = sb("eyemodS", 64, 64)
    trimask = sb("trimaskS", 128, 128)
    maskT = sb("maskTS", 128, 128)
    ones128 = sb("ones128S", 128, 128, rows=1)
    agc = sb("agc", NCH, 1)
    abc = sb("abc", NCH, 1)
    rbr = sb("rbr", 8, 8)
    rwh = sb("rwh", NCH * 8, 8, dt=BF16)
    rwl = sb("rwl", NCH * 8, 8, dt=BF16)
    st6 = sb("st6", 48, 6)
    mv = sb("mv", 2 * NT, 2)
    rr = sb("rr", NT, 1)
    ss = sb("ss", NT, 1)
    mx4 = sb("mx4", 8, 1)
    sums4 = sb("sums4", 8, 1)
    rsum = sb("rsum", 2, 1)
    biasq = sb("biasq", NCH, 8)
    biaskv = sb("biaskv", 1, 1)
    gates = sb("gates", NT * 8, 8)
    lgt = sb("lgt", 8, 8)
    l1t = sb("l1t", 1, 1)
    nl1t = sb("nl1t", 1, 1)
    l2t = sb("l2t", 1, 1)
    m1t = sb("m1t", 8, 8)
    ge2t = sb("ge2t", 8, 8)
    epsc = sb("epsc", 1, 1)
    nhc = sb("nhc", 1, 1)
    dect = sb("dect", 1, 1, rows=1)
    lnd128 = sb("lnd128", 1, 1)

    ps = [Buf(nc.alloc_psum_tensor("ps%d" % i, [128, 512], F32).ap(),
              "ps%d" % i, 512) for i in range(8)]

    for name, dst in (("ident", ident), ("eyemod", eyemod),
                      ("trimask", trimask), ("ones128", ones128),
                      ("ag_col", agc), ("ab_col", abc), ("rb_rep", rbr),
                      ("ld0", dect),):
        p.dma("sync", dst.ap, hbm[name], writes=dst.k())
    p.op("vector", lambda e: e.memset(epsc.ap, float(EPS)), writes=epsc.k())
    p.op("vector", lambda e: e.memset(nhc.ap, -0.5), writes=nhc.k())

    p.dma("sync", PTb[:, 0:128], hbm["diffT"], writes=PTb.k(0, 128))
    p.dma("sync", tmp5[:, 0:128], hbm["indT"], writes=tmp5.k(0, 128))
    p.op("scalar", lambda e: e.activation(dect.ap, dect.ap, AF.Sigmoid),
         reads=dect.k(), writes=dect.k())
    p.op("scalar", lambda e: e.activation(dect.ap, dect.ap, AF.Ln),
         reads=dect.k(), writes=dect.k())
    p.op("tensor", lambda e: e.matmul(ps[0][:, 0:1], ones128.ap, dect.ap,
                                      start=True, stop=True),
         reads=dect.k() + ones128.k(), writes=ps[0].k())
    p.op("vector", lambda e: e.tensor_copy(lnd128.ap, ps[0][:, 0:1]),
         reads=ps[0].k(), writes=lnd128.k())
    p.op("scalar", lambda e: e.activation(maskT.ap, PTb[:, 0:128], AF.Exp,
                                          scale=lnd128.ap),
         reads=PTb.k(0, 128) + lnd128.k(), writes=maskT.k())
    p.op("vector", lambda e: e.tensor_mul(maskT.ap, maskT.ap, tmp5[:, 0:128]),
         reads=maskT.k() + tmp5.k(0, 128), writes=maskT.k())

    wp_i = [0]

    def wprep(src_h, cols, dst_pair, prep=None, eslice=None):
        npc = (cols + 1023) // 1024
        for pc in range(npc):
            w = min(1024, cols - 1024 * pc)
            o = 1024 * (wp_i[0] % 2)
            wp_i[0] += 1
            src = src_h[eslice] if eslice is not None else src_h
            p.dma("sync", Pb[:, o:o + w], src[:, 1024 * pc:1024 * pc + w],
                  writes=Pb.k(o, o + w))
            if prep is not None:
                prep(pc, w, o)
            p.op("scalar", lambda e, w=w, o=o: e.copy(
                b2h[:, o:o + w], Pb[:, o:o + w]),
                reads=Pb.k(o, o + w), writes=b2h.k(o, o + w))
            p.op("vector", lambda e, w=w, o=o: e.tensor_sub(
                b2l[:, o:o + w], Pb[:, o:o + w], b2h[:, o:o + w]),
                reads=Pb.k(o, o + w) + b2h.k(o, o + w),
                writes=b2l.k(o, o + w))
            dh, dl = dst_pair
            dsth = dh[eslice] if eslice is not None else dh
            dstl = dl[eslice] if eslice is not None else dl
            p.dma("gpsimd", dsth[:, 1024 * pc:1024 * pc + w], b2h[:, o:o + w],
                  reads=b2h.k(o, o + w))
            p.dma("gpsimd", dstl[:, 1024 * pc:1024 * pc + w], b2l[:, o:o + w],
                  reads=b2l.k(o, o + w))

    for pc in range(4):
        p.dma("sync", Pb[:, 0:2048], hbm["wqT"][:, 2048 * pc:2048 * pc + 2048],
              writes=Pb.k(0, 2048))
        def bq_mm(e, pc=pc):
            for eo in range(NCH):
                for i, ci in enumerate(range(2 * pc, 2 * pc + 2)):
                    lo = 1024 * (ci - 2 * pc)
                    last = e.matmul(
                        ps[4][:, eo:eo + 1],
                        Pb[:, lo + 128 * eo:lo + 128 * eo + 128],
                        abc[:, ci:ci + 1],
                        start=(i == 0), stop=(i == 1))
            return last
        p.op("tensor", bq_mm, reads=Pb.k(0, 2048) + abc.k(), writes=ps[4].k(),
             waw_ok=True)
        if pc == 0:
            p.op("vector", lambda e: e.tensor_copy(biasq.ap, ps[4][:, 0:NCH]),
                 reads=ps[4].k(), writes=biasq.k())
        else:
            p.op("vector", lambda e: e.tensor_add(
                biasq.ap, biasq.ap, ps[4][:, 0:NCH]),
                reads=ps[4].k() + biasq.k(), writes=biasq.k())
    p.op("vector", lambda e: e.tensor_scalar_mul(
        biasq.ap, biasq.ap, float(QSCALE)),
        reads=biasq.k(), writes=biasq.k())

    def prep_wq(pc, w, o):
        p.op("vector", lambda e, o=o, ci=pc: e.tensor_scalar(
            Pb[:, o:o + 1024], Pb[:, o:o + 1024],
            agc[:, ci:ci + 1], float(QSCALE), ALU.mult, ALU.mult),
            reads=Pb.k(o, o + 1024) + agc.k(),
            writes=Pb.k(o, o + 1024))
    wprep(hbm["wqT"], 8192, wsplit["wq"], prep=prep_wq)

    p.dma("sync", Pb[:, 0:1024], hbm["wkvT"], writes=Pb.k(0, 1024))
    def bkv_mm(e):
        for ci in range(NCH):
            last = e.matmul(ps[2][:, 0:1], Pb[:, 128 * ci:128 * ci + 128],
                            abc[:, ci:ci + 1], start=(ci == 0), stop=(ci == 7))
        return last
    p.op("tensor", bkv_mm, reads=Pb.k(0, 1024) + abc.k(), writes=ps[2].k())
    p.op("vector", lambda e: e.tensor_copy(biaskv.ap, ps[2][:, 0:1]),
         reads=ps[2].k(), writes=biaskv.k())
    p.op("vector", lambda e: e.tensor_scalar_mul(
        biaskv[0:64, :], biaskv[0:64, :], float(KSCALE)),
        reads=biaskv.k(), writes=biaskv.k())

    def prep_wkv(pc, w, o):
        for ci in range(NCH):
            c0 = o + 128 * ci
            p.op("vector", lambda e, c0=c0, ci=ci: e.tensor_scalar_mul(
                Pb[:, c0:c0 + 128], Pb[:, c0:c0 + 128],
                agc[:, ci:ci + 1]),
                reads=Pb.k(c0, c0 + 128) + agc.k(),
                writes=Pb.k(c0, c0 + 128))
            p.op("vector", lambda e, c0=c0: e.tensor_scalar_mul(
                Pb[:, c0:c0 + 64], Pb[:, c0:c0 + 64], float(KSCALE)),
                reads=Pb.k(c0, c0 + 64),
                writes=Pb.k(c0, c0 + 64))
    wprep(hbm["wkvT"], 1024, wsplit["wkv"], prep=prep_wkv)

    wprep(hbm["woutT"], 8192, wsplit["wout"])
    for ei in (8, 9, 0, 1, 2, 3, 4, 5, 6, 7):
        wprep(ew1, 4096, wsplit["e1"], eslice=ei)
        wprep(ew3, 4096, wsplit["e3"], eslice=ei)
        wprep(ew2, 4096, wsplit["e2"], eslice=ei)
    for w in ("q", "k", "v", "g", "o"):
        wprep(hbm["wb%sT" % w], 8192, wsplit["wb" + w])
    p.dma("sync", PTb[:, 0:64], hbm["r_w"], writes=PTb.k(0, 64))
    p.op("scalar", lambda e: e.copy(rwh.ap, PTb[:, 0:64]),
         reads=PTb.k(0, 64), writes=rwh.k())
    p.op("vector", lambda e: e.tensor_sub(rwl.ap, PTb[:, 0:64], rwh.ap),
         reads=PTb.k(0, 64) + rwh.k(), writes=rwl.k())

    evac_tgl = [0]

    def evac_eng():
        evac_tgl[0] ^= 1
        return "vector" if evac_tgl[0] else "scalar"

    def copy_op(eng, dst, src):
        if eng == "vector":
            return lambda e: e.tensor_copy(dst, src)
        return lambda e: e.copy(dst, src)

    def split_psum(bank, w, dsth, dstl, d0):
        p.op("scalar", lambda e: e.copy(dsth[:, d0:d0 + w], bank[:, 0:w]),
             reads=bank.k(), writes=dsth.k(d0, d0 + w))
        p.op("vector", lambda e: e.tensor_sub(
            dstl[:, d0:d0 + w], bank[:, 0:w], dsth[:, d0:d0 + w]),
            reads=bank.k() + dsth.k(d0, d0 + w), writes=dstl.k(d0, d0 + w))

    def ln_tile(j, src=None, dst=None):
        src = src if src is not None else x
        dst = dst if dst is not None else xn
        sl = src[:, D * j:D * j + D]
        s0 = 12 * j
        p.op("vector", lambda e, sl=sl, s0=s0: e.bn_stats(
            st6[:, s0:s0 + 6], sl[:, 0:512]),
            reads=src.k(D * j, D * j + 512), writes=st6.k(s0, s0 + 6))
        p.op("vector", lambda e, sl=sl, s0=s0: e.bn_stats(
            st6[:, s0 + 6:s0 + 12], sl[:, 512:1024]),
            reads=src.k(D * j + 512, D * j + D), writes=st6.k(s0 + 6, s0 + 12))
        p.op("vector", lambda e, j=j, s0=s0: e.bn_aggr(
            mv[:, 2 * j:2 * j + 2], st6[:, s0:s0 + 12]),
            reads=st6.k(s0, s0 + 12), writes=mv.k(2 * j, 2 * j + 2))
        p.op("scalar", lambda e, j=j: e.activation(
            rr[:, j:j + 1], mv[:, 2 * j + 1:2 * j + 2], AF.Ln, bias=epsc.ap),
            reads=mv.k(2 * j + 1, 2 * j + 2) + epsc.k(), writes=rr.k(j, j + 1))
        p.op("scalar", lambda e, j=j: e.activation(
            rr[:, j:j + 1], rr[:, j:j + 1], AF.Exp, scale=nhc.ap),
            reads=rr.k(j, j + 1) + nhc.k(), writes=rr.k(j, j + 1))
        p.op("vector", lambda e, j=j: e.scalar_tensor_tensor(
            ss[:, j:j + 1], mv[:, 2 * j:2 * j + 1], -1.0, rr[:, j:j + 1],
            ALU.mult, ALU.mult),
            reads=mv.k(2 * j, 2 * j + 1) + rr.k(j, j + 1), writes=ss.k(j, j + 1))
        p.op("scalar", lambda e, j=j: e.activation(
            dst[:, D * j:D * j + D], src[:, D * j:D * j + D], AF.Identity,
            bias=ss[:, j:j + 1], scale=rr[:, j:j + 1]),
            reads=src.k(D * j, D * j + D) + rr.k(j, j + 1) + ss.k(j, j + 1),
            writes=dst.k(D * j, D * j + D))

    def ln_pass(src, dst):
        for j in range(NT):
            ln_tile(j, src, dst)

    def transpose_split_fm(src):
        for ci in range(NCH):
            bank = ps[ci % 2]
            for j in range(NT):
                p.op("tensor", lambda e, ci=ci, j=j, bank=bank: e.transpose(
                    bank[:, 128 * j:128 * j + 128],
                    src[:, D * j + 128 * ci:D * j + 128 * ci + 128], ident.ap),
                    reads=src.k(D * j + 128 * ci, D * j + 128 * ci + 128) + ident.k(),
                    writes=bank.k(), waw_ok=True)
            split_psum(bank, GT, fmh, fml, GT * ci)

    def loadw(slot_h, slot_l, name, col0=0, cols=8192, eslice=None):
        sh, sl_ = wsplit[name]
        if eslice is not None:
            sh, sl_ = sh[eslice], sl_[eslice]
        p.dma("sync", slot_h[:, col0:col0 + cols], sh[:, 0:cols],
              writes=slot_h.k(col0, col0 + cols))
        p.dma("sync", slot_l[:, col0:col0 + cols], sl_[:, 0:cols],
              writes=slot_l.k(col0, col0 + cols))

    def mm3(e, bank_ap, wh, wl, xh, xl, start, stop):
        e.matmul(bank_ap, wh, xh, start=start, stop=False)
        e.matmul(bank_ap, wh, xl, start=False, stop=False)
        return e.matmul(bank_ap, wl, xh, start=False, stop=stop)

    def fm_keys(c0, c1):
        return fmh.k(c0, c1) + fml.k(c0, c1)

    def attn_a(g, cb):
        grel = g % 4
        seqh = g // 4
        kcb = SEQ * seqh
        vb = 1024 * seqh
        ln_pass(xn, xn)
        transpose_split_fm(xn)

        loadw(W2h, W2l, "wkv", 0, 1024)
        def kv_mm(e):
            for ci in range(NCH):
                last = mm3(e, ps[2].ap,
                           W2h[:, 128 * ci:128 * ci + 128],
                           W2l[:, 128 * ci:128 * ci + 128],
                           fmh[:, GT * ci:GT * ci + GT],
                           fml[:, GT * ci:GT * ci + GT],
                           ci == 0, ci == 7)
            return last
        p.op("tensor", kv_mm,
             reads=W2h.k(0, 1024) + W2l.k(0, 1024) + fm_keys(0, NCH * GT),
             writes=ps[2].k())
        p.op("vector", lambda e: e.tensor_scalar_add(PTb.ap, ps[2].ap, biaskv.ap),
             reads=ps[2].k() + biaskv.k(), writes=PTb.k())
        ktok = kcb + 512 * grel
        p.dma("gpsimd", kcache[64:128, ktok:ktok + 512], PTb[0:64, :],
              reads=PTb.k(), writes=kcache.k(ktok, ktok + 512))
        p.op("vector", lambda e, ktok=ktok: e.tensor_copy(
            kcache[0:64, ktok:ktok + 512], PTb[0:64, :]),
            reads=PTb.k(), writes=kcache.k(ktok, ktok + 512))
        for j in range(NT):
            p.op("tensor", lambda e, j=j: e.transpose(
                ps[3][:, 64 * j:64 * j + 64], PTb[64:128, 128 * j:128 * j + 128],
                eyemod.ap[64:128, :]),
                reads=PTb.k(128 * j, 128 * j + 128) + eyemod.k(),
                writes=ps[3].k(), waw_ok=True)
        vtok = vb + 256 * grel
        p.op("vector", lambda e, vtok=vtok: e.tensor_copy(
            vcache[:, vtok:vtok + 256], ps[3][:, 0:256]),
            reads=ps[3].k(), writes=vcache.k(vtok, vtok + 256))

        loadw(W1h, W1l, "wq")
        for eo in range(NCH):
            bank = ps[4 + (eo % 2)]
            def q_mm(e, eo=eo, bank=bank):
                for ci in range(NCH):
                    last = mm3(e, bank.ap,
                               W1h[:, 1024 * ci + 128 * eo:1024 * ci + 128 * eo + 128],
                               W1l[:, 1024 * ci + 128 * eo:1024 * ci + 128 * eo + 128],
                               fmh[:, GT * ci:GT * ci + GT],
                               fml[:, GT * ci:GT * ci + GT],
                               ci == 0, ci == 7)
                return last
            p.op("tensor", q_mm, reads=W1h.k() + W1l.k() + fm_keys(0, NCH * GT),
                 writes=bank.k())
            p.op("vector", lambda e, eo=eo, bank=bank: e.tensor_scalar_add(
                buf1[:, GT * eo:GT * eo + GT], bank.ap, biasq[:, eo:eo + 1]),
                reads=bank.k() + biasq.k(), writes=buf1.k(GT * eo, GT * eo + GT))

        pairs = [(h, jt) for jt in range(NT) for h in range(NH)]

        def sbase(n):
            h, jt = pairs[n]
            nchunk = ((4 * grel + jt + 1) * 128 + 511) // 512
            return 2 * (n % 2) if nchunk <= 2 else 0

        def att_S(n):
            h, jt = pairs[n]
            hrow = 64 * (h % 2)
            ig = 4 * grel + jt
            nk = (ig + 1) * 128
            nchunk = (nk + 511) // 512
            sb0 = sbase(n)
            lq0 = GT * (h // 2) + 128 * jt
            lhs_q = buf1[hrow:hrow + 64, lq0:lq0 + 128]
            for c in range(nchunk):
                w = min(512, nk - 512 * c)
                p.op("tensor", lambda e, c=c, w=w, lhs_q=lhs_q, hrow=hrow,
                     sb0=sb0:
                     e.matmul(
                         ps[sb0 + c][:, 0:w], lhs_q,
                         kcache[hrow:hrow + 64, kcb + 512 * c:kcb + 512 * c + w],
                         start=True, stop=True),
                     reads=buf1.k(lq0, lq0 + 128) +
                           kcache.k(kcb + 512 * c, kcb + 512 * c + w),
                     writes=ps[sb0 + c].k())
            dc, doff = (ig * 128) // 512, (ig * 128) % 512
            p.op("vector", lambda e, dc=dc, doff=doff, sb0=sb0: e.tensor_add(
                ps[sb0 + dc][:, doff:doff + 128],
                ps[sb0 + dc][:, doff:doff + 128], trimask.ap),
                reads=ps[sb0 + dc].k() + trimask.k(),
                writes=ps[sb0 + dc].k())

        def att_p1(n):
            h, jt = pairs[n]
            ig = 4 * grel + jt
            nk = (ig + 1) * 128
            nchunk = (nk + 511) // 512
            sb0 = sbase(n)
            s4 = 4 * (n % 2)
            for c in range(nchunk):
                w = min(512, nk - 512 * c)
                p.op("vector", lambda e, c=c, w=w, s4=s4, sb0=sb0: e.reduce_max(
                    mx4[:, s4 + c:s4 + c + 1], ps[sb0 + c][:, 0:w], axis=AX.X,
                    negate=True),
                    reads=ps[sb0 + c].k(), writes=mx4.k(s4 + c, s4 + c + 1))
            if nchunk > 1:
                p.op("vector", lambda e, n_=nchunk, s4=s4: e.tensor_reduce(
                    mx4[:, s4:s4 + 1], mx4[:, s4:s4 + n_], AX.X, ALU.min),
                    reads=mx4.k(s4, s4 + nchunk), writes=mx4.k(s4, s4 + 1))
            for c in range(nchunk):
                w = min(512, nk - 512 * c)
                p.op("scalar", lambda e, c=c, w=w, s4=s4, sb0=sb0: e.activation(
                    Pb[:, 512 * c:512 * c + w], ps[sb0 + c][:, 0:w], AF.Exp,
                    bias=mx4[:, s4:s4 + 1], accum_out=sums4[:, s4 + c:s4 + c + 1]),
                    reads=ps[sb0 + c].k() + mx4.k(s4, s4 + 1),
                    writes=Pb.k(512 * c, 512 * c + w) + sums4.k(s4 + c, s4 + c + 1))

        def att_p2v(n):
            h, jt = pairs[n]
            nk = (4 * grel + jt + 1) * 128
            nchunk = (nk + 511) // 512
            s4 = 4 * (n % 2)
            r0 = n % 2
            p.op("vector", lambda e, n_=nchunk, s4=s4, r0=r0: e.tensor_reduce(
                rsum[:, r0:r0 + 1], sums4[:, s4:s4 + n_], AX.X, ALU.add),
                reads=sums4.k(s4, s4 + nchunk), writes=rsum.k(r0, r0 + 1))
            p.op("vector", lambda e, r0=r0: e.reciprocal(
                rsum[:, r0:r0 + 1], rsum[:, r0:r0 + 1]),
                reads=rsum.k(r0, r0 + 1), writes=rsum.k(r0, r0 + 1))

        def att_p2s(n):
            h, jt = pairs[n]
            nk = (4 * grel + jt + 1) * 128
            nchunk = (nk + 511) // 512
            r0 = n % 2
            for c in range(nchunk):
                w = min(512, nk - 512 * c)
                p.op("scalar", lambda e, c=c, w=w, r0=r0: e.activation(
                    Pb[:, 512 * c:512 * c + w], Pb[:, 512 * c:512 * c + w],
                    AF.Identity, scale=rsum[:, r0:r0 + 1]),
                    reads=Pb.k(512 * c, 512 * c + w) + rsum.k(r0, r0 + 1),
                    writes=Pb.k(512 * c, 512 * c + w))

        def att_ptpv(n):
            h, jt = pairs[n]
            hrow = 64 * (h % 2)
            obank = ps[7] if h % 2 == 0 else ps[5]
            ig = 4 * grel + jt
            lq0 = GT * (h // 2) + 128 * jt
            nkb = ig + 1
            for b0 in range(0, nkb, 4):
                bcnt = min(4, nkb - b0)
                ptbank = ps[6] if (b0 // 4) % 2 == 0 else ps[4]
                for bi in range(bcnt):
                    kb = b0 + bi
                    p.op("tensor", lambda e, kb=kb, bi=bi, ptbank=ptbank:
                         e.transpose(
                             ptbank[:, 128 * bi:128 * bi + 128],
                             Pb[:, 128 * kb:128 * kb + 128], ident.ap),
                         reads=Pb.k(128 * kb, 128 * kb + 128) + ident.k(),
                         writes=ptbank.k(), waw_ok=True)
                eng = evac_eng()
                p.op(eng, copy_op(eng, PTb[:, 0:128 * bcnt],
                                  ptbank[:, 0:128 * bcnt]),
                     reads=ptbank.k(), writes=PTb.k(0, 128 * bcnt))
                def pv(e, b0=b0, bcnt=bcnt, jt=jt, nkb=nkb, vb=vb,
                       obank=obank, hrow=hrow):
                    tp = None if hrow == 0 else (0, 64)
                    for bi in range(bcnt):
                        kb = b0 + bi
                        last = e.matmul(
                            obank[hrow:hrow + 64, 128 * jt:128 * jt + 128],
                            vcache[:, vb + 64 * kb:vb + 64 * kb + 64],
                            PTb[:, 128 * bi:128 * bi + 128],
                            start=(kb == 0), stop=(kb == nkb - 1),
                            tile_position=tp)
                    return last
                p.op("tensor", pv,
                     reads=PTb.k(0, 128 * bcnt) + vcache.k(vb, vb + 64 * nkb),
                     writes=obank.k(), waw_ok=True)
            p.op("scalar", lambda e, hrow=hrow, lq0=lq0, jt=jt, obank=obank:
                 e.copy(b2h[hrow:hrow + 64, lq0:lq0 + 128],
                        obank[hrow:hrow + 64, 128 * jt:128 * jt + 128]),
                 reads=obank.k(), writes=b2h.k(lq0, lq0 + 128))
            p.op("vector", lambda e, hrow=hrow, lq0=lq0, jt=jt, obank=obank:
                 e.tensor_sub(b2l[hrow:hrow + 64, lq0:lq0 + 128],
                              obank[hrow:hrow + 64, 128 * jt:128 * jt + 128],
                              b2h[hrow:hrow + 64, lq0:lq0 + 128]),
                 reads=obank.k() + b2h.k(lq0, lq0 + 128),
                 writes=b2l.k(lq0, lq0 + 128))

        att_S(0)
        for n in range(len(pairs)):
            att_p1(n)
            att_p2v(n)
            if n + 1 < len(pairs):
                att_S(n + 1)
            att_p2s(n)
            att_ptpv(n)

        p.dma("sync", Pb[:, 0:1024], hbm["og_rep"], writes=Pb.k(0, 1024))
        p.dma("sync", Pb[:, 1024:2048], hbm["ob_rep"], writes=Pb.k(1024, 2048))
        loadw(W2h, W2l, "wout")
        for jt in range(NT):
            for half in range(2):
                bank = ps[half]
                def o_mm(e, jt=jt, half=half, bank=bank):
                    for ci in range(NCH):
                        last = mm3(
                            e, bank.ap,
                            b2h[:, GT * ci + 128 * jt:GT * ci + 128 * jt + 128],
                            b2l[:, GT * ci + 128 * jt:GT * ci + 128 * jt + 128],
                            W2h[:, 1024 * ci + 512 * half:1024 * ci + 512 * half + 512],
                            W2l[:, 1024 * ci + 512 * half:1024 * ci + 512 * half + 512],
                            ci == 0, ci == 7)
                    return last
                p.op("tensor", o_mm,
                     reads=W2h.k() + W2l.k() +
                           [k for ci in range(NCH) for k in
                            b2h.k(GT * ci + 128 * jt, GT * ci + 128 * jt + 128) +
                            b2l.k(GT * ci + 128 * jt, GT * ci + 128 * jt + 128)],
                     writes=bank.k())
            s0 = 12 * jt
            p.op("vector", lambda e, s0=s0: e.bn_stats(st6[:, s0:s0 + 6], ps[0].ap),
                 reads=ps[0].k(), writes=st6.k(s0, s0 + 6))
            p.op("vector", lambda e, s0=s0: e.bn_stats(st6[:, s0 + 6:s0 + 12],
                                                       ps[1].ap),
                 reads=ps[1].k(), writes=st6.k(s0 + 6, s0 + 12))
            p.op("vector", lambda e, jt=jt, s0=s0: e.bn_aggr(
                mv[:, 2 * jt:2 * jt + 2], st6[:, s0:s0 + 12]),
                reads=st6.k(s0, s0 + 12), writes=mv.k(2 * jt, 2 * jt + 2))
            p.op("scalar", lambda e, jt=jt: e.activation(
                rr[:, jt:jt + 1], mv[:, 2 * jt + 1:2 * jt + 2], AF.Ln,
                bias=epsc.ap),
                reads=mv.k(2 * jt + 1, 2 * jt + 2) + epsc.k(),
                writes=rr.k(jt, jt + 1))
            p.op("scalar", lambda e, jt=jt: e.activation(
                rr[:, jt:jt + 1], rr[:, jt:jt + 1], AF.Exp, scale=nhc.ap),
                reads=rr.k(jt, jt + 1) + nhc.k(), writes=rr.k(jt, jt + 1))
            p.op("vector", lambda e, jt=jt: e.scalar_tensor_tensor(
                ss[:, jt:jt + 1], mv[:, 2 * jt:2 * jt + 1], -1.0, rr[:, jt:jt + 1],
                ALU.mult, ALU.mult),
                reads=mv.k(2 * jt, 2 * jt + 1) + rr.k(jt, jt + 1),
                writes=ss.k(jt, jt + 1))
            for half in range(2):
                bank = ps[half]
                a0 = D * jt + 512 * half
                g0 = 512 * half
                p.op("scalar", lambda e, bank=bank, a0=a0, jt=jt: e.activation(
                    xn[:, a0:a0 + 512], bank.ap, AF.Identity, bias=ss[:, jt:jt + 1],
                    scale=rr[:, jt:jt + 1]),
                    reads=bank.k() + rr.k(jt, jt + 1) + ss.k(jt, jt + 1),
                    writes=xn.k(a0, a0 + 512))
                p.op("vector", lambda e, a0=a0, g0=g0: e.tensor_mul(
                    xn[:, a0:a0 + 512], xn[:, a0:a0 + 512], Pb[:, g0:g0 + 512]),
                    reads=xn.k(a0, a0 + 512) + Pb.k(g0, g0 + 512),
                    writes=xn.k(a0, a0 + 512))
                p.op("vector", lambda e, a0=a0, g0=g0: e.tensor_add(
                    xn[:, a0:a0 + 512], xn[:, a0:a0 + 512],
                    Pb[:, 1024 + g0:1024 + g0 + 512]),
                    reads=xn.k(a0, a0 + 512) + Pb.k(1024 + g0, 1024 + g0 + 512),
                    writes=xn.k(a0, a0 + 512))
                p.op("vector", lambda e, a0=a0: e.tensor_add(
                    x[:, a0:a0 + 512], x[:, a0:a0 + 512], xn[:, a0:a0 + 512]),
                    reads=xn.k(a0, a0 + 512) + x.k(a0, a0 + 512),
                    writes=x.k(a0, a0 + 512))
            cb(jt)

    def ffn(g, cb):
        transpose_split_fm(xn)
        for jt in range(NT):
            def r_mm(e, jt=jt):
                for ci in range(NCH):
                    last = mm3(e, ps[2][:, 0:8],
                               fmh[:, GT * ci + 128 * jt:GT * ci + 128 * jt + 128],
                               fml[:, GT * ci + 128 * jt:GT * ci + 128 * jt + 128],
                               rwh[:, 8 * ci:8 * ci + 8],
                               rwl[:, 8 * ci:8 * ci + 8],
                               ci == 0, ci == 7)
                return last
            p.op("tensor", r_mm,
                 reads=[k for ci in range(NCH) for k in
                        fm_keys(GT * ci + 128 * jt, GT * ci + 128 * jt + 128)] +
                       rwh.k() + rwl.k(),
                 writes=ps[2].k())
            p.op("vector", lambda e: e.tensor_add(lgt.ap, ps[2][:, 0:8], rbr.ap),
                 reads=ps[2].k() + rbr.k(), writes=lgt.k())
            p.op("vector", lambda e: e.reduce_max(l1t.ap, lgt.ap, axis=AX.X),
                 reads=lgt.k(), writes=l1t.k())
            p.op("vector", lambda e: e.reduce_max(nl1t.ap, lgt.ap, axis=AX.X,
                                                  negate=True),
                 reads=lgt.k(), writes=nl1t.k())
            p.op("vector", lambda e: e.tensor_scalar(
                m1t.ap, lgt.ap, l1t.ap, None, ALU.is_ge),
                reads=lgt.k() + l1t.k(), writes=m1t.k())
            p.op("vector", lambda e: e.scalar_tensor_tensor(
                m1t.ap, m1t.ap, NEG, lgt.ap, ALU.mult, ALU.add),
                reads=m1t.k() + lgt.k(), writes=m1t.k())
            p.op("vector", lambda e: e.reduce_max(l2t.ap, m1t.ap, axis=AX.X),
                 reads=m1t.k(), writes=l2t.k())
            p.op("vector", lambda e: e.tensor_scalar(
                ge2t.ap, lgt.ap, l2t.ap, None, ALU.is_ge),
                reads=lgt.k() + l2t.k(), writes=ge2t.k())
            p.op("scalar", lambda e: e.activation(lgt.ap, lgt.ap, AF.Exp,
                                                  bias=nl1t.ap),
                 reads=lgt.k() + nl1t.k(), writes=lgt.k())
            p.op("vector", lambda e: e.tensor_mul(lgt.ap, lgt.ap, ge2t.ap),
                 reads=lgt.k() + ge2t.k(), writes=lgt.k())
            p.op("vector", lambda e: e.reduce_sum(l1t.ap, lgt.ap, axis=AX.X),
                 reads=lgt.k(), writes=l1t.k())
            p.op("vector", lambda e: e.reciprocal(l1t.ap, l1t.ap),
                 reads=l1t.k(), writes=l1t.k())
            p.op("vector", lambda e, jt=jt: e.tensor_scalar_mul(
                gates[:, 8 * jt:8 * jt + 8], lgt.ap, l1t.ap),
                reads=lgt.k() + l1t.k(), writes=gates.k(8 * jt, 8 * jt + 8))

        def split_h(d0):
            p.op("scalar", lambda e, d0=d0: e.copy(
                b2h[:, d0:d0 + 512], PTb.ap),
                reads=PTb.k(), writes=b2h.k(d0, d0 + 512))
            p.op("vector", lambda e, d0=d0: e.tensor_sub(
                b2l[:, d0:d0 + 512], PTb.ap, b2h[:, d0:d0 + 512]),
                reads=PTb.k() + b2h.k(d0, d0 + 512), writes=b2l.k(d0, d0 + 512))

        def expert_h(e_idx, hbase):
            loadw(W1h, W1l, "e1", 0, 4096, eslice=e_idx)
            loadw(W1h, W1l, "e3", 4096, 4096, eslice=e_idx)
            for hc in range(4):
                bank1 = ps[hc % 2]
                bank3 = ps[2 + hc % 2]
                def h1_mm(e, hc=hc, bank1=bank1):
                    for ci in range(NCH):
                        last = mm3(
                            e, bank1.ap,
                            W1h[:, 512 * ci + 128 * hc:512 * ci + 128 * hc + 128],
                            W1l[:, 512 * ci + 128 * hc:512 * ci + 128 * hc + 128],
                            fmh[:, GT * ci:GT * ci + GT],
                            fml[:, GT * ci:GT * ci + GT],
                            ci == 0, ci == 7)
                    return last
                def h3_mm(e, hc=hc, bank3=bank3):
                    for ci in range(NCH):
                        last = mm3(
                            e, bank3.ap,
                            W1h[:, 4096 + 512 * ci + 128 * hc:4096 + 512 * ci + 128 * hc + 128],
                            W1l[:, 4096 + 512 * ci + 128 * hc:4096 + 512 * ci + 128 * hc + 128],
                            fmh[:, GT * ci:GT * ci + GT],
                            fml[:, GT * ci:GT * ci + GT],
                            ci == 0, ci == 7)
                    return last
                p.op("tensor", h1_mm,
                     reads=W1h.k(0, 4096) + W1l.k(0, 4096) + fm_keys(0, NCH * GT),
                     writes=bank1.k())
                p.op("tensor", h3_mm,
                     reads=W1h.k(4096, 8192) + W1l.k(4096, 8192) +
                           fm_keys(0, NCH * GT),
                     writes=bank3.k())
                p.op("scalar", lambda e, bank1=bank1: e.activation(
                    tmp5.ap, bank1.ap, AF.Silu),
                    reads=bank1.k(), writes=tmp5.k())
                p.op("vector", lambda e, bank3=bank3: e.tensor_mul(
                    PTb.ap, tmp5.ap, bank3.ap),
                    reads=tmp5.k() + bank3.k(), writes=PTb.k())
                split_h(hbase + 512 * hc)

        def y_hc(e, hbase, wcol, jt, half, bank, hc, start, stop):
            a0 = hbase + 512 * hc + 128 * jt
            w0 = wcol + 1024 * hc + 512 * half
            return mm3(e, bank.ap, b2h[:, a0:a0 + 128], b2l[:, a0:a0 + 128],
                       W2h[:, w0:w0 + 512], W2l[:, w0:w0 + 512], start, stop)

        def y_ops(hbase, wcol, jt, half, bank, start, stop=True):
            wk0 = wcol
            for hc in range(4):
                a0 = hbase + 512 * hc + 128 * jt
                w0 = wcol + 1024 * hc + 512 * half
                p.op("tensor",
                     lambda e, hc=hc, jt=jt, half=half, bank=bank:
                     y_hc(e, hbase, wcol, jt, half, bank, hc,
                          start and hc == 0, stop and hc == 3),
                     reads=b2h.k(a0, a0 + 128) + b2l.k(a0, a0 + 128) +
                           W2h.k(w0, w0 + 512) + W2l.k(w0, w0 + 512) +
                           ([] if (start and hc == 0) else [("cont", 0)]),
                     writes=bank.k())

        expert_h(8, 0)
        expert_h(9, 2048)
        loadw(W2h, W2l, "e2", 0, 4096, eslice=8)
        loadw(W2h, W2l, "e2", 4096, 4096, eslice=9)
        for jt in range(NT):
            for half in range(2):
                bank = ps[4 + (2 * jt + half) % 4]
                y_ops(0, 0, jt, half, bank, True, stop=False)
                y_ops(2048, 4096, jt, half, bank, False)
                a0 = D * jt + 512 * half
                p.op("vector", lambda e, a0=a0, bank=bank: e.tensor_add(
                    x[:, a0:a0 + 512], x[:, a0:a0 + 512], bank.ap),
                    reads=bank.k() + x.k(a0, a0 + 512), writes=x.k(a0, a0 + 512))

        for e_idx in range(8):
            expert_h(e_idx, 0)
            loadw(W2h, W2l, "e2", 0, 4096, eslice=e_idx)
            for jt in range(NT):
                for half in range(2):
                    bank = ps[4 + (2 * jt + half) % 4]
                    y_ops(0, 0, jt, half, bank, True)
                    a0 = D * jt + 512 * half
                    p.op("vector",
                         lambda e, a0=a0, bank=bank, jt=jt, e_idx=e_idx:
                         e.scalar_tensor_tensor(
                             x[:, a0:a0 + 512], bank.ap,
                             gates[:, 8 * jt + e_idx:8 * jt + e_idx + 1],
                             x[:, a0:a0 + 512], ALU.mult, ALU.add),
                         reads=bank.k() +
                               gates.k(8 * jt + e_idx, 8 * jt + e_idx + 1) +
                               x.k(a0, a0 + 512),
                         writes=x.k(a0, a0 + 512))
                    if e_idx == 7 and half == 1:
                        cb(jt)

    def attn_b(g, cb):
        transpose_split_fm(xn)
        for wname, dsth, dstl in (("wbq", b1h, b1l), ("wbk", b2h, b2l)):
            loadw(W1h, W1l, wname)
            for eo in range(NCH):
                bank = ps[eo % 2]
                def p_mm(e, eo=eo, bank=bank):
                    for ci in range(NCH):
                        last = mm3(
                            e, bank.ap,
                            W1h[:, 1024 * ci + 128 * eo:1024 * ci + 128 * eo + 128],
                            W1l[:, 1024 * ci + 128 * eo:1024 * ci + 128 * eo + 128],
                            fmh[:, GT * ci:GT * ci + GT],
                            fml[:, GT * ci:GT * ci + GT],
                            ci == 0, ci == 7)
                    return last
                p.op("tensor", p_mm,
                     reads=W1h.k() + W1l.k() + fm_keys(0, NCH * GT),
                     writes=bank.k())
                split_psum(bank, GT, dsth, dstl, GT * eo)
        for jt in range(NT):
            def st_mm(e, jt=jt):
                for ci in range(NCH):
                    a0 = GT * ci + 128 * jt
                    last = mm3(e, ps[2][:, 128 * jt:128 * jt + 128],
                               b2h[:, a0:a0 + 128], b2l[:, a0:a0 + 128],
                               b1h[:, a0:a0 + 128], b1l[:, a0:a0 + 128],
                               ci == 0, ci == 7)
                return last
            p.op("tensor", st_mm,
                 reads=[k for ci in range(NCH) for k in
                        b2h.k(GT * ci + 128 * jt, GT * ci + 128 * jt + 128) +
                        b2l.k(GT * ci + 128 * jt, GT * ci + 128 * jt + 128) +
                        b1h.k(GT * ci + 128 * jt, GT * ci + 128 * jt + 128) +
                        b1l.k(GT * ci + 128 * jt, GT * ci + 128 * jt + 128)],
                 writes=ps[2].k())
            p.op("vector", lambda e, jt=jt: e.tensor_mul(
                PTb[:, 0:128], ps[2][:, 128 * jt:128 * jt + 128], maskT.ap),
                reads=ps[2].k() + maskT.k(), writes=PTb.k(0, 128))
            p.op("scalar", lambda e, jt=jt: e.copy(
                stmh[:, 128 * jt:128 * jt + 128], PTb[:, 0:128]),
                reads=PTb.k(0, 128), writes=stmh.k(128 * jt, 128 * jt + 128))
            p.op("vector", lambda e, jt=jt: e.tensor_sub(
                stml[:, 128 * jt:128 * jt + 128], PTb[:, 0:128],
                stmh[:, 128 * jt:128 * jt + 128]),
                reads=PTb.k(0, 128) + stmh.k(128 * jt, 128 * jt + 128),
                writes=stml.k(128 * jt, 128 * jt + 128))
        loadw(W1h, W1l, "wbv")
        loadw(W2h, W2l, "wbg")
        for jt in range(NT):
            for half in range(2):
                vbank = ps[3 + half]
                def v_mm(e, jt=jt, half=half, vbank=vbank):
                    for ci in range(NCH):
                        a0 = GT * ci + 128 * jt
                        w0 = 1024 * ci + 512 * half
                        last = mm3(e, vbank.ap,
                                   fmh[:, a0:a0 + 128], fml[:, a0:a0 + 128],
                                   W1h[:, w0:w0 + 512], W1l[:, w0:w0 + 512],
                                   ci == 0, ci == 7)
                    return last
                p.op("tensor", v_mm,
                     reads=W1h.k() + W1l.k() +
                           [k for ci in range(NCH) for k in
                            fm_keys(GT * ci + 128 * jt, GT * ci + 128 * jt + 128)],
                     writes=vbank.k())
                split_psum(vbank, 512, b1h, b1l, D * jt + 512 * half)
            for half in range(2):
                gbank = ps[5 + half]
                obank = ps[3 + half]
                def g_mm(e, jt=jt, half=half, gbank=gbank):
                    for ci in range(NCH):
                        a0 = GT * ci + 128 * jt
                        w0 = 1024 * ci + 512 * half
                        last = mm3(e, gbank.ap,
                                   fmh[:, a0:a0 + 128], fml[:, a0:a0 + 128],
                                   W2h[:, w0:w0 + 512], W2l[:, w0:w0 + 512],
                                   ci == 0, ci == 7)
                    return last
                p.op("tensor", g_mm,
                     reads=W2h.k() + W2l.k() +
                           [k for ci in range(NCH) for k in
                            fm_keys(GT * ci + 128 * jt, GT * ci + 128 * jt + 128)],
                     writes=gbank.k())
                p.op("scalar", lambda e, gbank=gbank: e.activation(
                    tmp5.ap, gbank.ap, AF.Sigmoid),
                    reads=gbank.k(), writes=tmp5.k())
                a0 = D * jt + 512 * half
                def o_mm(e, jt=jt, a0=a0, obank=obank):
                    return mm3(e, obank.ap,
                               stmh[:, 128 * jt:128 * jt + 128],
                               stml[:, 128 * jt:128 * jt + 128],
                               b1h[:, a0:a0 + 512], b1l[:, a0:a0 + 512],
                               True, True)
                p.op("tensor", o_mm,
                     reads=stmh.k(128 * jt, 128 * jt + 128) +
                           stml.k(128 * jt, 128 * jt + 128) +
                           b1h.k(a0, a0 + 512) + b1l.k(a0, a0 + 512),
                     writes=obank.k())
                p.op("vector", lambda e, a0=a0, obank=obank: e.tensor_mul(
                    buf1[:, a0:a0 + 512], obank.ap, tmp5.ap),
                    reads=obank.k() + tmp5.k(), writes=buf1.k(a0, a0 + 512))
        transpose_split_fm(buf1)
        loadw(W1h, W1l, "wbo")
        for jt in range(NT):
            for half in range(2):
                bank = ps[half]
                def ob_mm(e, jt=jt, half=half, bank=bank):
                    for ci in range(NCH):
                        a0 = GT * ci + 128 * jt
                        w0 = 1024 * ci + 512 * half
                        last = mm3(e, bank.ap,
                                   fmh[:, a0:a0 + 128], fml[:, a0:a0 + 128],
                                   W1h[:, w0:w0 + 512], W1l[:, w0:w0 + 512],
                                   ci == 0, ci == 7)
                    return last
                p.op("tensor", ob_mm,
                     reads=W1h.k() + W1l.k() +
                           [k for ci in range(NCH) for k in
                            fm_keys(GT * ci + 128 * jt, GT * ci + 128 * jt + 128)],
                     writes=bank.k())
                a0 = D * jt + 512 * half
                p.op("vector", lambda e, a0=a0, bank=bank: e.tensor_add(
                    xn[:, a0:a0 + 512], xn[:, a0:a0 + 512], bank.ap),
                    reads=xn.k(a0, a0 + 512) + bank.k(), writes=xn.k(a0, a0 + 512))
                p.op("vector", lambda e, a0=a0: e.tensor_add(
                    x[:, a0:a0 + 512], x[:, a0:a0 + 512], xn[:, a0:a0 + 512]),
                    reads=xn.k(a0, a0 + 512) + x.k(a0, a0 + 512),
                    writes=x.k(a0, a0 + 512))
                if half == 1:
                    cb(jt)

    def store_tile(g, j):
        row0 = GT * g
        p.dma("gpsimd", x_out[row0 + 128 * j:row0 + 128 * j + 128, :],
              x[:, D * j:D * j + D], reads=x.k(D * j, D * j + D))

    for g in range(n_groups):
        row0 = GT * g
        for j in range(NT):
            p.dma("sync", x[:, D * j:D * j + D],
                  x_in[row0 + 128 * j:row0 + 128 * j + 128, :],
                  writes=x.k(D * j, D * j + D))
            ln_tile(j)
        for si, s in enumerate(subs):
            last = si == len(subs) - 1
            cb = ((lambda j, g=g: store_tile(g, j)) if last
                  else (lambda j: ln_tile(j)))
            if s == "a":
                attn_a(g, cb)
            elif s == "b":
                attn_b(g, cb)
            else:
                ffn(g, cb)

    p.dma_barrier("gpsimd", "gpsimd")
    p.finalize()
    n_ins = sum(len(v) for v in p.thunks.values())
    return nc, n_ins



def _img(wT):
    k, m = wT.shape
    return np.ascontiguousarray(
        wT.reshape(k // 128, 128, m).transpose(1, 0, 2).reshape(128, (k // 128) * m))


def make_inputs(inputs, core):
    f = np.ascontiguousarray
    g = lambda n: np.asarray(inputs[n])
    x = g("L0_attn_a_x")
    d = {
        "x_in": f(x[2 * core:2 * core + 2].reshape(CORE_TOK, D)),
        "wqT": _img(g("L0_attn_a_w_q").T),
        "wkvT": _img(g("L0_attn_a_w_kv").T),
        "woutT": _img(g("L0_attn_a_w_out").T),
        "ag_col": f(g("L0_attn_a_ln_g").reshape(NCH, 128).T),
        "ab_col": f(g("L0_attn_a_ln_b").reshape(NCH, 128).T),
        "og_rep": f(np.broadcast_to(g("L0_attn_a_lno_g"), (128, D))),
        "ob_rep": f(np.broadcast_to(g("L0_attn_a_lno_b"), (128, D))),
        "ld0": f(g("L0_attn_b_log_decay")[0:1].reshape(1, 1)),
        "r_w": _img(g("L0_ffn_router_w")),
        "rb_rep": f(np.broadcast_to(g("L0_ffn_router_b"), (128, 8))),
        "ew1": np.stack([_img(w) for w in np.concatenate(
            [g("L0_ffn_routed_w1"), g("L0_ffn_shared_w1")], 0)]),
        "ew3": np.stack([_img(w) for w in np.concatenate(
            [g("L0_ffn_routed_w3"), g("L0_ffn_shared_w3")], 0)]),
        "ew2": np.stack([_img(w) for w in np.concatenate(
            [g("L0_ffn_routed_w2"), g("L0_ffn_shared_w2")], 0)]),
        "ident": np.eye(128, dtype=np.float32),
        "eyemod": (np.arange(128)[:, None] % 64 == np.arange(64)[None, :]
                   ).astype(np.float32),
        "trimask": np.triu(np.full((128, 128), NEG, np.float32), 1),
        "ones128": np.ones((1, 128), np.float32),
    }
    for w in ("q", "k", "v", "g", "o"):
        d["wb%sT" % w] = _img(g("L0_attn_b_W_" + w).T)
    u = np.arange(128)[:, None]
    q = np.arange(128)[None, :]
    low = ((u // 32) == (q // 32)) & (q >= u)
    d["diffT"] = ((q - u) * low).astype(np.float32)
    d["indT"] = low.astype(np.float32)
    return d


def kernel(**inputs):
    nc, _ = build_core()
    in_maps = [make_inputs(inputs, c) for c in range(8)]
    res = run_bass_kernel_spmd(nc, in_maps, core_ids=list(range(8)))
    out = np.concatenate([r["x_out"].reshape(2, SEQ, D) for r in res.results], 0)
    return out.astype(np.float32)
